# version 27
# baseline (speedup 1.0000x reference)
import numpy as np
import ml_dtypes
import concourse.bass as bass
import concourse.mybir as mybir
from concourse.bass_utils import run_bass_kernel_spmd

F32 = mybir.dt.float32
BF16 = mybir.dt.bfloat16
AF = mybir.ActivationFunctionType
ALU = mybir.AluOpType
AX = mybir.AxisListType

NEG = -1.0e9
BF = ml_dtypes.bfloat16
EPS_L = 1e-30


class Prog:
    ENGINES = ("tensor", "vector", "scalar", "gpsimd", "sync")
    DMAQ = ("gpsimd", "sync")
    NLANE = 4
    _SEQ = [0]

    def __init__(self, nc):
        self.nc = nc
        self._SEQ[0] += 1
        sid = self._SEQ[0]
        self.ops = {e: [] for e in self.ENGINES}
        self.sems = {}
        self.cnt = {}
        self.lane = {}
        for e in self.ENGINES:
            if e in self.DMAQ:
                self.sems[e] = [nc.alloc_semaphore(f"sem{sid}_{e}{i}")
                                for i in range(self.NLANE)]
                self.cnt[e] = [0] * self.NLANE
                self.lane[e] = 0
            else:
                self.sems[e] = nc.alloc_semaphore(f"sem{sid}_" + e)
                self.cnt[e] = 0

    def at(self, eng):
        if eng in self.DMAQ:
            return [(self.sems[eng][i], self.cnt[eng][i])
                    for i in range(self.NLANE) if self.cnt[eng][i] > 0]
        return [(self.sems[eng], self.cnt[eng])] if self.cnt[eng] > 0 else []

    @staticmethod
    def _flat(waits):
        out = []
        for tok in waits:
            for p in tok:
                if p not in out:
                    out.append(p)
        return out

    def emit(self, eng, fn, waits=(), inc=0, chain=False):
        w = self._flat(waits)
        if eng in self.DMAQ and inc:
            ln = self.lane[eng]
            self.lane[eng] = (ln + 1) % self.NLANE
            sem = self.sems[eng][ln]
            if self.cnt[eng][ln] > 0:
                w.append((sem, self.cnt[eng][ln]))
            self.cnt[eng][ln] += inc
        elif inc:
            sem = self.sems[eng]
            if chain and self.cnt[eng] > 0:
                w.append((sem, self.cnt[eng]))
            self.cnt[eng] += inc
        else:
            sem = None

        def g(e, w=w, fn=fn, inc=inc, sem=sem):
            for s_, val in w:
                e.wait_ge(s_, val)
            ins = fn(e)
            if inc:
                ins.then_inc(sem, inc)

        self.ops[eng].append(g)

    def barrier(self):
        toks = [self.at(e) for e in self.ENGINES]
        for e in self.ENGINES:
            own = self.at(e)
            w = [t for t in toks if t is not own]
            self.emit(e, lambda _e: None,
                      waits=[p for t in w for p in [t]])

    def build(self):
        with self.nc.Block() as block:
            for eng in self.ENGINES:
                ops = self.ops[eng]
                if not ops:
                    continue

                def make(ops):
                    def _(e):
                        for f in ops:
                            f(e)
                    return _
                getattr(block, eng)(make(ops))


def np_consts():
    return {
        "id_bf": np.eye(128, dtype=BF),
        "id_f32": np.eye(128, dtype=np.float32),
        "ones_f32": np.ones((128, 128), dtype=np.float32),
        "ones_col_bf": np.ones((128, 1), dtype=BF),
    }


def load_consts(nc, P):
    out = {}
    for name, arr in np_consts().items():
        dt = BF16 if arr.dtype == BF else F32
        t = nc.alloc_sbuf_tensor("c_" + name, list(arr.shape), dt)
        dr = nc.inline_tensor(np.ascontiguousarray(arr), "ic_" + name)
        P.emit("sync", lambda e, t=t, dr=dr: e.dma_start(out=t[:], in_=dr.ap()),
               inc=16)
        out[name] = t
    return out


from concourse.bass_core_rust import IndirectOffsetOnAxis

I32 = mybir.dt.int32


class Arena:
    SZ = {F32: 4, BF16: 2, I32: 4}

    def __init__(self, nc, kib=203):
        self.base = nc.alloc_sbuf_tensor("ARENA", [128, kib * 1024],
                                         mybir.dt.uint8)
        self.cap = kib * 1024
        self.off = 0
        self.hwm = 0

    def reset(self):
        self.off = 0

    def __call__(self, name, shape, dt):
        Pn, N = shape
        nb = N * self.SZ[dt]
        v = self.base[0:Pn, self.off:self.off + nb].bitcast(dt)
        self.off += (nb + 31) // 32 * 32
        self.hwm = max(self.hwm, self.off)
        assert self.off <= self.cap, (name, self.off)
        return v

    def at(self, off, shape, dt):
        Pn, N = shape
        nb = N * self.SZ[dt]
        assert off + nb <= self.cap
        return self.base[0:Pn, off:off + nb].bitcast(dt)


def ffn_np_consts():
    c = {}
    c["id_bf"] = np.eye(128, dtype=BF)
    c["id_f32"] = np.eye(128, dtype=np.float32)
    c["ones_f32"] = np.ones((128, 128), dtype=np.float32)
    hs = np.zeros((128, 2), dtype=np.float32)
    hs[0:64, 0] = 1.0
    hs[64:128, 1] = 1.0
    c["headsum"] = hs
    R = np.zeros((128, 128), dtype=np.float32)
    for d in range(128):
        if d % 64 < 32:
            R[d + 32, d] = -1.0
        else:
            R[d - 32, d] = 1.0
    c["rotT"] = R
    k = np.arange(128)[:, None]
    q = np.arange(128)[None, :]
    c["tri_lo"] = np.where(k <= q, 0.0, NEG).astype(np.float32)
    c["tri_hi"] = np.where(k >= q, 0.0, NEG).astype(np.float32)
    up = np.zeros((128, 128), dtype=np.float32)
    for r in range(1, 128):
        up[r - 1, r] = 1.0
    c["shift_up"] = up
    l2f = np.zeros((128, 128), dtype=np.float32)
    l2f[127, 0] = 1.0
    c["last2first"] = l2f
    c["eps_col"] = np.full((128, 1), EPS_L, dtype=np.float32)
    return c




def all_np_consts():
    c = np_consts()
    c.update(ffn_np_consts())
    return c


def emit_stage_attn(nc, P, AL, consts, DS, l1=False):
    NQ, NK, D = 2048, 4096, 1024
    NH, HD = (1, 1024) if l1 else (16, 64)
    scale = (1.0 / 32.0) if l1 else (1.0 / 8.0)
    DMA_S2 = "sync"
    x_src = DS["x_src"]
    w_attn = DS["w_attn"]
    b_attn = DS["b_attn"]
    w_proj = DS["w_proj"]
    b_proj = DS["b_proj"]
    kdead = DS["kdead"]
    dmask = DS["dmask"]
    y = DS["ydst"]
    k_dram = DS["k_dram"]
    if l1:
        v_dram = DS["v_dram"]
    id_b = consts["id_bf"][:]
    id_f = consts["id_f32"]
    ones_f = consts["ones_f32"]
    ones_col_bf = consts["ones_col_bf"]
    ccw = DS.get("ccw") or []
    ccw_kv = DS.get("ccw_kv")
    ccw_own = DS.get("ccw_own")
    ccp = DS.get("ccp")

    xkv = [AL(f"xkv{i}", [128, NK], BF16) for i in range(8)]
    wt_off = AL.off
    wt = [AL(f"wt{i}", [128, 1024], BF16) for i in range(8)]
    xrow = AL("xrow", [128, 1024], BF16)
    wrow = xrow
    xrowf = AL("xrowf", [128, 1024], F32)
    idxkv_sb = AL("idxkv_sb", [128, 32], I32)
    idxown_sb = AL("idxown_sb", [128, 16], I32)
    P.emit(DMA_S2, lambda e: e.dma_start(out=idxkv_sb, in_=DS["idx_kv"]), inc=16)
    P.emit(DMA_S2, lambda e: e.dma_start(out=idxown_sb, in_=DS["idx_own"]),
           inc=16)
    VW = NH * (HD + 1) if not l1 else 0
    if not l1:
        v_sb = [AL(f"v{i}", [128, VW], BF16)
                for i in range(NK // 128)]
        k_h = AL("k_h", [128, NK], BF16)
        k_hb = [k_h, AL.at(wt_off, [128, NK], BF16)]
        pt_sb = [AL(f"pt{i}", [128, 512], BF16)
                 for i in range(2)]
    else:
        v_sb = None
        kh_off = AL.off
        k_h = AL("k_h", [128, NK], BF16)
        pt_sb = [AL(f"pt{i}", [128, 512], BF16)
                 for i in range(32)]
        vbuf2 = [AL(f"vb{i}", [128, D], BF16) for i in range(2)]
        vbuf = [vbuf2[0][:, 0:512], vbuf2[0][:, 512:1024],
                vbuf2[1][:, 0:512], vbuf2[1][:, 512:1024]]
        kbuf = ([AL.at(kh_off, [128, 4096], BF16),
                 AL.at(wt_off, [128, 4096], BF16),
                 AL.at(wt_off + 8192, [128, 4096], BF16)] +
                [AL(f"kb{i}", [128, 4096], BF16) for i in range(5)])
    dm_sb = [AL(f"dm{i}", [128, 512], BF16) for i in range(8)]
    kdT = [AL(f"kdT{i}", [128, 32], F32) for i in range(4)]
    kdraw = AL("kdraw", [32, 128], F32)
    biasall = AL("biasall", [128, 32], F32)
    rm = AL("rm", [128, 16], F32)
    msc = AL("msc", [128, 1024], F32)
    bv_row = msc[0:1, 0:1024]
    bp_row = msc[0:1, 0:1024]
    negM = AL("negM", [128, 1], F32)
    batraw = AL("batraw", [24, 128], F32)
    batT = AL("batT", [128, 24], F32)
    bcast_bv = AL("bcast_bv", [128, D], F32)
    bcast_bp = AL("bcast_bp", [128, D], F32)
    rb_sb = AL("rb_sb", [128, 512], F32)
    xo = [xrowf, AL("xo1", [128, D], F32)]
    xa = AL("xa", [128, D], F32)
    ln_s = AL("ln_s", [128, 24], F32)
    y_sb = bcast_bv
    sq = y_sb

    psA = DS["psA"]
    psS = DS["psS"]
    psV = DS["psV"]
    psT = DS["psT"]
    psTb = psT.bitcast(BF16)

    DMA_G, DMA_S = "gpsimd", "sync"

    def fm_load(dst_list, idx_sb, ntiles, col_off=0):
        xr2 = [xrow, msc.bitcast(BF16)[:, 0:1024]]
        ps2 = [psTb[0:128, 0:1024], psS[0].bitcast(BF16)[:, 0:1024]]
        t_tr = {-1: P.at("tensor"), -2: P.at("tensor")}
        v_cp = {-1: P.at("vector"), -2: P.at("vector")}
        for t in range(ntiles):
            tb = t % 2
            w = [t_tr[t - 2], ccw_kv[t] if ccw_kv else ccw]
            if t == 0:
                w += [P.at(DMA_S2), P.at("vector")]
            P.emit(DMA_G, lambda e, t=t, tb=tb, idx_sb=idx_sb:
                   e.indirect_dma_start(
                       out=xr2[tb][:], out_offset=None, in_=x_src[:, :],
                       in_offset=IndirectOffsetOnAxis(idx_sb[:, t:t + 1], 0)),
                   waits=w, inc=16)
            gw = P.at(DMA_G)
            P.emit("tensor", lambda e, tb=tb: (
                [e.transpose(ps2[tb][:, ic * 128:(ic + 1) * 128],
                             xr2[tb][:, ic * 128:(ic + 1) * 128], id_b)
                 for ic in range(8)][-1]),
                waits=[gw, v_cp[t - 2]], inc=1)
            t_tr[t] = P.at("tensor")
            P.emit("vector", lambda e, t=t, tb=tb: (
                [e.tensor_copy(
                    dst_list[ic][:, col_off + t * 128:col_off + (t + 1) * 128],
                    ps2[tb][:, ic * 128:(ic + 1) * 128]) for ic in range(8)][-1]),
                waits=[t_tr[t]], inc=1)
            v_cp[t] = P.at("vector")

    fm_load(xkv, idxkv_sb, NK // 128)

    P.emit(DMA_S, lambda e: e.dma_start(
        out=batraw[:], in_=b_attn.rearrange("(c p) -> c p", p=128)),
        waits=[P.at("tensor")], inc=16)
    P.emit(DMA_S, lambda e: e.dma_start(
        out=bv_row, in_=b_attn[2048:3072].unsqueeze(0)),
        waits=[P.at("tensor")], inc=16)
    P.barrier()
    P.emit("tensor", lambda e: e.transpose(psT[0:128, 0:24], batraw[:], id_f[0:24, 0:24]),
           inc=1)
    P.emit("vector", lambda e: e.tensor_copy(batT[:, 0:24], psT[0:128, 0:24]),
           waits=[P.at("tensor")], inc=1)
    for j in range(4):
        P.emit(DMA_S, lambda e, j=j: e.dma_start(out=kdraw[:], in_=kdead[j]),
               waits=[P.at("tensor")], inc=16)
        P.emit("tensor", lambda e: e.transpose(psT[0:128, 0:32], kdraw[:],
                                               id_f[0:32, 0:32]),
               waits=[P.at(DMA_S), P.at("vector")], inc=1)
        P.emit("vector", lambda e, j=j: e.tensor_copy(kdT[j][:], psT[0:128, 0:32]),
               waits=[P.at("tensor")], inc=1)
    def bcast_row(dst):
        for half in range(2):
            P.emit("tensor", lambda e, half=half: e.matmul(
                psT[0:128, 0:512], ones_f[0:1, 0:128],
                bv_row[0:1, half * 512:(half + 1) * 512], start=True, stop=True),
                waits=[P.at("vector"), P.at(DMA_S)], inc=1)
            P.emit("vector", lambda e, dst=dst, half=half: e.tensor_copy(
                dst[:, half * 512:(half + 1) * 512], psT[0:128, 0:512]),
                waits=[P.at("tensor")], inc=1)
    bcast_row(bcast_bv)
    P.barrier()
    P.emit(DMA_S, lambda e: e.dma_start(
        out=bp_row, in_=b_proj.unsqueeze(0)), inc=16)
    P.barrier()
    bcast_row(bcast_bp)
    P.barrier()

    def load_w_third(third):
        for ic in range(8):
            P.emit(DMA_G, lambda e, ic=ic, third=third: e.dma_start(
                out=wt[ic][:], in_=w_attn[ic * 128:(ic + 1) * 128,
                                          third * D:(third + 1) * D]),
                waits=[P.at("tensor")], inc=16)
        P.barrier()

    def load_w_proj():
        for ic in range(8):
            P.emit(DMA_G, lambda e, ic=ic: e.dma_start(
                out=wt[ic][:], in_=w_proj[ic * 128:(ic + 1) * 128, :]),
                waits=[P.at("tensor")], inc=16)
        P.barrier()

    load_w_third(1)
    s_slot = {i: P.at("scalar") for i in range(4)}
    st_tok = P.at(DMA_S)
    for oc in range(8):
        for nt in range(NK // 512):
            sl = nt % 4
            P.emit("tensor", lambda e, oc=oc, nt=nt: (
                [e.matmul(psA[:, (nt % 4) * 512:(nt % 4) * 512 + 512],
                          wt[ic][:, oc * 128:(oc + 1) * 128],
                          xkv[ic][:, nt * 512:(nt + 1) * 512],
                          start=(ic == 0), stop=(ic == 7)) for ic in range(8)][-1]),
                waits=[s_slot[sl]], inc=1)
            t_mm = P.at("tensor")
            P.emit("scalar", lambda e, oc=oc, nt=nt: e.activation(
                out=k_h[:, nt * 512:(nt + 1) * 512],
                in_=psA[:, (nt % 4) * 512:(nt % 4) * 512 + 512],
                func=AF.Identity, bias=batT[:, 8 + oc:9 + oc], scale=1.0),
                waits=[t_mm] + ([st_tok] if nt == 0 else []), inc=1)
            s_slot[sl] = P.at("scalar")
        P.emit(DMA_S, lambda e, oc=oc: e.dma_start(
            out=k_dram.ap()[oc], in_=k_h[:]),
            waits=[s_slot[3]], inc=16)
        st_tok = P.at(DMA_S)
    P.barrier()

    load_w_third(2)
    v_slot = {0: P.at("vector"), 1: P.at("vector")}
    st2 = {-1: P.at(DMA_S), -2: P.at(DMA_S)}
    for t in range(NK // 128):
        po = (t % 2) * 1024
        P.emit("tensor", lambda e, t=t, po=po: (
            [e.matmul(psA[:, po + half * 512:po + (half + 1) * 512],
                      xkv[ic][:, t * 128:(t + 1) * 128],
                      wt[ic][:, half * 512:(half + 1) * 512],
                      start=(ic == 0), stop=(ic == 7))
             for half in range(2) for ic in range(8)][-1]),
            waits=[v_slot[t % 2]], inc=1)
        t_mm = P.at("tensor")
        if not l1:
            P.emit("vector", lambda e, t=t, po=po: e.tensor_tensor(
                v_sb[t][:].rearrange("p (h c) -> p h c", c=HD + 1)[:, :, 0:HD],
                psA[:, po:po + 1024].rearrange("p (h c) -> p h c", c=HD),
                bcast_bv[:].rearrange("p (h c) -> p h c", c=HD), ALU.add),
                waits=[t_mm], inc=1)
            P.emit("vector", lambda e, t=t: e.memset(
                v_sb[t][:].rearrange("p (h c) -> p h c", c=HD + 1)[:, :, HD:HD + 1],
                1.0), inc=1)
            v_slot[t % 2] = P.at("vector")
        else:
            P.emit("vector", lambda e, t=t, po=po: e.tensor_tensor(
                k_h[:, po:po + 1024], psA[:, po:po + 1024], bcast_bv[:],
                ALU.add),
                waits=[t_mm, st2[t - 2]], inc=1)
            v_slot[t % 2] = P.at("vector")
            P.emit(DMA_S, lambda e, t=t, po=po: e.dma_start(
                out=v_dram.ap()[t], in_=k_h[:, po:po + 1024]),
                waits=[v_slot[t % 2]], inc=16)
            st2[t] = P.at(DMA_S)
    P.barrier()

    load_w_third(0)
    s_slot = {i: P.at("scalar") for i in range(4)}
    for nt in range(NQ // 512):
        for oc in range(8):
            sl = oc % 4
            P.emit("tensor", lambda e, oc=oc, nt=nt: (
                [e.matmul(psA[:, (oc % 4) * 512:(oc % 4) * 512 + 512],
                          wt[ic][:, oc * 128:(oc + 1) * 128],
                          xkv[ic][:, nt * 1024 + 512:(nt + 1) * 1024],
                          start=(ic == 0), stop=(ic == 7)) for ic in range(8)][-1]),
                waits=[s_slot[sl]], inc=1)
            t_mm = P.at("tensor")
            P.emit("scalar", lambda e, oc=oc, nt=nt: e.activation(
                out=xkv[oc][:, nt * 512:(nt + 1) * 512],
                in_=psA[:, (oc % 4) * 512:(oc % 4) * 512 + 512],
                func=AF.Identity, bias=batT[:, oc:oc + 1], scale=1.0),
                waits=[t_mm], inc=1)
            s_slot[sl] = P.at("scalar")
    P.barrier()

    q_of = 0
    ao_of = NQ

    def unit_l0(hp, hh, j, k_ap, gw_k):
        h = hp * 2 + (0 if hh == 0 else 1)
        nkt = 8 * (j + 1)
        klen = nkt * 128
        q_ap = xkv[hp][hh:hh + 64, q_of + j * 512:q_of + (j + 1) * 512]
        P.emit("vector", lambda e: e.memset(rm[:], -3.0e38),
               waits=[P.at("tensor")], inc=1)
        v0 = P.at("vector")
        v_red = {-1: v0, -2: v0}
        idx = 0
        for sub in range(4):
            qs = xkv[hp][hh:hh + 64,
                         q_of + j * 512 + sub * 128:q_of + j * 512 + (sub + 1) * 128]
            for r in range((klen + 1023) // 1024):
                kl = min(1024, klen - r * 1024)
                po = (idx % 2) * 1024
                nmm = (kl + 511) // 512
                w = [v_red[idx - 2]] + ([gw_k] if idx == 0 else [])
                P.emit("tensor", lambda e, qs=qs, r=r, kl=kl, nmm=nmm, po=po,
                       k_ap=k_ap: (
                    [e.matmul(psA[:, po + m * 512:
                                  po + m * 512 + min(512, kl - m * 512)],
                              qs, k_ap[hh:hh + 64,
                                       r * 1024 + m * 512:
                                       r * 1024 + m * 512 + min(512, kl - m * 512)],
                              start=True, stop=True) for m in range(nmm)][-1]),
                    waits=w, inc=1)
                t_mm = P.at("tensor")
                P.emit("vector", lambda e, idx=idx, kl=kl, po=po: e.reduce_max(
                    rm[:, idx:idx + 1], psA[:, po:po + kl], axis=AX.X),
                    waits=[t_mm], inc=1)
                v_red[idx] = P.at("vector")
                idx += 1
        P.emit("tensor", lambda e: e.transpose(psT[0:16, 0:128], rm[:], id_f[:]),
               waits=[P.at("vector")], inc=1)
        P.emit("vector", lambda e: e.reduce_max(msc[0:16, 0:1], psT[0:16, 0:128],
                                                axis=AX.X),
               waits=[P.at("tensor")], inc=1)
        P.emit("tensor", lambda e: e.transpose(psT[0:1, 0:16], msc[0:16, 0:1],
                                               id_f[0:16, 0:16]),
               waits=[P.at("vector")], inc=1)
        P.emit("vector", lambda e: e.reduce_max(msc[0:1, 64:65], psT[0:1, 0:16],
                                                axis=AX.X),
               waits=[P.at("tensor")], inc=1)
        P.emit("tensor", lambda e: e.matmul(psT[0:128, 0:1], ones_f[0:1, 0:128],
                                            msc[0:1, 64:65], start=True,
                                            stop=True),
               waits=[P.at("vector")], inc=1)
        P.emit("vector", lambda e: e.tensor_scalar_mul(negM[:], psT[0:128, 0:1],
                                                       -scale),
               waits=[P.at("tensor")], inc=1)
        P.emit("vector", lambda e, j=j: e.tensor_scalar(
            biasall[:, 0:32], kdT[j][:], negM[0:128, 0:1], None, ALU.add),
            chain=True, inc=1)
        v_bias = P.at("vector")
        mslots = {8 * j + s: s for s in range(8)}
        s_prev = P.at("scalar")
        t_S = {}
        t_AV = {}
        v_mask = {}
        s_exp = {}

        def emit_S(kt):
            w = [s_prev, v_bias] if kt < 2 else []
            P.emit("tensor", lambda e, kt=kt, q_ap=q_ap, k_ap=k_ap: e.matmul(
                psS[kt % 2][:], k_ap[hh:hh + 64, kt * 128:(kt + 1) * 128], q_ap,
                start=True, stop=True), waits=w, inc=1)
            t_S[kt] = P.at("tensor")

        emit_S(0)
        for kt in range(nkt):
            buf = kt % 2
            if kt + 1 < nkt:
                emit_S(kt + 1)
            ew = [t_S[kt]]
            if kt in mslots:
                P.emit("vector", lambda e, kt=kt, buf=buf, s=mslots[kt]:
                       e.tensor_tensor(psS[buf][:], psS[buf][:], dm_sb[s][:],
                                       ALU.add),
                       waits=[t_S[kt]], inc=1)
                v_mask[kt] = P.at("vector")
                ew.append(v_mask[kt])
            if kt >= 2:
                ew.append(t_AV[kt - 2])
            P.emit("scalar", lambda e, kt=kt, buf=buf: e.activation(
                out=pt_sb[buf][:], in_=psS[buf][:], func=AF.Exp,
                bias=biasall[:, kt:kt + 1], scale=scale),
                waits=ew, inc=1)
            s_exp[kt] = P.at("scalar")
            P.emit("tensor", lambda e, kt=kt, buf=buf, h=h: e.matmul(
                psV[0:65, :], v_sb[kt][:, h * 65:h * 65 + 65], pt_sb[buf][:],
                start=(kt == 0), stop=(kt == nkt - 1)),
                waits=[s_exp[kt]], inc=1)
            t_AV[kt] = P.at("tensor")
        P.emit("scalar", lambda e: e.activation(out=msc[64:65, 0:512],
                                                in_=psV[64:65, :],
                                                func=AF.Ln,
                                                bias=consts["eps_col"][64:65, 0:1]),
               waits=[P.at("tensor")], inc=1)
        s_l = P.at("scalar")
        P.emit("tensor", lambda e: e.matmul(psT[0:64, 0:512],
                                            ones_f[64:65, 0:64],
                                            msc[64:65, 0:512],
                                            start=True, stop=True),
               waits=[s_l], inc=1)
        t_bc = P.at("tensor")
        P.emit("scalar", lambda e: e.activation(out=rb_sb[0:64, 0:512],
                                                in_=psT[0:64, 0:512],
                                                func=AF.Exp, scale=-1.0),
               waits=[t_bc], inc=1)
        s_rb = P.at("scalar")
        P.emit("vector", lambda e, hp=hp, hh=hh, j=j: e.tensor_tensor(
            xkv[hp][hh:hh + 64, ao_of + j * 512:ao_of + (j + 1) * 512],
            psV[0:64, :], rb_sb[0:64, 0:512], ALU.mult),
            waits=[s_rb], inc=1)

    if not l1:
        for j in range(4):
            for s in range(8):
                P.emit(DMA_S, lambda e, j=j, s=s: e.dma_start(
                    out=dm_sb[s][:], in_=dmask[j, s]),
                    waits=[P.at("scalar"), P.at("vector")], inc=16)
            P.barrier()
            t_end = {}
            gw_k = {}

            def emit_kload(hp):
                kw = [t_end[hp - 2]] if hp >= 2 else [P.at("tensor")]
                P.emit(DMA_G, lambda e, hp=hp: e.dma_start(
                    out=k_hb[hp % 2][:], in_=k_dram.ap()[hp]),
                    waits=kw, inc=16)
                gw_k[hp] = P.at(DMA_G)

            emit_kload(0)
            emit_kload(1)
            for hp in range(8):
                for hh in (0, 64):
                    unit_l0(hp, hh, j, k_hb[hp % 2], gw_k[hp])
                t_end[hp] = P.at("tensor")
                if hp + 2 < 8:
                    emit_kload(hp + 2)
            P.barrier()
    else:
        t_kfree = P.at("tensor")
        s_prevexp = P.at("scalar")
        av0 = P.at("tensor")
        av_free = {0: av0, 1: av0, 2: av0, 3: av0}
        for j in range(4):
            nkt = 8 * (j + 1)
            klen = nkt * 128
            ka = min(2048, klen)
            for ic in range(8):
                P.emit(DMA_G if ic % 2 == 0 else DMA_S,
                       lambda e, ic=ic, ka=ka: e.dma_start(
                           out=kbuf[ic][:, 0:ka],
                           in_=k_dram.ap()[ic][:, 0:ka]),
                       waits=[t_kfree], inc=16)
            gwA = P.at(DMA_G) + P.at(DMA_S)
            for ic in range(8 if klen > 2048 else 0):
                P.emit(DMA_G if ic % 2 == 0 else DMA_S,
                       lambda e, ic=ic, klen=klen: e.dma_start(
                           out=kbuf[ic][:, 2048:klen],
                           in_=k_dram.ap()[ic][:, 2048:klen]),
                       waits=[t_kfree], inc=16)
            gwB = P.at(DMA_G) + P.at(DMA_S)
            for s8 in range(8):
                P.emit(DMA_S, lambda e, j=j, s8=s8: e.dma_start(
                    out=dm_sb[s8][:], in_=dmask[j, s8]),
                    waits=[s_prevexp, P.at("vector") if s8 == 0 else []],
                    inc=16)
            gw_dm = P.at(DMA_S)
            mslots = {8 * j + s8: s8 for s8 in range(8)}
            P.emit("vector", lambda e: e.memset(rm[:], -3.0e38),
                   waits=[P.at("tensor")], inc=1)
            v0 = P.at("vector")
            v_red = {-1: v0, -2: v0}
            idx = 0
            for sub in range(4):
                qcol = j * 512 + sub * 128
                for r in range((klen + 1023) // 1024):
                    kl = min(1024, klen - r * 1024)
                    po = (idx % 2) * 1024
                    nmm = (kl + 511) // 512
                    gw = gwA if (r + 1) * 1024 <= 2048 else gwB
                    P.emit("tensor", lambda e, qcol=qcol, r=r, kl=kl,
                           nmm=nmm, po=po: (
                        [e.matmul(psA[:, po + m * 512:
                                      po + m * 512 + min(512, kl - m * 512)],
                                  xkv[ic][:, q_of + qcol:q_of + qcol + 128],
                                  kbuf[ic][:, r * 1024 + m * 512:
                                           r * 1024 + m * 512 +
                                           min(512, kl - m * 512)],
                                  start=(ic == 0), stop=(ic == 7))
                         for m in range(nmm) for ic in range(8)][-1]),
                        waits=[gw, v_red[idx - 2]], inc=1)
                    t_mm = P.at("tensor")
                    P.emit("vector", lambda e, idx=idx, kl=kl, po=po:
                           e.reduce_max(rm[:, idx:idx + 1], psA[:, po:po + kl],
                                        axis=AX.X),
                           waits=[t_mm], inc=1)
                    v_red[idx] = P.at("vector")
                    idx += 1
            v_p1 = P.at("vector")
            P.emit("tensor", lambda e: e.transpose(psT[0:16, 0:128], rm[:],
                                                   id_f[:]),
                   waits=[v_p1], inc=1)
            P.emit("vector", lambda e: e.reduce_max(msc[0:16, 0:1],
                                                    psT[0:16, 0:128],
                                                    axis=AX.X),
                   waits=[P.at("tensor")], inc=1)
            P.emit("tensor", lambda e: e.transpose(psT[0:1, 0:16],
                                                   msc[0:16, 0:1],
                                                   id_f[0:16, 0:16]),
                   waits=[P.at("vector")], inc=1)
            P.emit("vector", lambda e: e.reduce_max(msc[0:1, 64:65],
                                                    psT[0:1, 0:16],
                                                    axis=AX.X),
                   waits=[P.at("tensor")], inc=1)
            P.emit("tensor", lambda e: e.matmul(psT[0:128, 0:1],
                                                ones_f[0:1, 0:128],
                                                msc[0:1, 64:65],
                                                start=True, stop=True),
                   waits=[P.at("vector")], inc=1)
            P.emit("vector", lambda e: e.tensor_scalar_mul(
                negM[:], psT[0:128, 0:1], -scale),
                waits=[P.at("tensor")], inc=1)
            P.emit("vector", lambda e, j=j: e.tensor_scalar(
                biasall[:, 0:32], kdT[j][:], negM[0:128, 0:1], None,
                ALU.add), chain=True, inc=1)
            v_bias = P.at("vector")
            s_exp = {}
            t_S = {}
            for kt in range(nkt):
                buf = kt % 2
                w = [s_exp[kt - 2]] if kt >= 2 else [s_prevexp, v_bias]
                if kt * 128 >= 2048 and (kt - 2 < 0 or (kt - 2) * 128 < 2048):
                    w.append(gwB)
                P.emit("tensor", lambda e, kt=kt, buf=buf, j=j: (
                    [e.matmul(psS[buf][:, 0:512],
                              kbuf[ic][:, kt * 128:(kt + 1) * 128],
                              xkv[ic][:, q_of + j * 512:q_of + (j + 1) * 512],
                              start=(ic == 0), stop=(ic == 7))
                     for ic in range(8)][-1]),
                    waits=w, inc=1)
                t_S[kt] = P.at("tensor")
                ew = [t_S[kt]]
                if kt in mslots:
                    P.emit("vector", lambda e, buf=buf, s8=mslots[kt]:
                           e.tensor_tensor(psS[buf][:, 0:512],
                                           psS[buf][:, 0:512],
                                           dm_sb[s8][:], ALU.add),
                           waits=[t_S[kt], gw_dm], inc=1)
                    ew.append(P.at("vector"))
                P.emit("scalar", lambda e, kt=kt, buf=buf: e.activation(
                    out=pt_sb[kt][:], in_=psS[buf][:, 0:512],
                    func=AF.Exp, bias=biasall[:, kt:kt + 1],
                    scale=scale),
                    waits=ew, inc=1)
                s_exp[kt] = P.at("scalar")
                P.emit("tensor", lambda e, kt=kt, nkt=nkt: e.matmul(
                    psV[0:1, 0:512], ones_col_bf[:], pt_sb[kt][:],
                    start=(kt == 0), stop=(kt == nkt - 1)),
                    waits=[s_exp[kt]], inc=1)
            t_kfree = P.at("tensor")
            P.emit("scalar", lambda e: e.activation(
                out=msc[0:1, 0:512], in_=psV[0:1, 0:512],
                func=AF.Ln, bias=consts["eps_col"][0:1, 0:1]),
                waits=[t_kfree], inc=1)
            s_l = P.at("scalar")
            P.emit("tensor", lambda e: e.matmul(psT[0:128, 0:512],
                                                ones_f[0:1, 0:128],
                                                msc[0:1, 0:512],
                                                start=True, stop=True),
                   waits=[s_l], inc=1)
            t_bc = P.at("tensor")
            P.emit("scalar", lambda e: e.activation(
                out=rb_sb[0:128, 0:512], in_=psT[0:128, 0:512],
                func=AF.Exp, scale=-1.0),
                waits=[t_bc], inc=1)
            s_rb = P.at("scalar")
            s_prevexp = s_rb
            v_mults = None
            for half in range(2):
                hof = half * 512
                for kt in range(nkt):
                    g = half * nkt + kt
                    slot = g % 4
                    vq = DMA_G if slot % 2 == 0 else DMA_S
                    P.emit(vq, lambda e, kt=kt, hof=hof, slot=slot:
                           e.dma_start(out=vbuf[slot],
                                       in_=v_dram.ap()[kt][:, hof:hof + 512]),
                           waits=[av_free[slot]], inc=16)
                    gwv = P.at(vq)
                    wv = [gwv]
                    if half == 0:
                        wv.append(s_exp[kt])
                        if kt == 0:
                            wv.append(v_p1)
                    elif kt == 0:
                        wv.append(v_mults)
                    P.emit("tensor", lambda e, kt=kt, slot=slot, nkt=nkt: (
                        [e.matmul(psA[:, dc * 512:(dc + 1) * 512],
                                  vbuf[slot][:, dc * 128:(dc + 1) * 128],
                                  pt_sb[kt][:],
                                  start=(kt == 0), stop=(kt == nkt - 1),
                                  skip_group_check=True)
                         for dc in range(4)][-1]),
                        waits=wv, inc=1)
                    av_free[slot] = P.at("tensor")
                t_avs = P.at("tensor")
                for dc2 in range(4):
                    dc = half * 4 + dc2
                    P.emit("vector", lambda e, dc=dc, dc2=dc2, j=j:
                           e.tensor_tensor(
                               xkv[dc][:, ao_of + j * 512:ao_of + (j + 1) * 512],
                               psA[:, dc2 * 512:(dc2 + 1) * 512],
                               rb_sb[0:128, 0:512], ALU.mult),
                           waits=[t_avs, s_rb] if dc2 == 0 else [], inc=1)
                v_mults = P.at("vector")
        P.barrier()


    load_w_proj()
    v_xa = {-1: P.at("vector"), -2: P.at("vector")}
    s_y = {-1: P.at("scalar")}
    st = {-1: P.at(DMA_S)}
    for t in range(NQ // 128):
        pa = (t % 2) * 1024
        P.emit(DMA_G, lambda e, t=t: e.indirect_dma_start(
            out=xo[t % 2][:], out_offset=None, in_=x_src[:, :],
            in_offset=IndirectOffsetOnAxis(idxown_sb[:, t:t + 1], 0)),
            waits=[v_xa[t - 2], ccw_own[t] if ccw_own else ccw], inc=16)
        dmaw = P.at(DMA_G)
        P.emit("tensor", lambda e, t=t, pa=pa: (
            [e.matmul(psA[:, pa + half * 512:pa + (half + 1) * 512],
                      xkv[ic][:, ao_of + t * 128:ao_of + (t + 1) * 128],
                      wt[ic][:, half * 512:(half + 1) * 512],
                      start=(ic == 0), stop=(ic == 7))
             for half in range(2) for ic in range(8)][-1]),
            waits=[v_xa[t - 2]] + ([P.at("scalar")] if t == 0 else []), inc=1)
        t_mm = P.at("tensor")
        P.emit("vector", lambda e, t=t, pa=pa: e.tensor_tensor(
            xa[:], psA[:, pa:pa + 1024], xo[t % 2][:], ALU.add),
            waits=[t_mm, dmaw, s_y[t - 1]], inc=1)
        v_xa[t] = P.at("vector")
        P.emit("vector", lambda e: e.tensor_tensor(xa[:], xa[:], bcast_bp[:],
                                                   ALU.add), chain=True, inc=1)
        P.emit("vector", lambda e: e.bn_stats(ln_s[:, 8:14], xa[:, 0:512]),
               chain=True, inc=1)
        P.emit("vector", lambda e: e.bn_stats(ln_s[:, 14:20], xa[:, 512:1024]),
               chain=True, inc=1)
        P.emit("vector", lambda e: e.bn_aggr(ln_s[:, 0:2], ln_s[:, 8:20]),
               chain=True, inc=1)
        P.emit("vector", lambda e: e.tensor_scalar_add(ln_s[:, 2:3],
                                                       ln_s[:, 1:2], 1e-5),
               chain=True, inc=1)
        v_ln = P.at("vector")
        P.emit("scalar", lambda e: e.activation(out=ln_s[:, 3:4],
                                                in_=ln_s[:, 2:3], func=AF.Sqrt),
               waits=[v_ln], inc=1)
        s_sq2 = P.at("scalar")
        P.emit("vector", lambda e: e.reciprocal(ln_s[:, 4:5], ln_s[:, 3:4]),
               waits=[s_sq2], inc=1)
        P.emit("vector", lambda e: e.tensor_tensor(ln_s[:, 5:6], ln_s[:, 0:1],
                                                   ln_s[:, 4:5], ALU.mult),
               chain=True, inc=1)
        P.emit("vector", lambda e: e.tensor_scalar_mul(ln_s[:, 5:6],
                                                       ln_s[:, 5:6], -1.0),
               chain=True, inc=1)
        v_fin = P.at("vector")
        P.emit("scalar", lambda e, t=t: e.activation(
            out=y_sb[:], in_=xa[:], func=AF.Identity, bias=ln_s[:, 5:6],
            scale=ln_s[:, 4:5]),
            waits=[v_fin, st[t - 1]], inc=1)
        s_y[t] = P.at("scalar")
        P.emit(DMA_S, lambda e, t=t: e.dma_start(
            out=y[t * 128:(t + 1) * 128, :], in_=y_sb[:]),
            waits=[s_y[t]], inc=16)
        st[t] = P.at(DMA_S)
        if ccp is not None and t % 4 == 3:
            ccp(P, t // 4, P.at(DMA_S))
    P.emit(DMA_S, lambda e: None, waits=[P.at(DMA_S)])




def emit_stage_ffn(nc, P, AL, consts, D, last=False):
    NH2 = 2816
    NKV = 2688
    NATT = 2176
    NOWN = 2048
    HIDP = 2816
    x_src = D["x_src"]
    wqkv = D["wqkv"]
    qs_blk = D["qs_blk"]
    ks_blk = D["ks_blk"]
    wo = D["wo"]
    ln_g = D["ln_g"]
    ln_b = D["ln_b"]
    w1a = D["w1a"]
    w1g = D["w1g"]
    b1a = D["b1a"]
    b1g = D["b1g"]
    w2p = D["w2p"]
    b2 = D["b2"]
    kdead = D["kdead"]
    tabs = {tn: D[tn] for tn in ("aq", "bq", "ak", "bk")}
    if last:
        y1_src = D["y1_src"]
    y = D["ydst"]
    xnew_scr = D["xnew_scr"]
    id_b = consts["id_bf"][:]
    id_f = consts["id_f32"]
    ones_f = consts["ones_f32"]
    headsum = consts["headsum"]
    rotT = consts["rotT"]
    ccw = D.get("ccw") or []
    ccw1 = D.get("ccw_y1") or []
    ccw_halo = D.get("ccw_halo")
    ccw_own1 = D.get("ccw_own1")
    ccp = D.get("ccp")
    w1abf = D["w1abf"]
    w1gbf = D["w1gbf"]
    w2bf = D["w2bf"]

    idxh_sb = AL("idxh_sb", [128, 22], I32)
    idxown_sb = AL("idxown2_sb", [128, 16], I32)
    P.emit("sync", lambda e: e.dma_start(out=idxh_sb, in_=D["idx_halo"]), inc=16)
    P.emit("sync", lambda e: e.dma_start(out=idxown_sb, in_=D["idx_own"]),
           inc=16)
    xf = [AL(f"xf{i}", [128, NH2], BF16) for i in range(8)]
    wq_sb = [AL(f"wq{i}", [128, 1536], BF16) for i in range(8)]
    qr = [AL(f"qr{i}", [128, NATT], BF16) for i in range(4)]
    kr = [AL(f"kr{i}", [128, NKV], BF16) for i in range(4)]
    v_sb = [AL(f"vv{i}", [128, 520], BF16) for i in range(21)]

    def hgt(j):
        if j < 20:
            return kr[j % 4][:, (j // 4) * 512:(j // 4) * 512 + 512]
        return qr[j - 20][:, 0:512]
    tbuf_off = AL.off
    tbuf = {tn: AL("tb_" + tn, [128, 512], F32)
            for tn in ("aq", "bq", "ak", "bk")}
    lm_sb = AL.at(tbuf_off, [128, 4096], BF16)
    qhat_off = AL.off
    qhat = AL("qhat", [128, 512], F32)
    t1 = AL("t1", [128, 512], F32)
    t2_off = AL.off
    t2 = AL("t2", [128, 512], F32)
    ssb = AL("ssb", [128, 512], F32)
    sqt = AL("sqt", [128, 512], F32)
    rn = AL("rn", [2, 512], F32)
    pt_sb = [AL(f"pt{i}", [128, 512], BF16) for i in range(2)]
    kdT = AL("kdT", [128, 21], F32)
    kdraw = AL("kdraw", [21, 128], F32)
    biasu = AL("biasu", [128, 21], F32)
    negM = AL("negM", [128, 1], F32)
    msc = AL("msc", [128, 1024], F32)
    rb_sb = AL("rb_sb", [128, 512], F32)
    b1aT = AL("b1aT", [128, 22], F32)
    b1gT = AL("b1gT", [128, 22], F32)
    bT_raw = AL("bT_raw", [22, 128], F32)
    bc_g = AL("bc_g", [128, 1024], F32)
    bc_b = AL("bc_b", [128, 1024], F32)
    bc_b2 = AL("bc_b2", [128, 1024], F32)
    qsb_sb = AL("qsb_sb", [2, 128], F32)
    ksb_sb = AL("ksb_sb", [2, 128], F32)
    w1abuf2 = [wq_sb[0][:, 0:1024], wq_sb[1][:, 0:1024]]
    w1gbuf2 = [wq_sb[2][:, 0:1024], wq_sb[3][:, 0:1024]]
    w2buf2 = [wq_sb[4][:, 0:1024], wq_sb[5][:, 0:1024]]
    xo = [AL(f"xo{i}", [128, 1024], F32) for i in range(2)]
    xn = [AL(f"xn{i}", [128, 1024], F32) for i in range(2)]
    h2 = AL("h2", [128, 1024], F32)
    xa = AL("xa", [128, 1024], F32)
    ag_sb = AL("ag_sb", [128, 512], BF16)
    ag_b = AL("ag_b", [128, 512], BF16)
    sg_b = AL("sg_b", [128, 512], BF16)
    ln_s = AL("ln_s", [128, 24], F32)
    y_sb = AL("y_sb", [128, 1024], F32)
    sq2 = y_sb

    psA = D["psA"]
    psS = D["psS"]
    psV = D["psV"]
    psT = D["psT"]
    psTb = psT.bitcast(BF16)

    DMA_G, DMA_S = "gpsimd", "sync"

    xs2 = [sqt.bitcast(BF16)[:, 0:1024], qhat.bitcast(BF16)[:, 0:1024]]
    pss2 = [psTb[0:128, 0:1024], psS[0].bitcast(BF16)[:, 0:1024]]
    t_tr = {-1: P.at("tensor"), -2: P.at("tensor")}
    v_cp = {-1: P.at("vector"), -2: P.at("vector")}
    for t in range(NH2 // 128):
        tb = t % 2
        w = [t_tr[t - 2], ccw_halo[t] if ccw_halo else ccw]
        if t == 0:
            w += [P.at("sync"), P.at("vector")]
        P.emit(DMA_G, lambda e, t=t, tb=tb: e.indirect_dma_start(
            out=xs2[tb][:], out_offset=None, in_=x_src[:, :],
            in_offset=IndirectOffsetOnAxis(idxh_sb[:, t:t + 1], 0)),
            waits=w, inc=16)
        gw = P.at(DMA_G)
        P.emit("tensor", lambda e, tb=tb: (
            [e.transpose(pss2[tb][:, ic * 128:(ic + 1) * 128],
                         xs2[tb][:, ic * 128:(ic + 1) * 128], id_b)
             for ic in range(8)][-1]),
            waits=[gw, v_cp[t - 2]], inc=1)
        t_tr[t] = P.at("tensor")
        P.emit("vector", lambda e, t=t, tb=tb: (
            [e.tensor_copy(xf[ic][:, t * 128:(t + 1) * 128],
                           pss2[tb][:, ic * 128:(ic + 1) * 128])
             for ic in range(8)][-1]),
            waits=[t_tr[t]], inc=1)
        v_cp[t] = P.at("vector")

    for ic in range(8):
        P.emit(DMA_G, lambda e, ic=ic: e.dma_start(
            out=wq_sb[ic][:], in_=wqkv[ic * 128:(ic + 1) * 128, :]), inc=16)
    P.emit(DMA_S, lambda e: e.dma_start(out=kdraw[:], in_=kdead), inc=16)
    P.emit(DMA_S, lambda e: e.dma_start(out=qsb_sb[:], in_=qs_blk), inc=16)
    P.emit(DMA_S, lambda e: e.dma_start(out=ksb_sb[:], in_=ks_blk), inc=16)
    P.barrier()
    P.emit(DMA_G, lambda e: e.dma_start(out=w1abf.ap(), in_=w1a), inc=16)
    P.emit(DMA_G, lambda e: e.dma_start(out=w1gbf.ap(), in_=w1g), inc=16)
    for c4 in range(4):
        P.emit(DMA_G, lambda e, c4=c4: e.dma_start(
            out=w2bf.ap()[c4 * 704:(c4 + 1) * 704, :],
            in_=w2p[c4 * 704:(c4 + 1) * 704, :]), inc=16)
    gw_cast = P.at(DMA_G)
    P.emit("tensor", lambda e: e.transpose(psT[0:128, 0:21], kdraw[:],
                                           id_f[0:21, 0:21]), inc=1)
    P.emit("vector", lambda e: e.tensor_copy(kdT[:], psT[0:128, 0:21]),
           waits=[P.at("tensor")], inc=1)
    P.barrier()

    def hcol(c, c0, w):
        off = 128 if c < 4 else 127
        return xf[c][:, off + c0: off + c0 + w]

    rst = {"s_sq": P.at("scalar"), "v_qhat": P.at("vector"),
           "t_hs": P.at("tensor"), "v_max": P.at("vector"),
           "t_scale": P.at("tensor"), "v_fin": P.at("vector")}

    def rotary_block(dst, oc, c0, w, qk, tile_cols):
        A = tbuf["a" + qk]
        B = tbuf["b" + qk]
        sblk = qsb_sb if qk == "q" else ksb_sb
        third = 0 if qk == "q" else 1
        pos0 = 128 + tile_cols
        P.emit(DMA_S, lambda e: e.dma_start(
            out=A[:, 0:w], in_=tabs["a" + qk][:, pos0:pos0 + w]),
            waits=[rst["v_fin"]], inc=16)
        P.emit(DMA_S, lambda e: e.dma_start(
            out=B[:, 0:w], in_=tabs["b" + qk][:, pos0:pos0 + w]),
            waits=[rst["v_fin"]], inc=16)
        dmw = P.at(DMA_S)
        P.emit("tensor", lambda e: (
            [e.matmul(psA[:, 0:w], wq_sb[ic][:, third * 512 + oc * 128:
                                             third * 512 + (oc + 1) * 128],
                      hcol(ic, tile_cols, w), start=(ic == 0), stop=(ic == 7))
             for ic in range(8)][-1]),
            waits=[rst["s_sq"], rst["v_qhat"]], inc=1)
        t_mm = P.at("tensor")
        P.emit("scalar", lambda e: e.activation(out=sqt[:, 0:w], in_=psA[:, 0:w],
                                                func=AF.Square),
               waits=[t_mm, rst["t_hs"]], inc=1)
        rst["s_sq"] = P.at("scalar")
        P.emit("tensor", lambda e: e.matmul(psT[0:2, 0:w], headsum[:],
                                            sqt[:, 0:w], start=True, stop=True),
               waits=[rst["s_sq"], rst["v_max"]], inc=1)
        rst["t_hs"] = P.at("tensor")
        P.emit("vector", lambda e: e.tensor_scalar_max(rn[0:2, 0:w],
                                                       psT[0:2, 0:w], 1e-24),
               waits=[rst["t_hs"], rst["t_scale"]], inc=1)
        rst["v_max"] = P.at("vector")
        P.emit("scalar", lambda e: e.activation(out=rn[0:2, 0:w], in_=rn[0:2, 0:w],
                                                func=AF.Sqrt),
               waits=[rst["v_max"]], inc=1)
        s_sqrt = P.at("scalar")
        P.emit("vector", lambda e: e.reciprocal(rn[0:2, 0:w], rn[0:2, 0:w]),
               waits=[s_sqrt], inc=1)
        v_recip = P.at("vector")
        P.emit("tensor", lambda e: e.matmul(psS[0][:, 0:w], sblk[:], rn[0:2, 0:w],
                                            start=True, stop=True),
               waits=[v_recip], inc=1)
        rst["t_scale"] = P.at("tensor")
        P.emit("vector", lambda e: e.tensor_copy(ssb[:, 0:w], psS[0][:, 0:w]),
               waits=[rst["t_scale"]], inc=1)
        P.emit("vector", lambda e: e.tensor_tensor(qhat[:, 0:w], psA[:, 0:w],
                                                   ssb[:, 0:w], ALU.mult),
               chain=True, inc=1)
        rst["v_qhat"] = P.at("vector")
        P.emit("tensor", lambda e: e.matmul(psS[1][:, 0:w],
                                            rotT[:], qhat[:, 0:w],
                                            start=True, stop=True),
               waits=[rst["v_qhat"]], inc=1)
        t_rot = P.at("tensor")
        P.emit("vector", lambda e: e.tensor_tensor(t1[:, 0:w], qhat[:, 0:w],
                                                   A[:, 0:w], ALU.mult),
               waits=[dmw], chain=True, inc=1)
        P.emit("vector", lambda e: e.tensor_tensor(t2[:, 0:w], psS[1][:, 0:w],
                                                   B[:, 0:w], ALU.mult),
               waits=[t_rot], chain=True, inc=1)
        P.emit("vector", lambda e, dst=dst, oc=oc, c0=c0: e.tensor_tensor(
            dst[oc][:, c0:c0 + w], t1[:, 0:w], t2[:, 0:w], ALU.add),
            chain=True, inc=1)
        rst["v_fin"] = P.at("vector")

    for oc in range(4):
        c0 = 0
        while c0 < NATT:
            w = min(512, NATT - c0)
            rotary_block(qr, oc, c0, w, "q", 512 + c0)
            c0 += w
    for oc in range(4):
        c0 = 0
        while c0 < NKV:
            w = min(512, NKV - c0)
            rotary_block(kr, oc, c0, w, "k", c0)
            c0 += w

    for t in range(21):
        P.emit("tensor", lambda e, t=t: (
            [e.matmul(psA[:, 0:512],
                      hcol(ic, t * 128, 128),
                      wq_sb[ic][:, 1024:1536],
                      start=(ic == 0), stop=(ic == 7)) for ic in range(8)][-1]),
            waits=[P.at("vector")], inc=1)
        P.emit("vector", lambda e, t=t: e.tensor_copy(
            v_sb[t][:].rearrange("p (h c) -> p h c", c=65)[:, :, 0:64],
            psA[:, 0:512].rearrange("p (h c) -> p h c", c=64)),
            waits=[P.at("tensor")], inc=1)
        P.emit("vector", lambda e, t=t: e.memset(
            v_sb[t][:].rearrange("p (h c) -> p h c", c=65)[:, :, 64:65], 1.0),
            inc=1)
    P.barrier()

    ao_of = 0

    def lattn_unit(hp, hh, h, qt0, G):
        nm = G + 4
        QW = G * 128
        q_ap = qr[hp][hh:hh + 64, qt0 * 128:(qt0 + G) * 128]
        v0 = P.at("vector")
        v_red = {-1: v0, -2: v0}
        for s in range(G):
            po = (s % 2) * 1024
            kw = nm * 128
            qs = qr[hp][hh:hh + 64, (qt0 + s) * 128:(qt0 + s + 1) * 128]
            P.emit("tensor", lambda e, qs=qs, po=po, kw=kw, hp=hp, hh=hh: (
                [e.matmul(psA[:, po + m5 * 512:po + m5 * 512 +
                              min(512, kw - m5 * 512)],
                          qs, kr[hp][hh:hh + 64,
                                     qt0 * 128 + m5 * 512:
                                     qt0 * 128 + m5 * 512 +
                                     min(512, kw - m5 * 512)],
                          start=True, stop=True)
                 for m5 in range((kw + 511) // 512)][-1]),
                waits=[v_red[s - 2]], inc=1)
            t_mm = P.at("tensor")
            P.emit("vector", lambda e, s=s, po=po, kw=kw: e.reduce_max(
                msc[:, s:s + 1], psA[:, po:po + kw], axis=AX.X),
                waits=[t_mm], inc=1)
            v_red[s] = P.at("vector")
        P.emit("tensor", lambda e, G=G: e.transpose(
            psT[0:G, 0:128], msc[:, 0:G], id_f[:]),
            waits=[P.at("vector")], inc=1)
        P.emit("vector", lambda e, G=G: e.reduce_max(
            msc[0:G, 8:9], psT[0:G, 0:128], axis=AX.X),
            waits=[P.at("tensor")], inc=1)
        P.emit("tensor", lambda e, G=G: e.transpose(
            psT[0:1, 0:G], msc[0:G, 8:9], id_f[0:G, 0:G]),
            waits=[P.at("vector")], inc=1)
        P.emit("vector", lambda e, G=G: e.reduce_max(
            msc[0:1, 8:9], psT[0:1, 0:G], axis=AX.X),
            waits=[P.at("tensor")], inc=1)
        P.emit("tensor", lambda e: e.matmul(psT[0:128, 0:1], ones_f[0:1, 0:128],
                                            msc[0:1, 8:9], start=True,
                                            stop=True),
               waits=[P.at("vector")], inc=1)
        P.emit("vector", lambda e: e.tensor_scalar_mul(negM[:],
                                                       psT[0:128, 0:1], -8.0),
               waits=[P.at("tensor")], inc=1)
        P.emit("vector", lambda e, qt0=qt0, nm=nm: e.tensor_scalar(
            biasu[:, 0:nm], kdT[:, qt0:qt0 + nm], negM[0:128, 0:1], None,
            ALU.add), chain=True, inc=1)
        v_bias = P.at("vector")
        s_prev = P.at("scalar")
        t_S = {}
        t_AV = {}
        s_exp = {}

        def emit_S(m):
            w = [s_prev, v_bias] if m < 2 else []
            P.emit("tensor", lambda e, m=m, q_ap=q_ap, hp=hp, hh=hh: e.matmul(
                psS[m % 2][:, 0:QW],
                kr[hp][hh:hh + 64, (qt0 + m) * 128:(qt0 + m + 1) * 128],
                q_ap, start=True, stop=True), waits=w, inc=1)
            t_S[m] = P.at("tensor")

        emit_S(0)
        for m in range(nm):
            buf = m % 2
            if m + 1 < nm:
                emit_S(m + 1)
            ew = [t_S[m]]
            if G == 4:
                tris = []
                for s_ in range(4):
                    ms = m - s_
                    if ms == 0:
                        tris.append((s_, "tri_hi"))
                    elif ms == 4:
                        tris.append((s_, "tri_lo"))
                if tris:
                    P.emit("vector", lambda e, buf=buf, tris=tuple(tris): (
                        [e.tensor_tensor(psS[buf][:, s_ * 128:(s_ + 1) * 128],
                                         psS[buf][:, s_ * 128:(s_ + 1) * 128],
                                         consts[tn][:], ALU.add)
                         for s_, tn in tris][-1]),
                        waits=[t_S[m]], inc=1)
                    ew.append(P.at("vector"))
            elif m == 0 or m == nm - 1:
                tri = consts["tri_hi"] if m == 0 else consts["tri_lo"]
                P.emit("vector", lambda e, m=m, buf=buf, tri=tri: e.tensor_tensor(
                    psS[buf][:, 0:QW], psS[buf][:, 0:QW], tri[:], ALU.add),
                    waits=[t_S[m]], inc=1)
                ew.append(P.at("vector"))
            if m >= 2:
                ew.append(t_AV[m - 2])
            P.emit("scalar", lambda e, m=m, buf=buf: e.activation(
                out=pt_sb[buf][:, 0:QW], in_=psS[buf][:, 0:QW], func=AF.Exp,
                bias=biasu[:, m:m + 1], scale=8.0),
                waits=ew, inc=1)
            s_exp[m] = P.at("scalar")
            avw = [s_exp[m]]
            if G == 4 and (m <= 2 or m >= 5):
                lo, hi = ((m + 1) * 128, 512) if m <= 2 else (0, (m - 4) * 128)
                P.emit("vector", lambda e, buf=buf, lo=lo, hi=hi: e.memset(
                    pt_sb[buf][:, lo:hi], 0.0),
                    waits=[s_exp[m]], inc=1)
                avw.append(P.at("vector"))
            P.emit("tensor", lambda e, m=m, buf=buf, h=h, nm=nm: e.matmul(
                psV[0:65, 0:QW], v_sb[qt0 + m][:, h * 65:h * 65 + 65],
                pt_sb[buf][:, 0:QW], start=(m == 0), stop=(m == nm - 1)),
                waits=avw, inc=1)
            t_AV[m] = P.at("tensor")
        P.emit("scalar", lambda e: e.activation(
            out=msc[64:65, 512:512 + QW], in_=psV[64:65, 0:QW],
            func=AF.Ln, bias=consts["eps_col"][64:65, 0:1]),
            waits=[P.at("tensor")], inc=1)
        s_l = P.at("scalar")
        P.emit("tensor", lambda e: e.matmul(psT[0:64, 0:QW],
                                            ones_f[64:65, 0:64],
                                            msc[64:65, 512:512 + QW],
                                            start=True, stop=True),
               waits=[s_l], inc=1)
        t_bc = P.at("tensor")
        P.emit("scalar", lambda e: e.activation(
            out=rb_sb[0:64, 0:QW], in_=psT[0:64, 0:QW],
            func=AF.Exp, scale=-1.0),
            waits=[t_bc], inc=1)
        s_rb = P.at("scalar")
        P.emit("vector", lambda e, hp=hp, hh=hh, qt0=qt0: e.tensor_tensor(
            xf[hp][hh:hh + 64, qt0 * 128:qt0 * 128 + QW],
            psV[0:64, 0:QW], rb_sb[0:64, 0:QW], ALU.mult),
            waits=[s_rb], inc=1)

    for qt0, G in ((0, 4), (4, 4), (8, 4), (12, 4), (16, 1)):
        for h in range(8):
            lattn_unit(h // 2, (h % 2) * 64, h, qt0, G)
    P.barrier()

    for ic in range(4):
        P.emit(DMA_G, lambda e, ic=ic: e.dma_start(
            out=wq_sb[ic][:, 0:1024], in_=wo[ic * 128:(ic + 1) * 128, :]),
            waits=[P.at("tensor")], inc=16)
    P.barrier()
    for tq in range(17):
        P.emit(DMA_G, lambda e, tq=tq: e.indirect_dma_start(
            out=xo[tq % 2][:], out_offset=None, in_=x_src[:, :],
            in_offset=IndirectOffsetOnAxis(idxh_sb[:, 5 + tq:6 + tq], 0)),
            waits=[P.at("vector"),
                   ccw_halo[5 + tq] if ccw_halo else ccw], inc=16)
        dmw = P.at(DMA_G)
        for half in range(2):
            P.emit("tensor", lambda e, tq=tq, half=half: (
                [e.matmul(psA[:, half * 512:(half + 1) * 512],
                          xf[ic][:, tq * 128:(tq + 1) * 128],
                          wq_sb[ic][:, half * 512:(half + 1) * 512],
                          start=(ic == 0), stop=(ic == 3)) for ic in range(4)][-1]),
                waits=[P.at("vector")], inc=1)
        P.emit("vector", lambda e, tq=tq: e.tensor_tensor(
            xn[tq % 2][:], psA[:, 0:1024], xo[tq % 2][:], ALU.add),
            waits=[P.at("tensor"), dmw, P.at(DMA_S)], inc=1)
        P.emit(DMA_S, lambda e, tq=tq: e.dma_start(
            out=xnew_scr.ap()[tq], in_=xn[tq % 2][:]),
            waits=[P.at("vector")], inc=16)
    P.barrier()

    for (vec, dst) in ((ln_g, bc_g), (ln_b, bc_b), (b2, bc_b2)):
        P.emit(DMA_S, lambda e, vec=vec: e.dma_start(out=msc[0:1, 0:1024],
                                                     in_=vec.unsqueeze(0)),
               waits=[P.at("tensor")], inc=16)
        P.barrier()
        for half in range(2):
            P.emit("tensor", lambda e, half=half: e.matmul(
                psT[0:128, 0:512], ones_f[0:1, 0:128],
                msc[0:1, half * 512:(half + 1) * 512], start=True, stop=True),
                waits=[P.at("vector")], inc=1)
            P.emit("vector", lambda e, dst=dst, half=half: e.tensor_copy(
                dst[:, half * 512:(half + 1) * 512], psT[0:128, 0:512]),
                waits=[P.at("tensor")], inc=1)
        P.barrier()
    for (vec, dst) in ((b1a, b1aT), (b1g, b1gT)):
        P.emit(DMA_S, lambda e, vec=vec: e.dma_start(
            out=bT_raw[:], in_=vec.rearrange("(c p) -> c p", p=128)),
            waits=[P.at("tensor")], inc=16)
        P.barrier()
        P.emit("tensor", lambda e: e.transpose(psT[0:128, 0:22], bT_raw[:],
                                               id_f[0:22, 0:22]), inc=1)
        P.emit("vector", lambda e, dst=dst: e.tensor_copy(dst[:],
                                                          psT[0:128, 0:22]),
               waits=[P.at("tensor")], inc=1)
        P.barrier()

    h2b2 = [h2, AL.at(qhat_off, [128, 1024], F32)]
    xa2b = [xa, AL.at(t2_off, [128, 1024], F32)]
    pst2 = [psTb[0:128, 0:1024], psS[0].bitcast(BF16)[:, 0:1024]]
    t_sh = {-1: P.at("tensor"), 0: P.at("tensor")}
    t_tr = {-1: P.at("tensor"), 0: P.at("tensor")}
    v_h2cp = {-1: P.at("vector"), 0: P.at("vector")}
    v_cast = {-1: P.at("vector"), 0: P.at("vector")}
    v_xfcp = {-1: P.at("vector"), 0: P.at("vector")}
    s_act = {-1: P.at("scalar"), 0: P.at("scalar")}
    for to in range(1, 17):
        tb = to % 2
        P.emit(DMA_S, lambda e, to=to: e.dma_start(
            out=xn[to % 2][:], in_=xnew_scr.ap()[to]),
            waits=[t_sh[to - 1], v_h2cp[to - 2]] +
                  ([P.at("vector")] if to == 1 else []),
            inc=16)
        if to == 1:
            P.emit(DMA_S, lambda e: e.dma_start(out=xn[0][:],
                                                in_=xnew_scr.ap()[0]),
                   inc=16)
        dmw = P.at(DMA_S)
        po = tb * 512
        P.emit("tensor", lambda e, to=to, po=po: (
            e.matmul(psA[:, po:po + 512], consts["shift_up"][:],
                     xn[to % 2][:, 512:1024], start=True, stop=False),
            e.matmul(psA[:, po:po + 512], consts["last2first"][:],
                     xn[(to - 1) % 2][:, 512:1024], start=False, stop=True))[-1],
            waits=[dmw, v_h2cp[to - 2]], inc=1)
        t_sh[to] = P.at("tensor")
        P.emit("vector", lambda e, to=to, tb=tb: e.tensor_copy(
            h2b2[tb][:, 0:512], xn[to % 2][:, 0:512]),
            waits=[dmw, s_act[to - 2]], inc=1)
        P.emit("vector", lambda e, tb=tb, po=po: e.tensor_copy(
            h2b2[tb][:, 512:1024], psA[:, po:po + 512]),
            waits=[t_sh[to]], inc=1)
        v_h2cp[to] = P.at("vector")
        P.emit("vector", lambda e, tb=tb: e.bn_stats(ln_s[:, 8:14],
                                                     h2b2[tb][:, 0:512]),
               chain=True, inc=1)
        P.emit("vector", lambda e, tb=tb: e.bn_stats(ln_s[:, 14:20],
                                                     h2b2[tb][:, 512:1024]),
               chain=True, inc=1)
        P.emit("vector", lambda e: e.bn_aggr(ln_s[:, 0:2], ln_s[:, 8:20]),
               chain=True, inc=1)
        P.emit("vector", lambda e: e.tensor_scalar_add(ln_s[:, 2:3],
                                                       ln_s[:, 1:2], 1e-5),
               chain=True, inc=1)
        v_ln = P.at("vector")
        P.emit("scalar", lambda e: e.activation(out=ln_s[:, 3:4],
                                                in_=ln_s[:, 2:3], func=AF.Sqrt),
               waits=[v_ln], inc=1)
        s_sq4 = P.at("scalar")
        P.emit("vector", lambda e: e.reciprocal(ln_s[:, 4:5], ln_s[:, 3:4]),
               waits=[s_sq4], inc=1)
        P.emit("vector", lambda e: e.tensor_tensor(ln_s[:, 5:6], ln_s[:, 0:1],
                                                   ln_s[:, 4:5], ALU.mult),
               chain=True, inc=1)
        P.emit("vector", lambda e: e.tensor_scalar_mul(ln_s[:, 5:6],
                                                       ln_s[:, 5:6], -1.0),
               chain=True, inc=1)
        v_fin = P.at("vector")
        P.emit("scalar", lambda e, tb=tb: e.activation(
            out=xa2b[tb][:], in_=h2b2[tb][:], func=AF.Identity,
            bias=ln_s[:, 5:6], scale=ln_s[:, 4:5]),
            waits=[v_fin, v_cast[to - 2]], inc=1)
        s_act[to] = P.at("scalar")
        P.emit("vector", lambda e, tb=tb: e.tensor_tensor(
            xa2b[tb][:], xa2b[tb][:], bc_g[:], ALU.mult),
            waits=[s_act[to]], inc=1)
        P.emit("vector", lambda e, tb=tb: e.tensor_tensor(
            xa2b[tb][:], xa2b[tb][:], bc_b[:], ALU.add), chain=True, inc=1)
        P.emit("vector", lambda e, tb=tb: e.tensor_copy(
            ag2[tb][:], xa2b[tb][:, 0:512]),
            waits=[t_tr[to - 2]], chain=True, inc=1)
        P.emit("vector", lambda e, tb=tb: e.tensor_copy(
            sg2[tb][:], xa2b[tb][:, 512:1024]),
            chain=True, inc=1)
        v_cast[to] = P.at("vector")
        P.emit("tensor", lambda e, tb=tb: (
            [e.transpose(pst2[tb][:, ic * 128:(ic + 1) * 128],
                         (ag2[tb] if ic < 4 else sg2[tb])[:, (ic % 4) * 128:
                                                          (ic % 4) * 128 + 128],
                         id_b) for ic in range(8)][-1]),
            waits=[v_cast[to], v_xfcp[to - 2]], inc=1)
        t_tr[to] = P.at("tensor")
        P.emit("vector", lambda e, to=to, tb=tb: (
            [e.tensor_copy(xf[ic][:, (to - 1) * 128:to * 128],
                           pst2[tb][:, ic * 128:(ic + 1) * 128])
             for ic in range(8)][-1]),
            waits=[t_tr[to]], inc=1)
        v_xfcp[to] = P.at("vector")
    P.barrier()

    w1ab3 = [wq_sb[0][:, 0:1024], wq_sb[1][:, 0:1024], wq_sb[6][:, 0:1024]]
    w1gb3 = [wq_sb[2][:, 0:1024], wq_sb[3][:, 0:1024], wq_sb[7][:, 0:1024]]
    ag2 = [ag_sb, ag_b]
    sg2 = [sqt.bitcast(BF16)[:, 0:512], sg_b]
    for g in range(4):
        gcol = g * 512
        t0 = P.at("tensor")
        wtoks = {}
        ctoks = {-1: t0, -2: t0, -3: t0}
        s_g = {-1: P.at("scalar"), -2: P.at("scalar")}
        v_m = {-1: P.at("vector"), -2: P.at("vector")}
        h1slots = [(psA[:, 0:512], psA[:, 512:1024]),
                   (psA[:, 1024:1536], psA[:, 1536:2048]),
                   (psS[0][:, 0:512], psS[1][:, 0:512]),
                   (psV[0:128, 0:512], psT[0:128, 0:512])]
        s_g.update({-3: s_g[-1], -4: s_g[-1]})
        for j in range(22 + 2):
            if j < 22:
                jb = j % 3
                vq = DMA_G if j % 2 == 0 else DMA_S
                P.emit(vq, lambda e, j=j, jb=jb: e.dma_start(
                    out=w1ab3[jb], in_=w1abf.ap()[:, j * 1024:(j + 1) * 1024]),
                    waits=[ctoks[j - 3], gw_cast if j < 3 else []], inc=16)
                P.emit(vq, lambda e, j=j, jb=jb: e.dma_start(
                    out=w1gb3[jb], in_=w1gbf.ap()[:, j * 1024:(j + 1) * 1024]),
                    waits=[ctoks[j - 3]], inc=16)
                wtoks[j] = P.at(vq)
            if j < 2:
                continue
            jj = j - 2
            jb = jj % 3
            sa, sgp = h1slots[jj % 4]
            gw = wtoks[jj]
            P.emit("tensor", lambda e, jj=jj, jb=jb, gcol=gcol, sa=sa: (
                [e.matmul(sa,
                          w1ab3[jb][:, ic * 128:(ic + 1) * 128],
                          xf[ic][:, gcol:gcol + 512],
                          start=(ic == 0), stop=(ic == 7)) for ic in range(8)][-1]),
                waits=[gw, s_g[jj - 4]] +
                      ([P.at("vector")] if jj < 4 and g == 0 else
                       [P.at("vector")] if jj < 2 else []),
                inc=1)
            P.emit("tensor", lambda e, jj=jj, jb=jb, gcol=gcol, sgp=sgp: (
                [e.matmul(sgp,
                          w1gb3[jb][:, ic * 128:(ic + 1) * 128],
                          xf[ic][:, gcol:gcol + 512],
                          start=(ic == 0), stop=(ic == 7)) for ic in range(8)][-1]),
                inc=1)
            ctoks[jj] = P.at("tensor")
            P.emit("scalar", lambda e, j=jj, sa=sa: e.activation(
                out=ag2[j % 2][:], in_=sa, func=AF.Identity,
                bias=b1aT[:, j:j + 1], scale=1.0),
                waits=[ctoks[jj], v_m[jj - 2]], inc=1)
            P.emit("scalar", lambda e, j=jj, sgp=sgp: e.activation(
                out=sg2[j % 2][:], in_=sgp,
                func=AF.Gelu, bias=b1gT[:, j:j + 1], scale=1.0),
                chain=True, inc=1)
            s_g[jj] = P.at("scalar")
            P.emit("vector", lambda e, j=jj: e.tensor_tensor(
                hgt(j), ag2[j % 2][:], sg2[j % 2][:], ALU.mult),
                waits=[s_g[jj]], inc=1)
            v_m[jj] = P.at("vector")
        for hgp in range(2):
            w2toks = {-1: P.at("tensor"), -2: P.at("tensor")}
            for j in range(22 + 1):
                if j < 22:
                    vq2 = DMA_S if j % 2 == 0 else DMA_G
                    P.emit(vq2, lambda e, j=j: e.dma_start(
                        out=w2buf2[j % 2],
                        in_=w2bf.ap()[j * 128:(j + 1) * 128, :]),
                        waits=[w2toks.get(j - 2, []),
                               gw_cast if j < 2 else []], inc=16)
                    w2toks[j] = P.at(vq2)
                if j == 0:
                    continue
                jj = j - 1
                gw2 = w2toks[jj]
                for ts2 in range(2):
                    ts = hgp * 2 + ts2
                    for half in range(2):
                        P.emit("tensor", lambda e, j=jj, ts=ts, ts2=ts2,
                               half=half: e.matmul(
                                   psA[:, ts2 * 1024 + half * 512:
                                       ts2 * 1024 + (half + 1) * 512],
                                   hgt(j)[:, ts * 128:(ts + 1) * 128],
                                   w2buf2[j % 2][:, half * 512:(half + 1) * 512],
                                   start=(j == 0), stop=(j == 21)),
                               waits=[gw2] + ([P.at("vector"), s_g[21]]
                                              if j == 1 else []),
                               inc=1)
                w2toks[jj] = P.at("tensor")
            for ts2 in range(2):
                ts = hgp * 2 + ts2
                to = g * 4 + ts + 1
                P.emit(DMA_S, lambda e, to=to: e.dma_start(
                    out=xn[to % 2][:], in_=xnew_scr.ap()[to]),
                    waits=[P.at("vector")], inc=16)
                dmw = (P.at(DMA_S),)
                if last:
                    P.emit(DMA_G, lambda e, to=to: e.indirect_dma_start(
                        out=xo[to % 2][:], out_offset=None, in_=y1_src[:, :],
                        in_offset=IndirectOffsetOnAxis(
                            idxown_sb[:, to - 1:to], 0)),
                        waits=[P.at("vector"),
                               ccw_own1[to - 1] if ccw_own1 else ccw1],
                        inc=16)
                dmw = (P.at(DMA_S), P.at(DMA_G))
                P.emit("vector", lambda e, ts2=ts2, to=to: e.tensor_tensor(
                    xa[:], psA[:, ts2 * 1024:(ts2 + 1) * 1024], xn[to % 2][:],
                    ALU.add),
                    waits=[P.at("tensor")] + list(dmw), inc=1)
                P.emit("vector", lambda e: e.tensor_tensor(xa[:], xa[:],
                                                           bc_b2[:], ALU.add),
                       chain=True, inc=1)
                if last:
                    P.emit("vector", lambda e, to=to: e.tensor_tensor(
                        xa[:], xa[:], xo[to % 2][:], ALU.add),
                        chain=True, inc=1)
                    P.emit("vector", lambda e: e.bn_stats(ln_s[:, 8:14], xa[:, 0:512]),
                           waits=[], chain=True, inc=1)
                    P.emit("vector", lambda e: e.bn_stats(ln_s[:, 14:20], xa[:, 512:1024]),
                           chain=True, inc=1)
                    P.emit("vector", lambda e: e.bn_aggr(ln_s[:, 0:2], ln_s[:, 8:20]),
                           chain=True, inc=1)
                    P.emit("vector", lambda e: e.tensor_scalar_add(ln_s[:, 2:3],
                                                                   ln_s[:, 1:2], 1e-5),
                           chain=True, inc=1)
                    P.emit("scalar", lambda e: e.activation(out=ln_s[:, 3:4],
                                                            in_=ln_s[:, 2:3], func=AF.Sqrt),
                           waits=[P.at("vector")], inc=1)
                    P.emit("vector", lambda e: e.reciprocal(ln_s[:, 4:5], ln_s[:, 3:4]),
                           waits=[P.at("scalar")], inc=1)
                    P.emit("vector", lambda e: e.tensor_tensor(ln_s[:, 5:6], ln_s[:, 0:1],
                                                               ln_s[:, 4:5], ALU.mult),
                           chain=True, inc=1)
                    P.emit("vector", lambda e: e.tensor_scalar_mul(ln_s[:, 5:6],
                                                                   ln_s[:, 5:6], -1.0),
                           chain=True, inc=1)
                    P.emit("scalar", lambda e: e.activation(
                        out=y_sb[:], in_=xa[:], func=AF.Identity,
                        bias=ln_s[:, 5:6], scale=ln_s[:, 4:5]),
                        waits=[P.at("vector"), P.at(DMA_S)], inc=1)
                    P.emit(DMA_S, lambda e, to=to: e.dma_start(
                        out=y[(to - 1) * 128:to * 128, :], in_=y_sb[:]),
                        waits=[P.at("scalar")], inc=16)
                else:
                    P.emit(DMA_S, lambda e, to=to: e.dma_start(
                        out=y[(to - 1) * 128:to * 128, :], in_=xa[:]),
                        waits=[P.at("vector")], inc=16)
        if ccp is not None:
            ccp(P, g, P.at(DMA_S))
    P.emit(DMA_S, lambda e: None, waits=[P.at(DMA_S)])




def emit_stage_lstm(nc, P, AL, consts, D):
    NT = 512
    x_src = D["x_src"]
    wnames = ["wf", "uf", "wi", "ui", "wc", "uc", "wo", "uo"]
    wd = {n: D[n] for n in wnames}
    vnames = ["bf_v", "bi_v", "bc_v", "bo_v", "in_c"]
    vd = {n: D[n] for n in vnames}
    y = D["ydst"]
    id_b = consts["id_bf"][:]
    id_f = consts["id_f32"]
    ones_f = consts["ones_f32"]
    ccw = D.get("ccw") or []

    idxl_sb = AL("idxl_sb", [128, 16], I32)
    P.emit("sync", lambda e: e.dma_start(out=idxl_sb, in_=D["idx_lstm"]), inc=16)
    xf = [[AL(f"xf{t}_{i}", [128, NT], BF16) for i in range(8)]
          for t in range(4)]
    wT = [[AL(f"w{n}_{i}", [128, 1024], BF16)
           for i in range(8)] for n in range(6)]
    c_a = [AL(f"ca{i}", [128, NT], F32) for i in range(8)]
    c_b = [AL(f"cb{i}", [128, NT], F32) for i in range(8)]
    fga = AL("fga", [128, NT], F32)
    fgb = AL("fgb", [128, NT], F32)
    fgc = AL("fgc", [128, NT], F32)
    wrow = AL("wrow", [128, 1024], BF16)
    bT = AL("bT", [128, 8 * 5], F32)
    bT_raw = AL("bT_raw", [8, 128], F32)
    bc_bo = AL("bc_bo", [128, 1024], F32)
    msc = AL("msc", [128, 1024], F32)
    xo = [AL(f"xo{i}", [128, 1024], F32) for i in range(2)]
    xa = AL("xa", [128, 1024], F32)
    ln_s = AL("ln_s", [128, 24], F32)
    y_sb = AL("y_sb", [128, 1024], F32)
    sq2 = y_sb
    th = y_sb

    cb16 = [AL(f"cb16_{i}", [128, NT], BF16) for i in range(8)]
    psA = D["psA"]
    psT = D["psT"]
    psTb = psT.bitcast(BF16)

    DMA_G, DMA_S = "gpsimd", "sync"

    def load_wT(dst, name):
        for ic in range(8):
            P.emit(DMA_G, lambda e, dst=dst, ic=ic, name=name: e.dma_start(
                out=dst[ic][:],
                in_=wd[name][ic * 128:(ic + 1) * 128, :]),
                waits=[P.at("tensor")], inc=16)
    for n6 in range(6):
        load_wT(wT[n6], wnames[n6])
    gw_w6 = P.at(DMA_G)

    xs2 = [wrow, msc.bitcast(BF16)[:, 0:1024]]
    pss2 = [psTb[0:128, 0:1024], D["psS"][0].bitcast(BF16)[:, 0:1024]]
    t_tr = {-1: P.at("tensor"), -2: P.at("tensor")}
    v_cp = {-1: P.at("vector"), -2: P.at("vector")}
    for t in range(4):
        for tt in range(NT // 128):
            ti = t * 4 + tt
            tb = ti % 2
            w = [t_tr[ti - 2], ccw]
            if ti == 0:
                w += [P.at("sync"), P.at("vector"), P.at("tensor")]
            P.emit(DMA_G, lambda e, ti=ti, tb=tb: e.indirect_dma_start(
                out=xs2[tb][:], out_offset=None, in_=x_src[:, :],
                in_offset=IndirectOffsetOnAxis(idxl_sb[:, ti:ti + 1], 0)),
                waits=w, inc=16)
            gw = P.at(DMA_G)
            P.emit("tensor", lambda e, tb=tb: (
                [e.transpose(pss2[tb][:, ic * 128:(ic + 1) * 128],
                             xs2[tb][:, ic * 128:(ic + 1) * 128], id_b)
                 for ic in range(8)][-1]),
                waits=[gw, v_cp[ti - 2]], inc=1)
            t_tr[ti] = P.at("tensor")
            P.emit("vector", lambda e, t=t, tt=tt, tb=tb: (
                [e.tensor_copy(xf[t][ic][:, tt * 128:(tt + 1) * 128],
                               pss2[tb][:, ic * 128:(ic + 1) * 128])
                 for ic in range(8)][-1]),
                waits=[t_tr[ti]], inc=1)
            v_cp[ti] = P.at("vector")

    for vi, n in enumerate(vnames):
        P.emit(DMA_S, lambda e, n=n: e.dma_start(
            out=bT_raw[:], in_=vd[n].rearrange("(c p) -> c p", p=128)),
            waits=[P.at("tensor")], inc=16)
        P.barrier()
        P.emit("tensor", lambda e: e.transpose(psT[0:128, 0:8], bT_raw[:],
                                               id_f[0:8, 0:8]), inc=1)
        P.emit("vector", lambda e, vi=vi: e.tensor_copy(
            bT[:, vi * 8:(vi + 1) * 8], psT[0:128, 0:8]),
            waits=[P.at("tensor")], inc=1)
        P.barrier()
    P.emit(DMA_S, lambda e: e.dma_start(out=msc[0:1, 0:1024],
                                        in_=vd["bo_v"].unsqueeze(0)),
           waits=[P.at("tensor")], inc=16)
    P.barrier()
    for half in range(2):
        P.emit("tensor", lambda e, half=half: e.matmul(
            psT[0:128, 0:512], ones_f[0:1, 0:128],
            msc[0:1, half * 512:(half + 1) * 512], start=True, stop=True),
            waits=[P.at("vector")], inc=1)
        P.emit("vector", lambda e, half=half: e.tensor_copy(
            bc_bo[:, half * 512:(half + 1) * 512], psT[0:128, 0:512]),
            waits=[P.at("tensor")], inc=1)
    P.barrier()

    P.emit("tensor", lambda e: None, waits=[gw_w6])
    P.barrier()

    for ic in range(8):
        P.emit("vector", lambda e, ic=ic: e.memset(c_a[ic][:], 0.0), inc=1)
        P.emit("vector", lambda e, ic=ic: e.tensor_scalar_add(
            c_a[ic][:], c_a[ic][:], bT[:, 4 * 8 + ic:4 * 8 + ic + 1]),
            waits=[P.at("vector")], inc=1)
    P.barrier()

    s_bank = {0: P.at("scalar"), 1: P.at("scalar"), 2: P.at("scalar")}
    v_cup = P.at("vector")
    for t in range(4):
        c_src = c_a if t % 2 == 0 else c_b
        c_dst = c_b if t % 2 == 0 else c_a
        for ic in range(8):
            P.emit("vector", lambda e, ic=ic, c_src=c_src: e.tensor_copy(
                cb16[ic][:], c_src[ic][:]),
                waits=[P.at("tensor")] if ic == 0 else [], chain=True, inc=1)
        v_cb = P.at("vector")
        for oc in range(8):
            for gi, (wn, un, bvi, fn, dst) in enumerate((
                    (0, 1, 0, AF.Sigmoid, fga),
                    (2, 3, 1, AF.Sigmoid, fgb),
                    (4, 5, 2, AF.Tanh, fgc))):
                w = [s_bank[gi]] + ([v_cb] if oc == 0 and gi == 0 else [])
                P.emit("tensor", lambda e, t=t, oc=oc, gi=gi, wn=wn, un=un: (
                    [e.matmul(psA[:, gi * 512:(gi + 1) * 512],
                              wT[wn][ic][:, oc * 128:(oc + 1) * 128],
                              xf[t][ic][:], start=(ic == 0), stop=False)
                     for ic in range(8)] +
                    [e.matmul(psA[:, gi * 512:(gi + 1) * 512],
                              wT[un][ic][:, oc * 128:(oc + 1) * 128],
                              cb16[ic][:], start=False, stop=(ic == 7))
                     for ic in range(8)])[-1],
                    waits=w, inc=1)
                t_mm = P.at("tensor")
                P.emit("scalar", lambda e, gi=gi, oc=oc, bvi=bvi, fn=fn,
                       dst=dst: e.activation(
                           out=dst[:], in_=psA[:, gi * 512:(gi + 1) * 512],
                           func=fn, bias=bT[:, bvi * 8 + oc:bvi * 8 + oc + 1],
                           scale=1.0),
                       waits=[t_mm, v_cup], inc=1)
                s_bank[gi] = P.at("scalar")
            s3 = P.at("scalar")
            P.emit("vector", lambda e, oc=oc, c_src=c_src, c_dst=c_dst:
                   e.tensor_tensor(c_dst[oc][:], fga[:], c_src[oc][:],
                                   ALU.mult),
                   waits=[s3], inc=1)
            P.emit("vector", lambda e: e.tensor_tensor(fgb[:], fgb[:], fgc[:],
                                                       ALU.mult),
                   chain=True, inc=1)
            P.emit("vector", lambda e, oc=oc, c_dst=c_dst: e.tensor_tensor(
                c_dst[oc][:], c_dst[oc][:], fgb[:], ALU.add),
                chain=True, inc=1)
            v_cup = P.at("vector")
    c_fin = c_a if 4 % 2 == 0 else c_b

    load_wT(wT[0], "wo")
    load_wT(wT[1], "uo")
    P.barrier()

    for ic in range(8):
        P.emit("vector", lambda e, ic=ic: e.tensor_copy(cb16[ic][:],
                                                        c_fin[ic][:]),
               waits=[P.at("tensor")] if ic == 0 else [], inc=1)
    P.barrier()
    v_xa = {-1: P.at("vector"), -2: P.at("vector")}
    v_xo = {-1: P.at("vector"), -2: P.at("vector")}
    s_y = {-1: P.at("scalar")}
    st = {-1: P.at(DMA_S)}
    for t in range(4):
        for tt in range(NT // 128):
            ti = t * 4 + tt
            pa = (ti % 2) * 1024
            P.emit(DMA_G, lambda e, ti=ti: e.indirect_dma_start(
                out=xo[ti % 2][:], out_offset=None, in_=x_src[:, :],
                in_offset=IndirectOffsetOnAxis(idxl_sb[:, ti:ti + 1], 0)),
                waits=[v_xo[ti - 2], ccw] +
                      ([P.at("vector")] if ti < 2 else []),
                inc=16)
            dmw = P.at(DMA_G)
            P.emit("tensor", lambda e, t=t, tt=tt, pa=pa: (
                [e.matmul(psA[:, pa + half * 512:pa + (half + 1) * 512],
                          xf[t][ic][:, tt * 128:(tt + 1) * 128],
                          wT[0][ic][:, half * 512:(half + 1) * 512],
                          start=(ic == 0), stop=False)
                 for half in range(2) for ic in range(8)] +
                [e.matmul(psA[:, pa + half * 512:pa + (half + 1) * 512],
                          cb16[ic][:, tt * 128:(tt + 1) * 128],
                          wT[1][ic][:, half * 512:(half + 1) * 512],
                          start=False, stop=(ic == 7))
                 for half in range(2) for ic in range(8)])[-1],
                waits=[v_xa[ti - 2]] +
                      ([P.at("vector"), P.at("scalar")] if ti == 0 else []),
                inc=1)
            t_mm = P.at("tensor")
            P.emit("vector", lambda e, pa=pa: e.tensor_tensor(
                th[:], psA[:, pa:pa + 1024], bc_bo[:], ALU.add),
                waits=[t_mm, s_y[ti - 1], st[ti - 1]], inc=1)
            v_xa[ti] = P.at("vector")
            P.emit("scalar", lambda e: e.activation(out=th[:], in_=th[:],
                                                    func=AF.Tanh),
                   waits=[v_xa[ti]], inc=1)
            s_th = P.at("scalar")
            P.emit("vector", lambda e, ti=ti: e.tensor_tensor(
                xa[:], th[:], xo[ti % 2][:], ALU.add),
                waits=[s_th, dmw], inc=1)
            v_xo[ti] = P.at("vector")
            P.emit("vector", lambda e: e.bn_stats(ln_s[:, 8:14], xa[:, 0:512]),
                   chain=True, inc=1)
            P.emit("vector", lambda e: e.bn_stats(ln_s[:, 14:20],
                                                  xa[:, 512:1024]),
                   chain=True, inc=1)
            P.emit("vector", lambda e: e.bn_aggr(ln_s[:, 0:2], ln_s[:, 8:20]),
                   chain=True, inc=1)
            P.emit("vector", lambda e: e.tensor_scalar_add(ln_s[:, 2:3],
                                                           ln_s[:, 1:2], 1e-5),
                   chain=True, inc=1)
            v_ln = P.at("vector")
            P.emit("scalar", lambda e: e.activation(
                out=ln_s[:, 3:4], in_=ln_s[:, 2:3], func=AF.Sqrt),
                waits=[v_ln], inc=1)
            s_sq3 = P.at("scalar")
            P.emit("vector", lambda e: e.reciprocal(ln_s[:, 4:5], ln_s[:, 3:4]),
                   waits=[s_sq3], inc=1)
            P.emit("vector", lambda e: e.tensor_tensor(
                ln_s[:, 5:6], ln_s[:, 0:1], ln_s[:, 4:5], ALU.mult),
                chain=True, inc=1)
            P.emit("vector", lambda e: e.tensor_scalar_mul(
                ln_s[:, 5:6], ln_s[:, 5:6], -1.0), chain=True, inc=1)
            v_fin = P.at("vector")
            P.emit("scalar", lambda e: e.activation(
                out=y_sb[:], in_=xa[:], func=AF.Identity, bias=ln_s[:, 5:6],
                scale=ln_s[:, 4:5]),
                waits=[v_fin], inc=1)
            s_y[ti] = P.at("scalar")
            P.emit(DMA_S, lambda e, t=t, tt=tt: e.dma_start(
                out=y[t, tt * 128:(tt + 1) * 128, :], in_=y_sb[:]),
                waits=[s_y[ti]], inc=16)
            st[ti] = P.at(DMA_S)
    P.emit(DMA_S, lambda e: None, waits=[P.at(DMA_S)])




def host_stage1_inputs(x, w_attn, b_attn, w_proj, b_proj, l1=False):
    in_maps = []
    for core in range(8):
        b, half = core // 2, core % 2
        xb = np.ascontiguousarray(x[b])
        own = xb[half * 2048:(half + 1) * 2048]
        other = xb[(1 - half) * 2048:(2 - half) * 2048]
        x_kv = np.ascontiguousarray(np.concatenate([other, own], axis=0))
        kvpos = np.concatenate([np.arange(2048) + (1 - half) * 2048,
                                np.arange(2048) + half * 2048])
        kdead = np.zeros((4, 32, 128), dtype=np.float32)
        dmask = np.zeros((4, 8, 128, 512), dtype=BF)
        qv = np.arange(512)
        kv128 = np.arange(128)
        for j in range(4):
            qpos = half * 2048 + j * 512 + qv
            for kt in range(32):
                kp = kvpos[kt * 128:(kt + 1) * 128]
                kdead[j, kt] = np.where(kp > qpos.max(), NEG, 0.0)
            slots = list(range(4 * j, 4 * j + 4)) + \
                    list(range(16 + 4 * j, 16 + 4 * j + 4))
            for si, kt in enumerate(slots):
                kp = kvpos[kt * 128:(kt + 1) * 128]
                m = np.where(qpos[None, :] >= kp[:, None], 0.0, NEG)
                dmask[j, si] = m.astype(BF)
        in_maps.append({
            "x_own": own.copy(),
            "x_kv": x_kv,
            "w_attn": np.ascontiguousarray(w_attn),
            "b_attn": np.ascontiguousarray(b_attn),
            "w_proj": np.ascontiguousarray(w_proj),
            "b_proj": np.ascontiguousarray(b_proj),
            "kdead": kdead,
            "dmask": dmask,
        })
    return in_maps




def _rotary_tables(pos, n=4096):
    half_idx = np.arange(0, 64, 2, dtype=np.float64)
    inv_freq = 1.0 / 10000.0 ** (half_idx / 64.0)
    t = pos.astype(np.float64)
    freqs = t[:, None] * inv_freq[None, :]
    base = (half_idx + 0.4 * 64) / (1.4 * 64)
    power = (t - n // 2) / 256.0
    scale = base[None, :] ** power[:, None]
    cos = np.cos(freqs); sin = np.sin(freqs)
    def cat(v):
        v64 = np.concatenate([v, v], axis=1).T
        return np.concatenate([v64, v64], axis=0).astype(np.float32)
    aq = cat(cos * scale); bq = cat(sin * scale)
    ak = cat(cos / scale); bk = cat(sin / scale)
    return aq, bq, ak, bk


def host_stage_ffn_inputs(x_full, wqkv_l, qs_l, ks_l, wo_l, lng_l, lnb_l,
                          w1_l, b1_l, w2_l, b2_l, y1=None):
    HID = 2730
    HIDP = 2816
    w1a = np.zeros((1024, HIDP), dtype=np.float32)
    w1g = np.zeros((1024, HIDP), dtype=np.float32)
    w1a[:, :HID] = w1_l[:, :HID]; w1g[:, :HID] = w1_l[:, HID:]
    b1a = np.zeros(HIDP, dtype=np.float32); b1g = np.zeros(HIDP, dtype=np.float32)
    b1a[:HID] = b1_l[:HID]; b1g[:HID] = b1_l[HID:]
    w2p = np.zeros((HIDP, 1024), dtype=np.float32)
    w2p[:HID] = w2_l
    qs_blk = np.zeros((2, 128), dtype=np.float32)
    qs_blk[0, 0:64] = qs_l; qs_blk[1, 64:128] = qs_l
    ks_blk = np.zeros((2, 128), dtype=np.float32)
    ks_blk[0, 0:64] = ks_l; ks_blk[1, 64:128] = ks_l
    in_maps = []
    for core in range(8):
        b, half = core // 2, core % 2
        start = half * 2048
        xh = np.zeros((2816, 1024), dtype=np.float32)
        lo = start - 768
        src_lo = max(lo, 0)
        xh[src_lo - lo:] = x_full[b, src_lo:start + 2048]
        pos = np.clip(np.arange(lo, start + 2048), 0, 4095)
        aq, bq, ak, bk = _rotary_tables(pos)
        kd = np.zeros((21, 128), dtype=np.float32)
        gpos = np.arange(lo + 128, lo + 128 + 2688)
        kd_flat = np.where(gpos < 0, NEG, 0.0).astype(np.float32)
        kd[:] = kd_flat.reshape(21, 128)
        m = {
            "x_halo": xh, "wqkv": np.ascontiguousarray(wqkv_l),
            "qs_blk": qs_blk, "ks_blk": ks_blk,
            "wo": np.ascontiguousarray(wo_l),
            "ln_g": np.ascontiguousarray(lng_l),
            "ln_b": np.ascontiguousarray(lnb_l),
            "w1a": w1a, "w1g": w1g, "b1a": b1a, "b1g": b1g,
            "w2p": w2p, "b2": np.ascontiguousarray(b2_l),
            "kdead": kd,
            "aq": aq, "bq": bq, "ak": ak, "bk": bk,
        }
        if y1 is not None:
            m["y1_own"] = np.ascontiguousarray(y1[b, start:start + 2048])
        in_maps.append(m)
    return in_maps


def host_stage_lstm_inputs(x_full, inp):
    in_maps = []
    for core in range(8):
        sl = slice(core * 512, (core + 1) * 512)
        m = {"x_own": np.ascontiguousarray(x_full[:, sl, :])}
        for n, k in (("wf", "L1_ffn_wf"), ("uf", "L1_ffn_uf"),
                     ("wi", "L1_ffn_wi"), ("ui", "L1_ffn_ui"),
                     ("wc", "L1_ffn_wc"), ("uc", "L1_ffn_uc"),
                     ("wo", "L1_ffn_wo"), ("uo", "L1_ffn_uo")):
            m[n] = np.ascontiguousarray(inp[k])
        for n, k in (("bf_v", "L1_ffn_bf"), ("bi_v", "L1_ffn_bi"),
                     ("bc_v", "L1_ffn_bc"), ("bo_v", "L1_ffn_bo"),
                     ("in_c", "L1_ffn_in_c")):
            m[n] = np.ascontiguousarray(inp[k])
        in_maps.append(m)
    return in_maps





NFULL = 16384
ZROW = 16384

ABLK = [0, 3, 4, 7, 8, 11, 12, 15]
BBLK = [1, 2, 5, 6, 9, 10, 13, 14]
HALFOF = np.zeros(16, dtype=np.int64)
ORDIX = np.zeros(16, dtype=np.int64)
for _i, _b in enumerate(ABLK):
    HALFOF[_b] = 0
    ORDIX[_b] = _i
for _i, _b in enumerate(BBLK):
    HALFOF[_b] = 1
    ORDIX[_b] = _i


def piece_of_zig(pos):
    beta = pos // 256
    pr = ORDIX[beta] * 256 + pos % 256
    return int(pr // 512)


def piece_of_contig(pos):
    return int((pos % 2048) // 512)


def piece_reqs():
    def halo_ps(t, zig):
        ps = set()
        for half in (0, 1):
            for i in range(128):
                p = half * 2048 - 768 + t * 128 + i
                if 0 <= p < 4096:
                    ps.add(piece_of_zig(p) if zig else piece_of_contig(p))
        return sorted(ps)

    def kv_ps(t):
        ps = set()
        for half in (0, 1):
            own = ABLK if half == 0 else BBLK
            kvloc = []
            for g in range(4):
                grp = list(range(4 * g, 4 * g + 4))
                for x_ in ([x for x in grp if x not in own] +
                           [x for x in grp if x in own]):
                    kvloc.extend(range(x_ * 256, x_ * 256 + 256))
            for p in kvloc[t * 128:(t + 1) * 128]:
                ps.add(piece_of_contig(p))
        return sorted(ps)

    def attnown_ps(t):
        ps = set()
        for half in (0, 1):
            own = ABLK if half == 0 else BBLK
            for i in range(128):
                ps.add(piece_of_contig(own[t // 2] * 256 + (t % 2) * 128 + i))
        return sorted(ps)

    def ffnown_ps(t):
        ps = set()
        for half in (0, 1):
            for i in range(128):
                ps.add(piece_of_zig(half * 2048 + t * 128 + i))
        return sorted(ps)

    def lstm_ps(ti):
        tt = ti % 4
        ps = set()
        for core in range(8):
            for i in range(128):
                ps.add(piece_of_zig(core * 512 + tt * 128 + i))
        return sorted(ps)

    return {
        "halo0": [halo_ps(t, True) for t in range(22)],
        "halo1": [halo_ps(t, False) for t in range(22)],
        "kv_p": [kv_ps(t) for t in range(32)],
        "own_p": [attnown_ps(t) for t in range(16)],
        "own_f": [ffnown_ps(t) for t in range(16)],
        "lstm": [lstm_ps(ti) for ti in range(16)],
    }


def build_fused():
    nc = bass.Bass("TRN2", target_bir_lowering=False, num_devices=8)

    x0 = nc.dram_tensor("x0", [NFULL, 1024], F32, kind="ExternalInput").ap()
    ein = {}

    def EIN(name, shape, dt=F32):
        ein[name] = nc.dram_tensor(name, shape, dt, kind="ExternalInput").ap()
        return ein[name]

    EIN("idx_kv", [128, 32], I32)
    EIN("idx_own", [128, 16], I32)
    EIN("idx_kv_p", [128, 32], I32)
    EIN("idx_own_p", [128, 16], I32)
    EIN("idx_halo0", [128, 22], I32)
    EIN("idx_halo1", [128, 22], I32)
    EIN("idx_own_f", [128, 16], I32)
    EIN("idx_lstm", [128, 16], I32)
    EIN("s1_w_attn", [1024, 3072])
    EIN("s1_b_attn", [3072])
    EIN("s1_w_proj", [1024, 1024])
    EIN("s1_b_proj", [1024])
    EIN("kdead_attn", [4, 32, 128])
    EIN("dmask_attn", [4, 8, 128, 512], BF16)
    for pfx in ("f0", "f1"):
        EIN(pfx + "_wqkv", [1024, 1536])
        EIN(pfx + "_qs_blk", [2, 128])
        EIN(pfx + "_ks_blk", [2, 128])
        EIN(pfx + "_wo", [512, 1024])
        EIN(pfx + "_ln_g", [1024])
        EIN(pfx + "_ln_b", [1024])
        EIN(pfx + "_w1a", [128, 22528])
        EIN(pfx + "_w1g", [128, 22528])
        EIN(pfx + "_b1a", [2816])
        EIN(pfx + "_b1g", [2816])
        EIN(pfx + "_w2p", [2816, 1024])
        EIN(pfx + "_b2", [1024])
    EIN("f_kdead", [21, 128])
    for tn in ("aq", "bq", "ak", "bk"):
        EIN("f_" + tn, [128, 2816])
    EIN("s4_w_attn", [1024, 3072])
    EIN("s4_b_attn", [3072])
    EIN("s4_w_proj", [1024, 1024])
    EIN("s4_b_proj", [1024])
    for n in ("wf", "uf", "wi", "ui", "wc", "uc", "wo", "uo"):
        EIN("s5_" + n, [1024, 1024])
    for n in ("bf_v", "bi_v", "bc_v", "bo_v", "in_c"):
        EIN("s5_" + n, [1024])
    y_out = nc.dram_tensor("y", [4, 512, 1024], F32, kind="ExternalOutput").ap()

    parts = [nc.dram_tensor(f"part{i}", [2048, 1024], F32, kind="Internal")
             for i in range(4)]
    fulls = [nc.dram_tensor(f"full{i}", [NFULL + 1, 1024], F32, kind="Internal")
             for i in range(4)]
    k_s1 = nc.dram_tensor("k_s1", [8, 128, 4096], BF16, kind="Internal")
    k_s4 = nc.dram_tensor("k_s4", [8, 128, 4096], BF16, kind="Internal")
    v_s4 = nc.dram_tensor("v_s4", [32, 128, 1024], BF16, kind="Internal")
    xnew_f = [nc.dram_tensor(f"xnew_f{i}", [17, 128, 1024], F32,
                             kind="Internal") for i in range(2)]
    w1abf = [nc.dram_tensor(f"w1abf{i}", [128, 22528], BF16, kind="Internal")
             for i in range(2)]
    w1gbf = [nc.dram_tensor(f"w1gbf{i}", [128, 22528], BF16, kind="Internal")
             for i in range(2)]
    w2bf = [nc.dram_tensor(f"w2bf{i}", [2816, 1024], BF16, kind="Internal")
            for i in range(2)]

    psA = nc.alloc_psum_tensor("psA", [128, 2048], F32)
    psS = [nc.alloc_psum_tensor(f"psS{i}", [128, 512], F32) for i in range(2)]
    psV = nc.alloc_psum_tensor("psV", [128, 512], F32)
    psT = nc.alloc_psum_tensor("psT", [128, 512], F32)
    PS = {"psA": psA, "psS": psS, "psV": psV, "psT": psT}

    AL = Arena(nc)
    consts = {}
    P0 = Prog(nc)
    for name, arr in all_np_consts().items():
        dt = BF16 if arr.dtype == BF else F32
        t = nc.alloc_sbuf_tensor("c_" + name, list(arr.shape), dt)
        dr = nc.inline_tensor(np.ascontiguousarray(arr), "ic_" + name)
        P0.emit("sync", lambda e, t=t, dr=dr: e.dma_start(out=t[:],
                                                          in_=dr.ap()),
                inc=16)
        consts[name] = t
    zrow = AL("zrow", [1, 1024], F32)
    AL.reset()
    P0.emit("vector", lambda e: e.memset(zrow[:], 0.0), inc=1)
    P0.barrier()
    for fl in fulls:
        P0.emit("sync", lambda e, fl=fl: e.dma_start(
            out=fl.ap()[ZROW:ZROW + 1, :], in_=zrow[:]), inc=16)
    P0.barrier()
    P0.build()

    _ccn = [0]

    def mk_ccp(part_t, full_t):
        _ccn[0] += 1
        sems = [nc.alloc_semaphore(f"ccp{_ccn[0]}_{g}") for g in range(4)]

        def emit_piece(P, g, wait_tok, part_t=part_t, full_t=full_t,
                       sems=sems):
            def f(e, g=g):
                e.collective_compute(
                    "AllGather", ALU.bypass, replica_groups=[list(range(8))],
                    ins=[part_t.ap()[g * 512:(g + 1) * 512, :].opt()],
                    outs=[full_t.ap()[g * 4096:(g + 1) * 4096, :].opt()]
                ).then_inc(sems[g], 1)
            P.emit("gpsimd", f, waits=[wait_tok], inc=0)
        return emit_piece, [(s, 1) for s in sems]

    AL.reset()
    P1 = Prog(nc)
    ccp = [None] * 4
    cc_tok = [None] * 4
    for i in range(4):
        ccp[i], cc_tok[i] = mk_ccp(parts[i], fulls[i])
    PR = piece_reqs()

    def ptoks(tok4, reqlist):
        return [[tok4[p] for p in ps] for ps in reqlist]
    D1 = {"x_src": x0, "idx_kv": ein["idx_kv"], "idx_own": ein["idx_own"],
          "w_attn": ein["s1_w_attn"], "b_attn": ein["s1_b_attn"],
          "w_proj": ein["s1_w_proj"], "b_proj": ein["s1_b_proj"],
          "kdead": ein["kdead_attn"], "dmask": ein["dmask_attn"],
          "ydst": parts[0].ap(), "k_dram": k_s1, "ccp": ccp[0]}
    D1.update(PS)
    emit_stage_attn(nc, P1, AL, consts, D1, l1=False)
    P1.barrier()
    P1.build()

    for li in range(2):
        AL.reset()
        Pf = Prog(nc)
        pfx = f"f{li}"
        Df = {"x_src": fulls[0].ap() if li == 0 else fulls[1].ap(),
              "idx_halo": ein[f"idx_halo{li}"], "idx_own": ein["idx_own_f"],
              "wqkv": ein[pfx + "_wqkv"], "qs_blk": ein[pfx + "_qs_blk"],
              "ks_blk": ein[pfx + "_ks_blk"], "wo": ein[pfx + "_wo"],
              "ln_g": ein[pfx + "_ln_g"], "ln_b": ein[pfx + "_ln_b"],
              "w1a": ein[pfx + "_w1a"], "w1g": ein[pfx + "_w1g"],
              "b1a": ein[pfx + "_b1a"], "b1g": ein[pfx + "_b1g"],
              "w2p": ein[pfx + "_w2p"], "b2": ein[pfx + "_b2"],
              "kdead": ein["f_kdead"],
              "aq": ein["f_aq"], "bq": ein["f_bq"], "ak": ein["f_ak"],
              "bk": ein["f_bk"],
              "y1_src": fulls[0].ap(),
              "ydst": parts[1 + li].ap(), "xnew_scr": xnew_f[li],
              "ccw": cc_tok[li], "ccw_y1": cc_tok[0], "ccp": ccp[1 + li],
              "ccw_halo": ptoks(cc_tok[li], PR["halo0" if li == 0 else "halo1"]),
              "ccw_own1": ptoks(cc_tok[0], PR["own_f"]),
              "w1abf": w1abf[li], "w1gbf": w1gbf[li], "w2bf": w2bf[li]}
        Df.update(PS)
        emit_stage_ffn(nc, Pf, AL, consts, Df, last=(li == 1))
        Pf.barrier()
        Pf.build()

    AL.reset()
    P4 = Prog(nc)
    D4 = {"x_src": fulls[2].ap(), "idx_kv": ein["idx_kv_p"],
          "idx_own": ein["idx_own_p"],
          "w_attn": ein["s4_w_attn"], "b_attn": ein["s4_b_attn"],
          "w_proj": ein["s4_w_proj"], "b_proj": ein["s4_b_proj"],
          "kdead": ein["kdead_attn"], "dmask": ein["dmask_attn"],
          "ydst": parts[3].ap(), "k_dram": k_s4, "v_dram": v_s4,
          "ccw": cc_tok[2], "ccp": ccp[3],
          "ccw_kv": ptoks(cc_tok[2], PR["kv_p"]),
          "ccw_own": ptoks(cc_tok[2], PR["own_p"])}
    D4.update(PS)
    emit_stage_attn(nc, P4, AL, consts, D4, l1=True)
    P4.barrier()
    P4.build()

    AL.reset()
    P5 = Prog(nc)
    D5 = {"x_src": fulls[3].ap(), "idx_lstm": ein["idx_lstm"], "ydst": y_out,
          "ccw": cc_tok[3]}
    for n in ("wf", "uf", "wi", "ui", "wc", "uc", "wo", "uo",
              "bf_v", "bi_v", "bc_v", "bo_v", "in_c"):
        D5[n] = ein["s5_" + n]
    D5.update(PS)
    emit_stage_lstm(nc, P5, AL, consts, D5)
    P5.build()
    return nc


def host_fused_inputs(inputs):
    x0 = np.asarray(inputs["L0_attn_hidden_states"],
                    dtype=np.float32).reshape(NFULL, 1024)
    HID = 2730
    in_maps = []
    shared = {
        "s1_w_attn": np.ascontiguousarray(
            np.asarray(inputs["L0_attn_c_attn_kernel"]).T),
        "s1_b_attn": np.ascontiguousarray(inputs["L0_attn_c_attn_bias"]),
        "s1_w_proj": np.ascontiguousarray(
            np.asarray(inputs["L0_attn_c_proj_kernel"]).T),
        "s1_b_proj": np.ascontiguousarray(inputs["L0_attn_c_proj_bias"]),
        "s4_w_attn": np.ascontiguousarray(inputs["L1_attn_w_attn"]),
        "s4_b_attn": np.ascontiguousarray(inputs["L1_attn_b_attn"]),
        "s4_w_proj": np.ascontiguousarray(inputs["L1_attn_w_proj"]),
        "s4_b_proj": np.ascontiguousarray(inputs["L1_attn_b_proj"]),
    }
    for li in range(2):
        pfx = f"f{li}"
        w1_l = inputs["L0_ffn_w1"][li]
        b1_l = inputs["L0_ffn_b1"][li]
        w1a = np.zeros((1024, 2816), dtype=np.float32)
        w1g = np.zeros((1024, 2816), dtype=np.float32)
        w1a[:, :HID] = w1_l[:, :HID]
        w1g[:, :HID] = w1_l[:, HID:]
        b1a = np.zeros(2816, dtype=np.float32)
        b1g = np.zeros(2816, dtype=np.float32)
        b1a[:HID] = b1_l[:HID]
        b1g[:HID] = b1_l[HID:]
        w2p = np.zeros((2816, 1024), dtype=np.float32)
        w2p[:HID] = inputs["L0_ffn_w2"][li]
        qs_blk = np.zeros((2, 128), dtype=np.float32)
        qs_blk[0, 0:64] = inputs["L0_ffn_q_scale"][li]
        qs_blk[1, 64:128] = inputs["L0_ffn_q_scale"][li]
        ks_blk = np.zeros((2, 128), dtype=np.float32)
        ks_blk[0, 0:64] = inputs["L0_ffn_k_scale"][li]
        ks_blk[1, 64:128] = inputs["L0_ffn_k_scale"][li]
        def _w1rel(w):
            return np.ascontiguousarray(
                w.reshape(8, 128, 22, 128).transpose(1, 2, 0, 3)
                 .reshape(128, 22528))
        shared.update({
            pfx + "_wqkv": np.ascontiguousarray(inputs["L0_ffn_wqkv"][li]),
            pfx + "_qs_blk": qs_blk, pfx + "_ks_blk": ks_blk,
            pfx + "_wo": np.ascontiguousarray(inputs["L0_ffn_wo"][li]),
            pfx + "_ln_g": np.ascontiguousarray(inputs["L0_ffn_ln_g"][li]),
            pfx + "_ln_b": np.ascontiguousarray(inputs["L0_ffn_ln_b"][li]),
            pfx + "_w1a": _w1rel(w1a), pfx + "_w1g": _w1rel(w1g),
            pfx + "_b1a": b1a, pfx + "_b1g": b1g,
            pfx + "_w2p": w2p,
            pfx + "_b2": np.ascontiguousarray(inputs["L0_ffn_b2"][li]),
        })
    for n in ("wf", "uf", "wi", "ui", "wc", "uc", "wo", "uo"):
        shared["s5_" + n] = np.ascontiguousarray(inputs["L1_ffn_" + n[:1] +
                                                        n[1:]])
    shared["s5_wf"] = np.ascontiguousarray(np.asarray(inputs["L1_ffn_wf"]).T)
    shared["s5_uf"] = np.ascontiguousarray(np.asarray(inputs["L1_ffn_uf"]).T)
    shared["s5_wi"] = np.ascontiguousarray(np.asarray(inputs["L1_ffn_wi"]).T)
    shared["s5_ui"] = np.ascontiguousarray(np.asarray(inputs["L1_ffn_ui"]).T)
    shared["s5_wc"] = np.ascontiguousarray(np.asarray(inputs["L1_ffn_wc"]).T)
    shared["s5_uc"] = np.ascontiguousarray(np.asarray(inputs["L1_ffn_uc"]).T)
    shared["s5_wo"] = np.ascontiguousarray(np.asarray(inputs["L1_ffn_wo"]).T)
    shared["s5_uo"] = np.ascontiguousarray(np.asarray(inputs["L1_ffn_uo"]).T)
    shared["s5_bf_v"] = np.ascontiguousarray(inputs["L1_ffn_bf"])
    shared["s5_bi_v"] = np.ascontiguousarray(inputs["L1_ffn_bi"])
    shared["s5_bc_v"] = np.ascontiguousarray(inputs["L1_ffn_bc"])
    shared["s5_bo_v"] = np.ascontiguousarray(inputs["L1_ffn_bo"])
    shared["s5_in_c"] = np.ascontiguousarray(inputs["L1_ffn_in_c"])

    ii = np.arange(128)
    halfof, ordix = HALFOF, ORDIX

    def _pmap_z(ix):
        ix = np.asarray(ix)
        bb = ix // 4096
        pos = ix % 4096
        beta = np.clip(pos // 256, 0, 15)
        corez = bb * 2 + halfof[beta]
        pr = ordix[beta] * 256 + pos % 256
        pm = (pr // 512) * 4096 + corez * 512 + pr % 512
        return np.where(ix >= NFULL, ix, pm).astype(np.int32)

    for core in range(8):
        b, half = core // 2, core % 2
        start = half * 2048
        base = b * 4096
        own_blocks = ABLK if half == 0 else BBLK
        kvloc_l = []
        for g in range(4):
            grp = list(range(4 * g, 4 * g + 4))
            oth = [x_ for x_ in grp if x_ not in own_blocks]
            owb = [x_ for x_ in grp if x_ in own_blocks]
            for x_ in oth + owb:
                kvloc_l.append(np.arange(256) + x_ * 256)
        kvloc = np.concatenate(kvloc_l)
        kvpos = kvloc + base
        idx_kv = np.zeros((128, 32), dtype=np.int32)
        for kt in range(32):
            idx_kv[:, kt] = kvpos[kt * 128:(kt + 1) * 128]
        idx_own = np.zeros((128, 16), dtype=np.int32)
        for t in range(16):
            idx_own[:, t] = (base + own_blocks[t // 2] * 256 +
                             (t % 2) * 128 + ii)
        idx_halo = np.zeros((128, 22), dtype=np.int32)
        for t in range(22):
            loc = start - 768 + t * 128 + ii
            idx_halo[:, t] = np.where(loc < 0, ZROW, base + loc)
        idx_lstm = np.zeros((128, 16), dtype=np.int32)
        for t in range(4):
            for tt in range(4):
                idx_lstm[:, t * 4 + tt] = (t * 4096 + core * 512 +
                                           tt * 128 + ii)

        def _pmap(ix):
            ix = np.asarray(ix)
            pm = ((ix % 2048) // 512) * 4096 + (ix // 2048) * 512 + ix % 512
            return np.where(ix >= NFULL, ix, pm).astype(np.int32)
        kdead = np.zeros((4, 32, 128), dtype=np.float32)
        dmask = np.zeros((4, 8, 128, 512), dtype=BF)
        for j in range(4):
            qpos = np.concatenate([own_blocks[2 * j] * 256 + np.arange(256),
                                   own_blocks[2 * j + 1] * 256 + np.arange(256)])
            for kt in range(32):
                kp = kvloc[kt * 128:(kt + 1) * 128]
                kdead[j, kt] = np.where(kp > qpos.max(), NEG, 0.0)
            for si in range(8):
                kt = 8 * j + si
                kp = kvloc[kt * 128:(kt + 1) * 128]
                dmask[j, si] = np.where(qpos[None, :] >= kp[:, None],
                                        0.0, NEG).astype(BF)
        lo = start - 768
        gpos = np.arange(lo + 128, lo + 128 + 2688)
        f_kdead = np.where(gpos < 0, NEG, 0.0).astype(np.float32) \
            .reshape(21, 128)
        pos = np.clip(np.arange(lo, start + 2048), 0, 4095)
        aq, bq, ak, bk = _rotary_tables(pos)
        m = dict(shared)
        m.update({
            "x0": x0, "idx_kv": idx_kv, "idx_own": idx_own,
            "idx_kv_p": _pmap(idx_kv), "idx_own_p": _pmap(idx_own),
            "idx_halo0": _pmap_z(idx_halo), "idx_halo1": _pmap(idx_halo),
            "idx_own_f": _pmap_z(base + start + np.arange(16)[None, :] * 128 +
                                 ii[:, None]),
            "idx_lstm": _pmap_z(idx_lstm),
            "kdead_attn": kdead, "dmask_attn": dmask,
            "f_kdead": f_kdead,
            "f_aq": aq, "f_bq": bq, "f_ak": ak, "f_bk": bk,
        })
        in_maps.append(m)
    return in_maps


_NC_CACHE = {}


def kernel(**inputs):
    if "fused" not in _NC_CACHE:
        _NC_CACHE["fused"] = build_fused()
    nc = _NC_CACHE["fused"]
    in_maps = host_fused_inputs(inputs)
    res = run_bass_kernel_spmd(nc, in_maps, core_ids=list(range(8)))
    out = np.zeros((4, 4096, 1024), dtype=np.float32)
    for core in range(8):
        out[:, core * 512:(core + 1) * 512, :] = res.results[core]["y"]
    return out


# version 28
# speedup vs baseline: 1.0133x; 1.0133x over previous
import numpy as np
import ml_dtypes
import concourse.bass as bass
import concourse.mybir as mybir
from concourse.bass_utils import run_bass_kernel_spmd

F32 = mybir.dt.float32
BF16 = mybir.dt.bfloat16
AF = mybir.ActivationFunctionType
ALU = mybir.AluOpType
AX = mybir.AxisListType

NEG = -1.0e9
BF = ml_dtypes.bfloat16
EPS_L = 1e-30


class Prog:
    ENGINES = ("tensor", "vector", "scalar", "gpsimd", "sync")
    DMAQ = ("gpsimd", "sync")
    NLANE = 4
    _SEQ = [0]

    def __init__(self, nc):
        self.nc = nc
        self._SEQ[0] += 1
        sid = self._SEQ[0]
        self.ops = {e: [] for e in self.ENGINES}
        self.sems = {}
        self.cnt = {}
        self.lane = {}
        for e in self.ENGINES:
            if e in self.DMAQ:
                self.sems[e] = [nc.alloc_semaphore(f"sem{sid}_{e}{i}")
                                for i in range(self.NLANE)]
                self.cnt[e] = [0] * self.NLANE
                self.lane[e] = 0
            else:
                self.sems[e] = nc.alloc_semaphore(f"sem{sid}_" + e)
                self.cnt[e] = 0

    def at(self, eng):
        if eng in self.DMAQ:
            return [(self.sems[eng][i], self.cnt[eng][i])
                    for i in range(self.NLANE) if self.cnt[eng][i] > 0]
        return [(self.sems[eng], self.cnt[eng])] if self.cnt[eng] > 0 else []

    @staticmethod
    def _flat(waits):
        out = []
        for tok in waits:
            for p in tok:
                if p not in out:
                    out.append(p)
        return out

    def emit(self, eng, fn, waits=(), inc=0, chain=False):
        w = self._flat(waits)
        if eng in self.DMAQ and inc:
            ln = self.lane[eng]
            self.lane[eng] = (ln + 1) % self.NLANE
            sem = self.sems[eng][ln]
            if self.cnt[eng][ln] > 0:
                w.append((sem, self.cnt[eng][ln]))
            self.cnt[eng][ln] += inc
        elif inc:
            sem = self.sems[eng]
            if chain and self.cnt[eng] > 0:
                w.append((sem, self.cnt[eng]))
            self.cnt[eng] += inc
        else:
            sem = None

        def g(e, w=w, fn=fn, inc=inc, sem=sem):
            for s_, val in w:
                e.wait_ge(s_, val)
            ins = fn(e)
            if inc:
                ins.then_inc(sem, inc)

        self.ops[eng].append(g)

    def barrier(self):
        toks = [self.at(e) for e in self.ENGINES]
        for e in self.ENGINES:
            own = self.at(e)
            w = [t for t in toks if t is not own]
            self.emit(e, lambda _e: None,
                      waits=[p for t in w for p in [t]])

    def build(self):
        with self.nc.Block() as block:
            for eng in self.ENGINES:
                ops = self.ops[eng]
                if not ops:
                    continue

                def make(ops):
                    def _(e):
                        for f in ops:
                            f(e)
                    return _
                getattr(block, eng)(make(ops))


def np_consts():
    return {
        "id_bf": np.eye(128, dtype=BF),
        "id_f32": np.eye(128, dtype=np.float32),
        "ones_f32": np.ones((128, 128), dtype=np.float32),
        "ones_col_bf": np.ones((128, 1), dtype=BF),
    }


def load_consts(nc, P):
    out = {}
    for name, arr in np_consts().items():
        dt = BF16 if arr.dtype == BF else F32
        t = nc.alloc_sbuf_tensor("c_" + name, list(arr.shape), dt)
        dr = nc.inline_tensor(np.ascontiguousarray(arr), "ic_" + name)
        P.emit("sync", lambda e, t=t, dr=dr: e.dma_start(out=t[:], in_=dr.ap()),
               inc=16)
        out[name] = t
    return out


from concourse.bass_core_rust import IndirectOffsetOnAxis

I32 = mybir.dt.int32


class Arena:
    SZ = {F32: 4, BF16: 2, I32: 4}

    def __init__(self, nc, kib=203):
        self.base = nc.alloc_sbuf_tensor("ARENA", [128, kib * 1024],
                                         mybir.dt.uint8)
        self.cap = kib * 1024
        self.off = 0
        self.hwm = 0

    def reset(self):
        self.off = 0

    def __call__(self, name, shape, dt):
        Pn, N = shape
        nb = N * self.SZ[dt]
        v = self.base[0:Pn, self.off:self.off + nb].bitcast(dt)
        self.off += (nb + 31) // 32 * 32
        self.hwm = max(self.hwm, self.off)
        assert self.off <= self.cap, (name, self.off)
        return v

    def at(self, off, shape, dt):
        Pn, N = shape
        nb = N * self.SZ[dt]
        assert off + nb <= self.cap
        return self.base[0:Pn, off:off + nb].bitcast(dt)


def ffn_np_consts():
    c = {}
    c["id_bf"] = np.eye(128, dtype=BF)
    c["id_f32"] = np.eye(128, dtype=np.float32)
    c["ones_f32"] = np.ones((128, 128), dtype=np.float32)
    hs = np.zeros((128, 2), dtype=np.float32)
    hs[0:64, 0] = 1.0
    hs[64:128, 1] = 1.0
    c["headsum"] = hs
    R = np.zeros((128, 128), dtype=np.float32)
    for d in range(128):
        if d % 64 < 32:
            R[d + 32, d] = -1.0
        else:
            R[d - 32, d] = 1.0
    c["rotT"] = R
    k = np.arange(128)[:, None]
    q = np.arange(128)[None, :]
    c["tri_lo"] = np.where(k <= q, 0.0, NEG).astype(np.float32)
    c["tri_hi"] = np.where(k >= q, 0.0, NEG).astype(np.float32)
    up = np.zeros((128, 128), dtype=np.float32)
    for r in range(1, 128):
        up[r - 1, r] = 1.0
    c["shift_up"] = up
    l2f = np.zeros((128, 128), dtype=np.float32)
    l2f[127, 0] = 1.0
    c["last2first"] = l2f
    c["eps_col"] = np.full((128, 1), EPS_L, dtype=np.float32)
    return c




def all_np_consts():
    c = np_consts()
    c.update(ffn_np_consts())
    return c


def emit_stage_attn(nc, P, AL, consts, DS, l1=False):
    NQ, NK, D = 2048, 4096, 1024
    NH, HD = (1, 1024) if l1 else (16, 64)
    scale = (1.0 / 32.0) if l1 else (1.0 / 8.0)
    DMA_S2 = "sync"
    x_src = DS["x_src"]
    w_attn = DS["w_attn"]
    b_attn = DS["b_attn"]
    w_proj = DS["w_proj"]
    b_proj = DS["b_proj"]
    kdead = DS["kdead"]
    dmask = DS["dmask"]
    y = DS["ydst"]
    k_dram = DS["k_dram"]
    if l1:
        v_dram = DS["v_dram"]
    id_b = consts["id_bf"][:]
    id_f = consts["id_f32"]
    ones_f = consts["ones_f32"]
    ones_col_bf = consts["ones_col_bf"]
    ccw = DS.get("ccw") or []
    ccw_kv = DS.get("ccw_kv")
    ccw_own = DS.get("ccw_own")
    ccp = DS.get("ccp")

    xkv = [AL(f"xkv{i}", [128, NK], BF16) for i in range(8)]
    wt_off = AL.off
    wt = [AL(f"wt{i}", [128, 1024], BF16) for i in range(8)]
    xrow = AL("xrow", [128, 1024], BF16)
    wrow = xrow
    xrowf = AL("xrowf", [128, 1024], F32)
    idxkv_sb = AL("idxkv_sb", [128, 32], I32)
    idxown_sb = AL("idxown_sb", [128, 16], I32)
    P.emit(DMA_S2, lambda e: e.dma_start(out=idxkv_sb, in_=DS["idx_kv"]), inc=16)
    P.emit(DMA_S2, lambda e: e.dma_start(out=idxown_sb, in_=DS["idx_own"]),
           inc=16)
    VW = NH * (HD + 1) if not l1 else 0
    if not l1:
        v_sb = [AL(f"v{i}", [128, VW], BF16)
                for i in range(NK // 128)]
        k_h = AL("k_h", [128, NK], BF16)
        k_hb = [k_h, AL.at(wt_off, [128, NK], BF16)]
        pt_sb = [AL(f"pt{i}", [128, 512], BF16)
                 for i in range(2)]
    else:
        v_sb = None
        kh_off = AL.off
        k_h = AL("k_h", [128, NK], BF16)
        pt_sb = [AL(f"pt{i}", [128, 512], BF16)
                 for i in range(32)]
        vbuf2 = [AL(f"vb{i}", [128, D], BF16) for i in range(2)]
        vbuf = [vbuf2[0][:, 0:512], vbuf2[0][:, 512:1024],
                vbuf2[1][:, 0:512], vbuf2[1][:, 512:1024]]
        kbuf = ([AL.at(kh_off, [128, 4096], BF16),
                 AL.at(wt_off, [128, 4096], BF16),
                 AL.at(wt_off + 8192, [128, 4096], BF16)] +
                [AL(f"kb{i}", [128, 4096], BF16) for i in range(5)])
    dm_sb = [AL(f"dm{i}", [128, 512], BF16) for i in range(8)]
    kdT = [AL(f"kdT{i}", [128, 32], F32) for i in range(4)]
    kdraw = AL("kdraw", [32, 128], F32)
    biasall = AL("biasall", [128, 32], F32)
    rm = AL("rm", [128, 16], F32)
    msc = AL("msc", [128, 1024], F32)
    bv_row = msc[0:1, 0:1024]
    bp_row = msc[0:1, 0:1024]
    negM = AL("negM", [128, 1], F32)
    batraw = AL("batraw", [24, 128], F32)
    batT = AL("batT", [128, 24], F32)
    bcast_bv = AL("bcast_bv", [128, D], F32)
    bcast_bp = AL("bcast_bp", [128, D], F32)
    rb_sb = AL("rb_sb", [128, 512], F32)
    xo = [xrowf, AL("xo1", [128, D], F32)]
    xa = AL("xa", [128, D], F32)
    ln_s = AL("ln_s", [128, 24], F32)
    y_sb = bcast_bv
    sq = y_sb

    psA = DS["psA"]
    psS = DS["psS"]
    psV = DS["psV"]
    psT = DS["psT"]
    psTb = psT.bitcast(BF16)

    DMA_G, DMA_S = "gpsimd", "sync"

    def fm_load(dst_list, idx_sb, ntiles, col_off=0):
        xr2 = [xrow, msc.bitcast(BF16)[:, 0:1024]]
        ps2 = [psTb[0:128, 0:1024], psS[0].bitcast(BF16)[:, 0:1024]]
        t_tr = {-1: P.at("tensor"), -2: P.at("tensor")}
        v_cp = {-1: P.at("vector"), -2: P.at("vector")}
        for t in range(ntiles):
            tb = t % 2
            w = [t_tr[t - 2], ccw_kv[t] if ccw_kv else ccw]
            if t == 0:
                w += [P.at(DMA_S2), P.at("vector")]
            P.emit(DMA_G, lambda e, t=t, tb=tb, idx_sb=idx_sb:
                   e.indirect_dma_start(
                       out=xr2[tb][:], out_offset=None, in_=x_src[:, :],
                       in_offset=IndirectOffsetOnAxis(idx_sb[:, t:t + 1], 0)),
                   waits=w, inc=16)
            gw = P.at(DMA_G)
            P.emit("tensor", lambda e, tb=tb: (
                [e.transpose(ps2[tb][:, ic * 128:(ic + 1) * 128],
                             xr2[tb][:, ic * 128:(ic + 1) * 128], id_b)
                 for ic in range(8)][-1]),
                waits=[gw, v_cp[t - 2]], inc=1)
            t_tr[t] = P.at("tensor")
            P.emit("vector", lambda e, t=t, tb=tb: (
                [e.tensor_copy(
                    dst_list[ic][:, col_off + t * 128:col_off + (t + 1) * 128],
                    ps2[tb][:, ic * 128:(ic + 1) * 128]) for ic in range(8)][-1]),
                waits=[t_tr[t]], inc=1)
            v_cp[t] = P.at("vector")

    fm_load(xkv, idxkv_sb, NK // 128)

    P.emit(DMA_S, lambda e: e.dma_start(
        out=batraw[:], in_=b_attn.rearrange("(c p) -> c p", p=128)),
        waits=[P.at("tensor")], inc=16)
    P.emit(DMA_S, lambda e: e.dma_start(
        out=bv_row, in_=b_attn[2048:3072].unsqueeze(0)),
        waits=[P.at("tensor")], inc=16)
    P.barrier()
    P.emit("tensor", lambda e: e.transpose(psT[0:128, 0:24], batraw[:], id_f[0:24, 0:24]),
           inc=1)
    P.emit("vector", lambda e: e.tensor_copy(batT[:, 0:24], psT[0:128, 0:24]),
           waits=[P.at("tensor")], inc=1)
    for j in range(4):
        P.emit(DMA_S, lambda e, j=j: e.dma_start(out=kdraw[:], in_=kdead[j]),
               waits=[P.at("tensor")], inc=16)
        P.emit("tensor", lambda e: e.transpose(psT[0:128, 0:32], kdraw[:],
                                               id_f[0:32, 0:32]),
               waits=[P.at(DMA_S), P.at("vector")], inc=1)
        P.emit("vector", lambda e, j=j: e.tensor_copy(kdT[j][:], psT[0:128, 0:32]),
               waits=[P.at("tensor")], inc=1)
    def bcast_row(dst):
        for half in range(2):
            P.emit("tensor", lambda e, half=half: e.matmul(
                psT[0:128, 0:512], ones_f[0:1, 0:128],
                bv_row[0:1, half * 512:(half + 1) * 512], start=True, stop=True),
                waits=[P.at("vector"), P.at(DMA_S)], inc=1)
            P.emit("vector", lambda e, dst=dst, half=half: e.tensor_copy(
                dst[:, half * 512:(half + 1) * 512], psT[0:128, 0:512]),
                waits=[P.at("tensor")], inc=1)
    bcast_row(bcast_bv)
    P.barrier()
    P.emit(DMA_S, lambda e: e.dma_start(
        out=bp_row, in_=b_proj.unsqueeze(0)), inc=16)
    P.barrier()
    bcast_row(bcast_bp)
    P.barrier()

    def load_w_third(third):
        for ic in range(8):
            P.emit(DMA_G, lambda e, ic=ic, third=third: e.dma_start(
                out=wt[ic][:], in_=w_attn[ic * 128:(ic + 1) * 128,
                                          third * D:(third + 1) * D]),
                waits=[P.at("tensor")], inc=16)
        P.barrier()

    def load_w_proj():
        for ic in range(8):
            P.emit(DMA_G, lambda e, ic=ic: e.dma_start(
                out=wt[ic][:], in_=w_proj[ic * 128:(ic + 1) * 128, :]),
                waits=[P.at("tensor")], inc=16)
        P.barrier()

    load_w_third(1)
    s_slot = {i: P.at("scalar") for i in range(4)}
    st_tok = P.at(DMA_S)
    for oc in range(8):
        for nt in range(NK // 512):
            sl = nt % 4
            P.emit("tensor", lambda e, oc=oc, nt=nt: (
                [e.matmul(psA[:, (nt % 4) * 512:(nt % 4) * 512 + 512],
                          wt[ic][:, oc * 128:(oc + 1) * 128],
                          xkv[ic][:, nt * 512:(nt + 1) * 512],
                          start=(ic == 0), stop=(ic == 7)) for ic in range(8)][-1]),
                waits=[s_slot[sl]], inc=1)
            t_mm = P.at("tensor")
            P.emit("scalar", lambda e, oc=oc, nt=nt: e.activation(
                out=k_h[:, nt * 512:(nt + 1) * 512],
                in_=psA[:, (nt % 4) * 512:(nt % 4) * 512 + 512],
                func=AF.Identity, bias=batT[:, 8 + oc:9 + oc], scale=1.0),
                waits=[t_mm] + ([st_tok] if nt == 0 else []), inc=1)
            s_slot[sl] = P.at("scalar")
        P.emit(DMA_S, lambda e, oc=oc: e.dma_start(
            out=k_dram.ap()[oc], in_=k_h[:]),
            waits=[s_slot[3]], inc=16)
        st_tok = P.at(DMA_S)
    P.barrier()

    load_w_third(2)
    v_slot = {0: P.at("vector"), 1: P.at("vector")}
    st2 = {-1: P.at(DMA_S), -2: P.at(DMA_S)}
    for t in range(NK // 128):
        po = (t % 2) * 1024
        P.emit("tensor", lambda e, t=t, po=po: (
            [e.matmul(psA[:, po + half * 512:po + (half + 1) * 512],
                      xkv[ic][:, t * 128:(t + 1) * 128],
                      wt[ic][:, half * 512:(half + 1) * 512],
                      start=(ic == 0), stop=(ic == 7))
             for half in range(2) for ic in range(8)][-1]),
            waits=[v_slot[t % 2]], inc=1)
        t_mm = P.at("tensor")
        if not l1:
            P.emit("vector", lambda e, t=t, po=po: e.tensor_tensor(
                v_sb[t][:].rearrange("p (h c) -> p h c", c=HD + 1)[:, :, 0:HD],
                psA[:, po:po + 1024].rearrange("p (h c) -> p h c", c=HD),
                bcast_bv[:].rearrange("p (h c) -> p h c", c=HD), ALU.add),
                waits=[t_mm], inc=1)
            P.emit("vector", lambda e, t=t: e.memset(
                v_sb[t][:].rearrange("p (h c) -> p h c", c=HD + 1)[:, :, HD:HD + 1],
                1.0), inc=1)
            v_slot[t % 2] = P.at("vector")
        else:
            P.emit("vector", lambda e, t=t, po=po: e.tensor_tensor(
                k_h[:, po:po + 1024], psA[:, po:po + 1024], bcast_bv[:],
                ALU.add),
                waits=[t_mm, st2[t - 2]], inc=1)
            v_slot[t % 2] = P.at("vector")
            P.emit(DMA_S, lambda e, t=t, po=po: e.dma_start(
                out=v_dram.ap()[t], in_=k_h[:, po:po + 1024]),
                waits=[v_slot[t % 2]], inc=16)
            st2[t] = P.at(DMA_S)
    P.barrier()

    load_w_third(0)
    s_slot = {i: P.at("scalar") for i in range(4)}
    for nt in range(NQ // 512):
        for oc in range(8):
            sl = oc % 4
            P.emit("tensor", lambda e, oc=oc, nt=nt: (
                [e.matmul(psA[:, (oc % 4) * 512:(oc % 4) * 512 + 512],
                          wt[ic][:, oc * 128:(oc + 1) * 128],
                          xkv[ic][:, nt * 1024 + 512:(nt + 1) * 1024],
                          start=(ic == 0), stop=(ic == 7)) for ic in range(8)][-1]),
                waits=[s_slot[sl]], inc=1)
            t_mm = P.at("tensor")
            P.emit("scalar", lambda e, oc=oc, nt=nt: e.activation(
                out=xkv[oc][:, nt * 512:(nt + 1) * 512],
                in_=psA[:, (oc % 4) * 512:(oc % 4) * 512 + 512],
                func=AF.Identity, bias=batT[:, oc:oc + 1], scale=1.0),
                waits=[t_mm], inc=1)
            s_slot[sl] = P.at("scalar")
    P.barrier()

    q_of = 0
    ao_of = NQ

    def unit_l0(hp, hh, j, k_ap, gw_k):
        h = hp * 2 + (0 if hh == 0 else 1)
        nkt = 8 * (j + 1)
        klen = nkt * 128
        q_ap = xkv[hp][hh:hh + 64, q_of + j * 512:q_of + (j + 1) * 512]
        P.emit("vector", lambda e: e.memset(rm[:], -3.0e38),
               waits=[P.at("tensor")], inc=1)
        v0 = P.at("vector")
        v_red = {-1: v0, -2: v0}
        idx = 0
        for sub in range(4):
            qs = xkv[hp][hh:hh + 64,
                         q_of + j * 512 + sub * 128:q_of + j * 512 + (sub + 1) * 128]
            for r in range((klen + 1023) // 1024):
                kl = min(1024, klen - r * 1024)
                po = (idx % 2) * 1024
                nmm = (kl + 511) // 512
                w = [v_red[idx - 2]] + ([gw_k] if idx == 0 else [])
                P.emit("tensor", lambda e, qs=qs, r=r, kl=kl, nmm=nmm, po=po,
                       k_ap=k_ap: (
                    [e.matmul(psA[:, po + m * 512:
                                  po + m * 512 + min(512, kl - m * 512)],
                              qs, k_ap[hh:hh + 64,
                                       r * 1024 + m * 512:
                                       r * 1024 + m * 512 + min(512, kl - m * 512)],
                              start=True, stop=True) for m in range(nmm)][-1]),
                    waits=w, inc=1)
                t_mm = P.at("tensor")
                P.emit("vector", lambda e, idx=idx, kl=kl, po=po: e.reduce_max(
                    rm[:, idx:idx + 1], psA[:, po:po + kl], axis=AX.X),
                    waits=[t_mm], inc=1)
                v_red[idx] = P.at("vector")
                idx += 1
        P.emit("tensor", lambda e: e.transpose(psT[0:16, 0:128], rm[:], id_f[:]),
               waits=[P.at("vector")], inc=1)
        P.emit("vector", lambda e: e.reduce_max(msc[0:16, 0:1], psT[0:16, 0:128],
                                                axis=AX.X),
               waits=[P.at("tensor")], inc=1)
        P.emit("tensor", lambda e: e.transpose(psT[0:1, 0:16], msc[0:16, 0:1],
                                               id_f[0:16, 0:16]),
               waits=[P.at("vector")], inc=1)
        P.emit("vector", lambda e: e.reduce_max(msc[0:1, 64:65], psT[0:1, 0:16],
                                                axis=AX.X),
               waits=[P.at("tensor")], inc=1)
        P.emit("tensor", lambda e: e.matmul(psT[0:128, 0:1], ones_f[0:1, 0:128],
                                            msc[0:1, 64:65], start=True,
                                            stop=True),
               waits=[P.at("vector")], inc=1)
        P.emit("vector", lambda e: e.tensor_scalar_mul(negM[:], psT[0:128, 0:1],
                                                       -scale),
               waits=[P.at("tensor")], inc=1)
        P.emit("vector", lambda e, j=j: e.tensor_scalar(
            biasall[:, 0:32], kdT[j][:], negM[0:128, 0:1], None, ALU.add),
            chain=True, inc=1)
        v_bias = P.at("vector")
        mslots = {8 * j + s: s for s in range(8)}
        s_prev = P.at("scalar")
        t_S = {}
        t_AV = {}
        v_mask = {}
        s_exp = {}

        def emit_S(kt):
            w = [s_prev, v_bias] if kt < 2 else []
            P.emit("tensor", lambda e, kt=kt, q_ap=q_ap, k_ap=k_ap: e.matmul(
                psS[kt % 2][:], k_ap[hh:hh + 64, kt * 128:(kt + 1) * 128], q_ap,
                start=True, stop=True), waits=w, inc=1)
            t_S[kt] = P.at("tensor")

        emit_S(0)
        for kt in range(nkt):
            buf = kt % 2
            if kt + 1 < nkt:
                emit_S(kt + 1)
            ew = [t_S[kt]]
            if kt in mslots:
                P.emit("vector", lambda e, kt=kt, buf=buf, s=mslots[kt]:
                       e.tensor_tensor(psS[buf][:], psS[buf][:], dm_sb[s][:],
                                       ALU.add),
                       waits=[t_S[kt]], inc=1)
                v_mask[kt] = P.at("vector")
                ew.append(v_mask[kt])
            if kt >= 2:
                ew.append(t_AV[kt - 2])
            P.emit("scalar", lambda e, kt=kt, buf=buf: e.activation(
                out=pt_sb[buf][:], in_=psS[buf][:], func=AF.Exp,
                bias=biasall[:, kt:kt + 1], scale=scale),
                waits=ew, inc=1)
            s_exp[kt] = P.at("scalar")
            P.emit("tensor", lambda e, kt=kt, buf=buf, h=h: e.matmul(
                psV[0:65, :], v_sb[kt][:, h * 65:h * 65 + 65], pt_sb[buf][:],
                start=(kt == 0), stop=(kt == nkt - 1)),
                waits=[s_exp[kt]], inc=1)
            t_AV[kt] = P.at("tensor")
        P.emit("scalar", lambda e: e.activation(out=msc[64:65, 0:512],
                                                in_=psV[64:65, :],
                                                func=AF.Ln,
                                                bias=consts["eps_col"][64:65, 0:1]),
               waits=[P.at("tensor")], inc=1)
        s_l = P.at("scalar")
        P.emit("tensor", lambda e: e.matmul(psT[0:64, 0:512],
                                            ones_f[64:65, 0:64],
                                            msc[64:65, 0:512],
                                            start=True, stop=True),
               waits=[s_l], inc=1)
        t_bc = P.at("tensor")
        P.emit("scalar", lambda e: e.activation(out=rb_sb[0:64, 0:512],
                                                in_=psT[0:64, 0:512],
                                                func=AF.Exp, scale=-1.0),
               waits=[t_bc], inc=1)
        s_rb = P.at("scalar")
        P.emit("vector", lambda e, hp=hp, hh=hh, j=j: e.tensor_tensor(
            xkv[hp][hh:hh + 64, ao_of + j * 512:ao_of + (j + 1) * 512],
            psV[0:64, :], rb_sb[0:64, 0:512], ALU.mult),
            waits=[s_rb], inc=1)

    if not l1:
        for j in range(4):
            for s in range(8):
                P.emit(DMA_S, lambda e, j=j, s=s: e.dma_start(
                    out=dm_sb[s][:], in_=dmask[j, s]),
                    waits=[P.at("scalar"), P.at("vector")], inc=16)
            P.barrier()
            t_end = {}
            gw_k = {}

            def emit_kload(hp):
                kw = [t_end[hp - 2]] if hp >= 2 else [P.at("tensor")]
                P.emit(DMA_G, lambda e, hp=hp: e.dma_start(
                    out=k_hb[hp % 2][:], in_=k_dram.ap()[hp]),
                    waits=kw, inc=16)
                gw_k[hp] = P.at(DMA_G)

            emit_kload(0)
            emit_kload(1)
            for hp in range(8):
                for hh in (0, 64):
                    unit_l0(hp, hh, j, k_hb[hp % 2], gw_k[hp])
                t_end[hp] = P.at("tensor")
                if hp + 2 < 8:
                    emit_kload(hp + 2)
            P.barrier()
    else:
        t_kfree = P.at("tensor")
        s_prevexp = P.at("scalar")
        av0 = P.at("tensor")
        av_free = {0: av0, 1: av0, 2: av0, 3: av0}
        for j in range(4):
            nkt = 8 * (j + 1)
            klen = nkt * 128
            ka = min(2048, klen)
            for ic in range(8):
                P.emit(DMA_G if ic % 2 == 0 else DMA_S,
                       lambda e, ic=ic, ka=ka: e.dma_start(
                           out=kbuf[ic][:, 0:ka],
                           in_=k_dram.ap()[ic][:, 0:ka]),
                       waits=[t_kfree], inc=16)
            gwA = P.at(DMA_G) + P.at(DMA_S)
            for ic in range(8 if klen > 2048 else 0):
                P.emit(DMA_G if ic % 2 == 0 else DMA_S,
                       lambda e, ic=ic, klen=klen: e.dma_start(
                           out=kbuf[ic][:, 2048:klen],
                           in_=k_dram.ap()[ic][:, 2048:klen]),
                       waits=[t_kfree], inc=16)
            gwB = P.at(DMA_G) + P.at(DMA_S)
            for s8 in range(8):
                P.emit(DMA_S, lambda e, j=j, s8=s8: e.dma_start(
                    out=dm_sb[s8][:], in_=dmask[j, s8]),
                    waits=[s_prevexp, P.at("vector") if s8 == 0 else []],
                    inc=16)
            gw_dm = P.at(DMA_S)
            mslots = {8 * j + s8: s8 for s8 in range(8)}
            P.emit("vector", lambda e: e.memset(rm[:], -3.0e38),
                   waits=[P.at("tensor")], inc=1)
            v0 = P.at("vector")
            v_red = {-1: v0, -2: v0}
            idx = 0
            for sub in range(4):
                qcol = j * 512 + sub * 128
                for r in range((klen + 1023) // 1024):
                    kl = min(1024, klen - r * 1024)
                    po = (idx % 2) * 1024
                    nmm = (kl + 511) // 512
                    gw = gwA if (r + 1) * 1024 <= 2048 else gwB
                    P.emit("tensor", lambda e, qcol=qcol, r=r, kl=kl,
                           nmm=nmm, po=po: (
                        [e.matmul(psA[:, po + m * 512:
                                      po + m * 512 + min(512, kl - m * 512)],
                                  xkv[ic][:, q_of + qcol:q_of + qcol + 128],
                                  kbuf[ic][:, r * 1024 + m * 512:
                                           r * 1024 + m * 512 +
                                           min(512, kl - m * 512)],
                                  start=(ic == 0), stop=(ic == 7))
                         for m in range(nmm) for ic in range(8)][-1]),
                        waits=[gw, v_red[idx - 2]], inc=1)
                    t_mm = P.at("tensor")
                    P.emit("vector", lambda e, idx=idx, kl=kl, po=po:
                           e.reduce_max(rm[:, idx:idx + 1], psA[:, po:po + kl],
                                        axis=AX.X),
                           waits=[t_mm], inc=1)
                    v_red[idx] = P.at("vector")
                    idx += 1
            v_p1 = P.at("vector")
            P.emit("tensor", lambda e: e.transpose(psT[0:16, 0:128], rm[:],
                                                   id_f[:]),
                   waits=[v_p1], inc=1)
            P.emit("vector", lambda e: e.reduce_max(msc[0:16, 0:1],
                                                    psT[0:16, 0:128],
                                                    axis=AX.X),
                   waits=[P.at("tensor")], inc=1)
            P.emit("tensor", lambda e: e.transpose(psT[0:1, 0:16],
                                                   msc[0:16, 0:1],
                                                   id_f[0:16, 0:16]),
                   waits=[P.at("vector")], inc=1)
            P.emit("vector", lambda e: e.reduce_max(msc[0:1, 64:65],
                                                    psT[0:1, 0:16],
                                                    axis=AX.X),
                   waits=[P.at("tensor")], inc=1)
            P.emit("tensor", lambda e: e.matmul(psT[0:128, 0:1],
                                                ones_f[0:1, 0:128],
                                                msc[0:1, 64:65],
                                                start=True, stop=True),
                   waits=[P.at("vector")], inc=1)
            P.emit("vector", lambda e: e.tensor_scalar_mul(
                negM[:], psT[0:128, 0:1], -scale),
                waits=[P.at("tensor")], inc=1)
            P.emit("vector", lambda e, j=j: e.tensor_scalar(
                biasall[:, 0:32], kdT[j][:], negM[0:128, 0:1], None,
                ALU.add), chain=True, inc=1)
            v_bias = P.at("vector")
            s_exp = {}
            t_S = {}
            for kt in range(nkt):
                buf = kt % 2
                w = [s_exp[kt - 2]] if kt >= 2 else [s_prevexp, v_bias]
                if kt * 128 >= 2048 and (kt - 2 < 0 or (kt - 2) * 128 < 2048):
                    w.append(gwB)
                P.emit("tensor", lambda e, kt=kt, buf=buf, j=j: (
                    [e.matmul(psS[buf][:, 0:512],
                              kbuf[ic][:, kt * 128:(kt + 1) * 128],
                              xkv[ic][:, q_of + j * 512:q_of + (j + 1) * 512],
                              start=(ic == 0), stop=(ic == 7))
                     for ic in range(8)][-1]),
                    waits=w, inc=1)
                t_S[kt] = P.at("tensor")
                ew = [t_S[kt]]
                if kt in mslots:
                    P.emit("vector", lambda e, buf=buf, s8=mslots[kt]:
                           e.tensor_tensor(psS[buf][:, 0:512],
                                           psS[buf][:, 0:512],
                                           dm_sb[s8][:], ALU.add),
                           waits=[t_S[kt], gw_dm], inc=1)
                    ew.append(P.at("vector"))
                P.emit("scalar", lambda e, kt=kt, buf=buf: e.activation(
                    out=pt_sb[kt][:], in_=psS[buf][:, 0:512],
                    func=AF.Exp, bias=biasall[:, kt:kt + 1],
                    scale=scale),
                    waits=ew, inc=1)
                s_exp[kt] = P.at("scalar")
                P.emit("tensor", lambda e, kt=kt, nkt=nkt: e.matmul(
                    psV[0:1, 0:512], ones_col_bf[:], pt_sb[kt][:],
                    start=(kt == 0), stop=(kt == nkt - 1)),
                    waits=[s_exp[kt]], inc=1)
            t_kfree = P.at("tensor")
            P.emit("scalar", lambda e: e.activation(
                out=msc[0:1, 0:512], in_=psV[0:1, 0:512],
                func=AF.Ln, bias=consts["eps_col"][0:1, 0:1]),
                waits=[t_kfree], inc=1)
            s_l = P.at("scalar")
            P.emit("tensor", lambda e: e.matmul(psT[0:128, 0:512],
                                                ones_f[0:1, 0:128],
                                                msc[0:1, 0:512],
                                                start=True, stop=True),
                   waits=[s_l], inc=1)
            t_bc = P.at("tensor")
            P.emit("scalar", lambda e: e.activation(
                out=rb_sb[0:128, 0:512], in_=psT[0:128, 0:512],
                func=AF.Exp, scale=-1.0),
                waits=[t_bc], inc=1)
            s_rb = P.at("scalar")
            s_prevexp = s_rb
            v_mults = None
            for half in range(2):
                hof = half * 512
                for kt in range(nkt):
                    g = half * nkt + kt
                    slot = g % 4
                    vq = DMA_G if slot % 2 == 0 else DMA_S
                    P.emit(vq, lambda e, kt=kt, hof=hof, slot=slot:
                           e.dma_start(out=vbuf[slot],
                                       in_=v_dram.ap()[kt][:, hof:hof + 512]),
                           waits=[av_free[slot]], inc=16)
                    gwv = P.at(vq)
                    wv = [gwv]
                    if half == 0:
                        wv.append(s_exp[kt])
                        if kt == 0:
                            wv.append(v_p1)
                    elif kt == 0:
                        wv.append(v_mults)
                    P.emit("tensor", lambda e, kt=kt, slot=slot, nkt=nkt: (
                        [e.matmul(psA[:, dc * 512:(dc + 1) * 512],
                                  vbuf[slot][:, dc * 128:(dc + 1) * 128],
                                  pt_sb[kt][:],
                                  start=(kt == 0), stop=(kt == nkt - 1),
                                  skip_group_check=True)
                         for dc in range(4)][-1]),
                        waits=wv, inc=1)
                    av_free[slot] = P.at("tensor")
                t_avs = P.at("tensor")
                for dc2 in range(4):
                    dc = half * 4 + dc2
                    P.emit("vector", lambda e, dc=dc, dc2=dc2, j=j:
                           e.tensor_tensor(
                               xkv[dc][:, ao_of + j * 512:ao_of + (j + 1) * 512],
                               psA[:, dc2 * 512:(dc2 + 1) * 512],
                               rb_sb[0:128, 0:512], ALU.mult),
                           waits=[t_avs, s_rb] if dc2 == 0 else [], inc=1)
                v_mults = P.at("vector")
        P.barrier()


    load_w_proj()
    v_xa = {-1: P.at("vector"), -2: P.at("vector")}
    s_y = {-1: P.at("scalar")}
    st = {-1: P.at(DMA_S)}
    for t in range(NQ // 128):
        pa = (t % 2) * 1024
        P.emit(DMA_G, lambda e, t=t: e.indirect_dma_start(
            out=xo[t % 2][:], out_offset=None, in_=x_src[:, :],
            in_offset=IndirectOffsetOnAxis(idxown_sb[:, t:t + 1], 0)),
            waits=[v_xa[t - 2], ccw_own[t] if ccw_own else ccw], inc=16)
        dmaw = P.at(DMA_G)
        P.emit("tensor", lambda e, t=t, pa=pa: (
            [e.matmul(psA[:, pa + half * 512:pa + (half + 1) * 512],
                      xkv[ic][:, ao_of + t * 128:ao_of + (t + 1) * 128],
                      wt[ic][:, half * 512:(half + 1) * 512],
                      start=(ic == 0), stop=(ic == 7))
             for half in range(2) for ic in range(8)][-1]),
            waits=[v_xa[t - 2]] + ([P.at("scalar")] if t == 0 else []), inc=1)
        t_mm = P.at("tensor")
        P.emit("vector", lambda e, t=t, pa=pa: e.tensor_tensor(
            xa[:], psA[:, pa:pa + 1024], xo[t % 2][:], ALU.add),
            waits=[t_mm, dmaw, s_y[t - 1]], inc=1)
        v_xa[t] = P.at("vector")
        P.emit("vector", lambda e: e.tensor_tensor(xa[:], xa[:], bcast_bp[:],
                                                   ALU.add), chain=True, inc=1)
        P.emit("vector", lambda e: e.bn_stats(ln_s[:, 8:14], xa[:, 0:512]),
               chain=True, inc=1)
        P.emit("vector", lambda e: e.bn_stats(ln_s[:, 14:20], xa[:, 512:1024]),
               chain=True, inc=1)
        P.emit("vector", lambda e: e.bn_aggr(ln_s[:, 0:2], ln_s[:, 8:20]),
               chain=True, inc=1)
        P.emit("vector", lambda e: e.tensor_scalar_add(ln_s[:, 2:3],
                                                       ln_s[:, 1:2], 1e-5),
               chain=True, inc=1)
        v_ln = P.at("vector")
        P.emit("scalar", lambda e: e.activation(out=ln_s[:, 3:4],
                                                in_=ln_s[:, 2:3], func=AF.Sqrt),
               waits=[v_ln], inc=1)
        s_sq2 = P.at("scalar")
        P.emit("vector", lambda e: e.reciprocal(ln_s[:, 4:5], ln_s[:, 3:4]),
               waits=[s_sq2], inc=1)
        P.emit("vector", lambda e: e.tensor_tensor(ln_s[:, 5:6], ln_s[:, 0:1],
                                                   ln_s[:, 4:5], ALU.mult),
               chain=True, inc=1)
        P.emit("vector", lambda e: e.tensor_scalar_mul(ln_s[:, 5:6],
                                                       ln_s[:, 5:6], -1.0),
               chain=True, inc=1)
        v_fin = P.at("vector")
        P.emit("scalar", lambda e, t=t: e.activation(
            out=y_sb[:], in_=xa[:], func=AF.Identity, bias=ln_s[:, 5:6],
            scale=ln_s[:, 4:5]),
            waits=[v_fin, st[t - 1]], inc=1)
        s_y[t] = P.at("scalar")
        P.emit(DMA_S, lambda e, t=t: e.dma_start(
            out=y[t * 128:(t + 1) * 128, :], in_=y_sb[:]),
            waits=[s_y[t]], inc=16)
        st[t] = P.at(DMA_S)
        if ccp is not None and t % 4 == 3:
            ccp(P, t // 4, P.at(DMA_S))
    P.emit(DMA_S, lambda e: None, waits=[P.at(DMA_S)])




def emit_stage_ffn(nc, P, AL, consts, D, last=False):
    NH2 = 2816
    NKV = 2688
    NATT = 2176
    NOWN = 2048
    HIDP = 2816
    x_src = D["x_src"]
    wqkv = D["wqkv"]
    qs_blk = D["qs_blk"]
    ks_blk = D["ks_blk"]
    wo = D["wo"]
    ln_g = D["ln_g"]
    ln_b = D["ln_b"]
    w1a = D["w1a"]
    w1g = D["w1g"]
    b1a = D["b1a"]
    b1g = D["b1g"]
    w2p = D["w2p"]
    b2 = D["b2"]
    kdead = D["kdead"]
    tabs = {tn: D[tn] for tn in ("aq", "bq", "ak", "bk")}
    if last:
        y1_src = D["y1_src"]
    y = D["ydst"]
    xnew_scr = D["xnew_scr"]
    id_b = consts["id_bf"][:]
    id_f = consts["id_f32"]
    ones_f = consts["ones_f32"]
    headsum = consts["headsum"]
    rotT = consts["rotT"]
    ccw = D.get("ccw") or []
    ccw1 = D.get("ccw_y1") or []
    ccw_halo = D.get("ccw_halo")
    ccw_own1 = D.get("ccw_own1")
    ccp = D.get("ccp")
    w1abf = D["w1abf"]
    w1gbf = D["w1gbf"]
    w2bf = D["w2bf"]

    idxh_sb = AL("idxh_sb", [128, 22], I32)
    idxown_sb = AL("idxown2_sb", [128, 16], I32)
    P.emit("sync", lambda e: e.dma_start(out=idxh_sb, in_=D["idx_halo"]), inc=16)
    P.emit("sync", lambda e: e.dma_start(out=idxown_sb, in_=D["idx_own"]),
           inc=16)
    xf = [AL(f"xf{i}", [128, NH2], BF16) for i in range(8)]
    wq_sb = [AL(f"wq{i}", [128, 1536], BF16) for i in range(8)]
    qr = [AL(f"qr{i}", [128, NATT], BF16) for i in range(4)]
    kr = [AL(f"kr{i}", [128, NKV], BF16) for i in range(4)]
    v_sb = [AL(f"vv{i}", [128, 520], BF16) for i in range(21)]

    def hgt(j):
        if j < 20:
            return kr[j % 4][:, (j // 4) * 512:(j // 4) * 512 + 512]
        return qr[j - 20][:, 0:512]
    tbuf_off = AL.off
    tbuf = {tn: AL("tb_" + tn, [128, 512], F32)
            for tn in ("aq", "bq", "ak", "bk")}
    lm_sb = AL.at(tbuf_off, [128, 4096], BF16)
    qhat_off = AL.off
    qhat = AL("qhat", [128, 512], F32)
    t1 = AL("t1", [128, 512], F32)
    t2_off = AL.off
    t2 = AL("t2", [128, 512], F32)
    ssb = AL("ssb", [128, 512], F32)
    sqt = AL("sqt", [128, 512], F32)
    rn = AL("rn", [2, 512], F32)
    pt_sb = [AL(f"pt{i}", [128, 512], BF16) for i in range(2)]
    kdT = AL("kdT", [128, 21], F32)
    kdraw = AL("kdraw", [21, 128], F32)
    biasu = AL("biasu", [128, 21], F32)
    negM = AL("negM", [128, 1], F32)
    msc = AL("msc", [128, 1024], F32)
    rb_sb = AL("rb_sb", [128, 512], F32)
    b1aT = AL("b1aT", [128, 22], F32)
    b1gT = AL("b1gT", [128, 22], F32)
    bT_raw = AL("bT_raw", [22, 128], F32)
    bc_g = AL("bc_g", [128, 1024], F32)
    bc_b = AL("bc_b", [128, 1024], F32)
    bc_b2 = AL("bc_b2", [128, 1024], F32)
    qsb_sb = AL("qsb_sb", [2, 128], F32)
    ksb_sb = AL("ksb_sb", [2, 128], F32)
    w1abuf2 = [wq_sb[0][:, 0:1024], wq_sb[1][:, 0:1024]]
    w1gbuf2 = [wq_sb[2][:, 0:1024], wq_sb[3][:, 0:1024]]
    w2buf2 = [wq_sb[4][:, 0:1024], wq_sb[5][:, 0:1024]]
    xo = [AL(f"xo{i}", [128, 1024], F32) for i in range(2)]
    xn = [AL(f"xn{i}", [128, 1024], F32) for i in range(2)]
    h2 = AL("h2", [128, 1024], F32)
    xa = AL("xa", [128, 1024], F32)
    ag_sb = AL("ag_sb", [128, 512], BF16)
    ag_b = AL("ag_b", [128, 512], BF16)
    sg_b = AL("sg_b", [128, 512], BF16)
    ln_s = AL("ln_s", [128, 24], F32)
    y_sb = AL("y_sb", [128, 1024], F32)
    sq2 = y_sb

    psA = D["psA"]
    psS = D["psS"]
    psV = D["psV"]
    psT = D["psT"]
    psTb = psT.bitcast(BF16)

    DMA_G, DMA_S = "gpsimd", "sync"

    xs2 = [sqt.bitcast(BF16)[:, 0:1024], qhat.bitcast(BF16)[:, 0:1024]]
    pss2 = [psTb[0:128, 0:1024], psS[0].bitcast(BF16)[:, 0:1024]]
    t_tr = {-1: P.at("tensor"), -2: P.at("tensor")}
    v_cp = {-1: P.at("vector"), -2: P.at("vector")}
    for t in range(NH2 // 128):
        tb = t % 2
        w = [t_tr[t - 2], ccw_halo[t] if ccw_halo else ccw]
        if t == 0:
            w += [P.at("sync"), P.at("vector")]
        P.emit(DMA_G, lambda e, t=t, tb=tb: e.indirect_dma_start(
            out=xs2[tb][:], out_offset=None, in_=x_src[:, :],
            in_offset=IndirectOffsetOnAxis(idxh_sb[:, t:t + 1], 0)),
            waits=w, inc=16)
        gw = P.at(DMA_G)
        P.emit("tensor", lambda e, tb=tb: (
            [e.transpose(pss2[tb][:, ic * 128:(ic + 1) * 128],
                         xs2[tb][:, ic * 128:(ic + 1) * 128], id_b)
             for ic in range(8)][-1]),
            waits=[gw, v_cp[t - 2]], inc=1)
        t_tr[t] = P.at("tensor")
        P.emit("vector", lambda e, t=t, tb=tb: (
            [e.tensor_copy(xf[ic][:, t * 128:(t + 1) * 128],
                           pss2[tb][:, ic * 128:(ic + 1) * 128])
             for ic in range(8)][-1]),
            waits=[t_tr[t]], inc=1)
        v_cp[t] = P.at("vector")

    for ic in range(8):
        P.emit(DMA_G, lambda e, ic=ic: e.dma_start(
            out=wq_sb[ic][:], in_=wqkv[ic * 128:(ic + 1) * 128, :]), inc=16)
    P.emit(DMA_S, lambda e: e.dma_start(out=kdraw[:], in_=kdead), inc=16)
    P.emit(DMA_S, lambda e: e.dma_start(out=qsb_sb[:], in_=qs_blk), inc=16)
    P.emit(DMA_S, lambda e: e.dma_start(out=ksb_sb[:], in_=ks_blk), inc=16)
    P.barrier()
    P.emit(DMA_G, lambda e: e.dma_start(out=w1abf.ap(), in_=w1a), inc=16)
    P.emit(DMA_G, lambda e: e.dma_start(out=w1gbf.ap(), in_=w1g), inc=16)
    for c4 in range(4):
        P.emit(DMA_G, lambda e, c4=c4: e.dma_start(
            out=w2bf.ap()[c4 * 704:(c4 + 1) * 704, :],
            in_=w2p[c4 * 704:(c4 + 1) * 704, :]), inc=16)
    gw_cast = P.at(DMA_G)
    P.emit("tensor", lambda e: e.transpose(psT[0:128, 0:21], kdraw[:],
                                           id_f[0:21, 0:21]), inc=1)
    P.emit("vector", lambda e: e.tensor_copy(kdT[:], psT[0:128, 0:21]),
           waits=[P.at("tensor")], inc=1)
    P.barrier()

    def hcol(c, c0, w):
        off = 128 if c < 4 else 127
        return xf[c][:, off + c0: off + c0 + w]

    rst = {"s_sq": P.at("scalar"), "v_qhat": P.at("vector"),
           "t_hs": P.at("tensor"), "v_max": P.at("vector"),
           "t_scale": P.at("tensor"), "v_fin": P.at("vector")}

    def rotary_block(dst, oc, c0, w, qk, tile_cols):
        A = tbuf["a" + qk]
        B = tbuf["b" + qk]
        sblk = qsb_sb if qk == "q" else ksb_sb
        third = 0 if qk == "q" else 1
        pos0 = 128 + tile_cols
        P.emit(DMA_S, lambda e: e.dma_start(
            out=A[:, 0:w], in_=tabs["a" + qk][:, pos0:pos0 + w]),
            waits=[rst["v_fin"]], inc=16)
        P.emit(DMA_S, lambda e: e.dma_start(
            out=B[:, 0:w], in_=tabs["b" + qk][:, pos0:pos0 + w]),
            waits=[rst["v_fin"]], inc=16)
        dmw = P.at(DMA_S)
        P.emit("tensor", lambda e: (
            [e.matmul(psA[:, 0:w], wq_sb[ic][:, third * 512 + oc * 128:
                                             third * 512 + (oc + 1) * 128],
                      hcol(ic, tile_cols, w), start=(ic == 0), stop=(ic == 7))
             for ic in range(8)][-1]),
            waits=[rst["s_sq"], rst["v_qhat"]], inc=1)
        t_mm = P.at("tensor")
        P.emit("scalar", lambda e: e.activation(out=sqt[:, 0:w], in_=psA[:, 0:w],
                                                func=AF.Square),
               waits=[t_mm, rst["t_hs"]], inc=1)
        rst["s_sq"] = P.at("scalar")
        P.emit("tensor", lambda e: e.matmul(psT[0:2, 0:w], headsum[:],
                                            sqt[:, 0:w], start=True, stop=True),
               waits=[rst["s_sq"], rst["v_max"]], inc=1)
        rst["t_hs"] = P.at("tensor")
        P.emit("vector", lambda e: e.tensor_scalar_max(rn[0:2, 0:w],
                                                       psT[0:2, 0:w], 1e-24),
               waits=[rst["t_hs"], rst["t_scale"]], inc=1)
        rst["v_max"] = P.at("vector")
        P.emit("scalar", lambda e: e.activation(out=rn[0:2, 0:w], in_=rn[0:2, 0:w],
                                                func=AF.Sqrt),
               waits=[rst["v_max"]], inc=1)
        s_sqrt = P.at("scalar")
        P.emit("vector", lambda e: e.reciprocal(rn[0:2, 0:w], rn[0:2, 0:w]),
               waits=[s_sqrt], inc=1)
        v_recip = P.at("vector")
        P.emit("tensor", lambda e: e.matmul(psS[0][:, 0:w], sblk[:], rn[0:2, 0:w],
                                            start=True, stop=True),
               waits=[v_recip], inc=1)
        rst["t_scale"] = P.at("tensor")
        P.emit("vector", lambda e: e.tensor_copy(ssb[:, 0:w], psS[0][:, 0:w]),
               waits=[rst["t_scale"]], inc=1)
        P.emit("vector", lambda e: e.tensor_tensor(qhat[:, 0:w], psA[:, 0:w],
                                                   ssb[:, 0:w], ALU.mult),
               chain=True, inc=1)
        rst["v_qhat"] = P.at("vector")
        P.emit("tensor", lambda e: e.matmul(psS[1][:, 0:w],
                                            rotT[:], qhat[:, 0:w],
                                            start=True, stop=True),
               waits=[rst["v_qhat"]], inc=1)
        t_rot = P.at("tensor")
        P.emit("vector", lambda e: e.tensor_tensor(t1[:, 0:w], qhat[:, 0:w],
                                                   A[:, 0:w], ALU.mult),
               waits=[dmw], chain=True, inc=1)
        P.emit("vector", lambda e: e.tensor_tensor(t2[:, 0:w], psS[1][:, 0:w],
                                                   B[:, 0:w], ALU.mult),
               waits=[t_rot], chain=True, inc=1)
        P.emit("vector", lambda e, dst=dst, oc=oc, c0=c0: e.tensor_tensor(
            dst[oc][:, c0:c0 + w], t1[:, 0:w], t2[:, 0:w], ALU.add),
            chain=True, inc=1)
        rst["v_fin"] = P.at("vector")

    for oc in range(4):
        c0 = 0
        while c0 < NATT:
            w = min(512, NATT - c0)
            rotary_block(qr, oc, c0, w, "q", 512 + c0)
            c0 += w
    for oc in range(4):
        c0 = 0
        while c0 < NKV:
            w = min(512, NKV - c0)
            rotary_block(kr, oc, c0, w, "k", c0)
            c0 += w

    v_vt = {-1: P.at("vector"), -2: P.at("vector")}
    for t in range(21):
        pa = (t % 2) * 512
        P.emit("tensor", lambda e, t=t, pa=pa: (
            [e.matmul(psA[:, pa:pa + 512],
                      hcol(ic, t * 128, 128),
                      wq_sb[ic][:, 1024:1536],
                      start=(ic == 0), stop=(ic == 7)) for ic in range(8)][-1]),
            waits=[v_vt[t - 2]] +
                  ([P.at("vector"), P.at("scalar")] if t == 0 else []),
            inc=1)
        t_mm = P.at("tensor")
        P.emit("vector", lambda e, t=t, pa=pa: e.tensor_copy(
            v_sb[t][:].rearrange("p (h c) -> p h c", c=65)[:, :, 0:64],
            psA[:, pa:pa + 512].rearrange("p (h c) -> p h c", c=64)),
            waits=[t_mm], inc=1)
        P.emit("vector", lambda e, t=t: e.memset(
            v_sb[t][:].rearrange("p (h c) -> p h c", c=65)[:, :, 64:65], 1.0),
            inc=1)
        v_vt[t] = P.at("vector")
    P.barrier()

    for m in range(8):
        for s in range(4):
            ms = m - s
            dst = lm_sb[:, m * 512 + s * 128:m * 512 + (s + 1) * 128]
            if ms == 0:
                P.emit("vector", lambda e, dst=dst: e.tensor_copy(
                    dst, consts["tri_hi"][:]), inc=1)
            elif ms == 4:
                P.emit("vector", lambda e, dst=dst: e.tensor_copy(
                    dst, consts["tri_lo"][:]), inc=1)
            elif 1 <= ms <= 3:
                P.emit("vector", lambda e, dst=dst: e.memset(dst, 0.0), inc=1)
            else:
                P.emit("vector", lambda e, dst=dst: e.memset(dst, NEG), inc=1)

    ao_of = 0

    def lattn_unit(hp, hh, h, qt0, G):
        nm = G + 4
        QW = G * 128
        q_ap = qr[hp][hh:hh + 64, qt0 * 128:(qt0 + G) * 128]
        v0 = P.at("vector")
        v_red = {-1: v0, -2: v0}
        for s in range(G):
            po = (s % 2) * 1024
            kw = nm * 128
            qs = qr[hp][hh:hh + 64, (qt0 + s) * 128:(qt0 + s + 1) * 128]
            P.emit("tensor", lambda e, qs=qs, po=po, kw=kw, hp=hp, hh=hh: (
                [e.matmul(psA[:, po + m5 * 512:po + m5 * 512 +
                              min(512, kw - m5 * 512)],
                          qs, kr[hp][hh:hh + 64,
                                     qt0 * 128 + m5 * 512:
                                     qt0 * 128 + m5 * 512 +
                                     min(512, kw - m5 * 512)],
                          start=True, stop=True)
                 for m5 in range((kw + 511) // 512)][-1]),
                waits=[v_red[s - 2]], inc=1)
            t_mm = P.at("tensor")
            P.emit("vector", lambda e, s=s, po=po, kw=kw: e.reduce_max(
                msc[:, s:s + 1], psA[:, po:po + kw], axis=AX.X),
                waits=[t_mm], inc=1)
            v_red[s] = P.at("vector")
        P.emit("tensor", lambda e, G=G: e.transpose(
            psT[0:G, 0:128], msc[:, 0:G], id_f[:]),
            waits=[P.at("vector")], inc=1)
        P.emit("vector", lambda e, G=G: e.reduce_max(
            msc[0:G, 8:9], psT[0:G, 0:128], axis=AX.X),
            waits=[P.at("tensor")], inc=1)
        P.emit("tensor", lambda e, G=G: e.transpose(
            psT[0:1, 0:G], msc[0:G, 8:9], id_f[0:G, 0:G]),
            waits=[P.at("vector")], inc=1)
        P.emit("vector", lambda e, G=G: e.reduce_max(
            msc[0:1, 8:9], psT[0:1, 0:G], axis=AX.X),
            waits=[P.at("tensor")], inc=1)
        P.emit("tensor", lambda e: e.matmul(psT[0:128, 0:1], ones_f[0:1, 0:128],
                                            msc[0:1, 8:9], start=True,
                                            stop=True),
               waits=[P.at("vector")], inc=1)
        P.emit("vector", lambda e: e.tensor_scalar_mul(negM[:],
                                                       psT[0:128, 0:1], -8.0),
               waits=[P.at("tensor")], inc=1)
        P.emit("vector", lambda e, qt0=qt0, nm=nm: e.tensor_scalar(
            biasu[:, 0:nm], kdT[:, qt0:qt0 + nm], negM[0:128, 0:1], None,
            ALU.add), chain=True, inc=1)
        v_bias = P.at("vector")
        s_prev = P.at("scalar")
        t_S = {}
        t_AV = {}
        s_exp = {}

        def emit_S(m):
            w = [s_prev, v_bias] if m < 2 else []
            P.emit("tensor", lambda e, m=m, q_ap=q_ap, hp=hp, hh=hh: e.matmul(
                psS[m % 2][:, 0:QW],
                kr[hp][hh:hh + 64, (qt0 + m) * 128:(qt0 + m + 1) * 128],
                q_ap, start=True, stop=True), waits=w, inc=1)
            t_S[m] = P.at("tensor")

        emit_S(0)
        for m in range(nm):
            buf = m % 2
            if m + 1 < nm:
                emit_S(m + 1)
            ew = [t_S[m]]
            if G == 4:
                P.emit("vector", lambda e, m=m, buf=buf: e.tensor_tensor(
                    psS[buf][:, 0:QW], psS[buf][:, 0:QW],
                    lm_sb[:, m * 512:m * 512 + QW], ALU.add),
                    waits=[t_S[m]], inc=1)
                ew.append(P.at("vector"))
            elif m == 0 or m == nm - 1:
                tri = consts["tri_hi"] if m == 0 else consts["tri_lo"]
                P.emit("vector", lambda e, m=m, buf=buf, tri=tri: e.tensor_tensor(
                    psS[buf][:, 0:QW], psS[buf][:, 0:QW], tri[:], ALU.add),
                    waits=[t_S[m]], inc=1)
                ew.append(P.at("vector"))
            if m >= 2:
                ew.append(t_AV[m - 2])
            P.emit("scalar", lambda e, m=m, buf=buf: e.activation(
                out=pt_sb[buf][:, 0:QW], in_=psS[buf][:, 0:QW], func=AF.Exp,
                bias=biasu[:, m:m + 1], scale=8.0),
                waits=ew, inc=1)
            s_exp[m] = P.at("scalar")
            P.emit("tensor", lambda e, m=m, buf=buf, h=h, nm=nm: e.matmul(
                psV[0:65, 0:QW], v_sb[qt0 + m][:, h * 65:h * 65 + 65],
                pt_sb[buf][:, 0:QW], start=(m == 0), stop=(m == nm - 1)),
                waits=[s_exp[m]], inc=1)
            t_AV[m] = P.at("tensor")
        P.emit("scalar", lambda e: e.activation(
            out=msc[64:65, 512:512 + QW], in_=psV[64:65, 0:QW],
            func=AF.Ln, bias=consts["eps_col"][64:65, 0:1]),
            waits=[P.at("tensor")], inc=1)
        s_l = P.at("scalar")
        P.emit("tensor", lambda e: e.matmul(psT[0:64, 0:QW],
                                            ones_f[64:65, 0:64],
                                            msc[64:65, 512:512 + QW],
                                            start=True, stop=True),
               waits=[s_l], inc=1)
        t_bc = P.at("tensor")
        P.emit("scalar", lambda e: e.activation(
            out=rb_sb[0:64, 0:QW], in_=psT[0:64, 0:QW],
            func=AF.Exp, scale=-1.0),
            waits=[t_bc], inc=1)
        s_rb = P.at("scalar")
        P.emit("vector", lambda e, hp=hp, hh=hh, qt0=qt0: e.tensor_tensor(
            xf[hp][hh:hh + 64, qt0 * 128:qt0 * 128 + QW],
            psV[0:64, 0:QW], rb_sb[0:64, 0:QW], ALU.mult),
            waits=[s_rb], inc=1)

    for qt0, G in ((0, 4), (4, 4), (8, 4), (12, 4), (16, 1)):
        for h in range(8):
            lattn_unit(h // 2, (h % 2) * 64, h, qt0, G)
    P.barrier()

    for ic in range(4):
        P.emit(DMA_G, lambda e, ic=ic: e.dma_start(
            out=wq_sb[ic][:, 0:1024], in_=wo[ic * 128:(ic + 1) * 128, :]),
            waits=[P.at("tensor")], inc=16)
    P.barrier()
    v_add = {-1: P.at("vector"), -2: P.at("vector")}
    st_x = {-1: P.at(DMA_S), -2: P.at(DMA_S)}
    for tq in range(17):
        pa = (tq % 2) * 1024
        P.emit(DMA_G, lambda e, tq=tq: e.indirect_dma_start(
            out=xo[tq % 2][:], out_offset=None, in_=x_src[:, :],
            in_offset=IndirectOffsetOnAxis(idxh_sb[:, 5 + tq:6 + tq], 0)),
            waits=[v_add[tq - 2],
                   ccw_halo[5 + tq] if ccw_halo else ccw], inc=16)
        dmw = P.at(DMA_G)
        P.emit("tensor", lambda e, tq=tq, pa=pa: (
            [e.matmul(psA[:, pa + half * 512:pa + (half + 1) * 512],
                      xf[ic][:, tq * 128:(tq + 1) * 128],
                      wq_sb[ic][:, half * 512:(half + 1) * 512],
                      start=(ic == 0), stop=(ic == 3))
             for half in range(2) for ic in range(4)][-1]),
            waits=[v_add[tq - 2]] +
                  ([P.at("vector"), P.at("scalar")] if tq == 0 else []),
            inc=1)
        t_mm = P.at("tensor")
        P.emit("vector", lambda e, tq=tq, pa=pa: e.tensor_tensor(
            xn[tq % 2][:], psA[:, pa:pa + 1024], xo[tq % 2][:], ALU.add),
            waits=[t_mm, dmw, st_x[tq - 2]], inc=1)
        v_add[tq] = P.at("vector")
        P.emit(DMA_S, lambda e, tq=tq: e.dma_start(
            out=xnew_scr.ap()[tq], in_=xn[tq % 2][:]),
            waits=[v_add[tq]], inc=16)
        st_x[tq] = P.at(DMA_S)
    P.barrier()

    for (vec, dst) in ((ln_g, bc_g), (ln_b, bc_b), (b2, bc_b2)):
        P.emit(DMA_S, lambda e, vec=vec: e.dma_start(out=msc[0:1, 0:1024],
                                                     in_=vec.unsqueeze(0)),
               waits=[P.at("tensor")], inc=16)
        P.barrier()
        for half in range(2):
            P.emit("tensor", lambda e, half=half: e.matmul(
                psT[0:128, 0:512], ones_f[0:1, 0:128],
                msc[0:1, half * 512:(half + 1) * 512], start=True, stop=True),
                waits=[P.at("vector")], inc=1)
            P.emit("vector", lambda e, dst=dst, half=half: e.tensor_copy(
                dst[:, half * 512:(half + 1) * 512], psT[0:128, 0:512]),
                waits=[P.at("tensor")], inc=1)
        P.barrier()
    for (vec, dst) in ((b1a, b1aT), (b1g, b1gT)):
        P.emit(DMA_S, lambda e, vec=vec: e.dma_start(
            out=bT_raw[:], in_=vec.rearrange("(c p) -> c p", p=128)),
            waits=[P.at("tensor")], inc=16)
        P.barrier()
        P.emit("tensor", lambda e: e.transpose(psT[0:128, 0:22], bT_raw[:],
                                               id_f[0:22, 0:22]), inc=1)
        P.emit("vector", lambda e, dst=dst: e.tensor_copy(dst[:],
                                                          psT[0:128, 0:22]),
               waits=[P.at("tensor")], inc=1)
        P.barrier()

    h2b2 = [h2, AL.at(qhat_off, [128, 1024], F32)]
    xa2b = [xa, AL.at(t2_off, [128, 1024], F32)]
    pst2 = [psTb[0:128, 0:1024], psS[0].bitcast(BF16)[:, 0:1024]]
    t_sh = {-1: P.at("tensor"), 0: P.at("tensor")}
    t_tr = {-1: P.at("tensor"), 0: P.at("tensor")}
    v_h2cp = {-1: P.at("vector"), 0: P.at("vector")}
    v_cast = {-1: P.at("vector"), 0: P.at("vector")}
    v_xfcp = {-1: P.at("vector"), 0: P.at("vector")}
    s_act = {-1: P.at("scalar"), 0: P.at("scalar")}
    for to in range(1, 17):
        tb = to % 2
        P.emit(DMA_S, lambda e, to=to: e.dma_start(
            out=xn[to % 2][:], in_=xnew_scr.ap()[to]),
            waits=[t_sh[to - 1], v_h2cp[to - 2]] +
                  ([P.at("vector")] if to == 1 else []),
            inc=16)
        if to == 1:
            P.emit(DMA_S, lambda e: e.dma_start(out=xn[0][:],
                                                in_=xnew_scr.ap()[0]),
                   inc=16)
        dmw = P.at(DMA_S)
        po = tb * 512
        P.emit("tensor", lambda e, to=to, po=po: (
            e.matmul(psA[:, po:po + 512], consts["shift_up"][:],
                     xn[to % 2][:, 512:1024], start=True, stop=False),
            e.matmul(psA[:, po:po + 512], consts["last2first"][:],
                     xn[(to - 1) % 2][:, 512:1024], start=False, stop=True))[-1],
            waits=[dmw, v_h2cp[to - 2]], inc=1)
        t_sh[to] = P.at("tensor")
        P.emit("vector", lambda e, to=to, tb=tb: e.tensor_copy(
            h2b2[tb][:, 0:512], xn[to % 2][:, 0:512]),
            waits=[dmw, s_act[to - 2]], inc=1)
        P.emit("vector", lambda e, tb=tb, po=po: e.tensor_copy(
            h2b2[tb][:, 512:1024], psA[:, po:po + 512]),
            waits=[t_sh[to]], inc=1)
        v_h2cp[to] = P.at("vector")
        P.emit("vector", lambda e, tb=tb: e.bn_stats(ln_s[:, 8:14],
                                                     h2b2[tb][:, 0:512]),
               chain=True, inc=1)
        P.emit("vector", lambda e, tb=tb: e.bn_stats(ln_s[:, 14:20],
                                                     h2b2[tb][:, 512:1024]),
               chain=True, inc=1)
        P.emit("vector", lambda e: e.bn_aggr(ln_s[:, 0:2], ln_s[:, 8:20]),
               chain=True, inc=1)
        P.emit("vector", lambda e: e.tensor_scalar_add(ln_s[:, 2:3],
                                                       ln_s[:, 1:2], 1e-5),
               chain=True, inc=1)
        v_ln = P.at("vector")
        P.emit("scalar", lambda e: e.activation(out=ln_s[:, 3:4],
                                                in_=ln_s[:, 2:3], func=AF.Sqrt),
               waits=[v_ln], inc=1)
        s_sq4 = P.at("scalar")
        P.emit("vector", lambda e: e.reciprocal(ln_s[:, 4:5], ln_s[:, 3:4]),
               waits=[s_sq4], inc=1)
        P.emit("vector", lambda e: e.tensor_tensor(ln_s[:, 5:6], ln_s[:, 0:1],
                                                   ln_s[:, 4:5], ALU.mult),
               chain=True, inc=1)
        P.emit("vector", lambda e: e.tensor_scalar_mul(ln_s[:, 5:6],
                                                       ln_s[:, 5:6], -1.0),
               chain=True, inc=1)
        v_fin = P.at("vector")
        P.emit("scalar", lambda e, tb=tb: e.activation(
            out=xa2b[tb][:], in_=h2b2[tb][:], func=AF.Identity,
            bias=ln_s[:, 5:6], scale=ln_s[:, 4:5]),
            waits=[v_fin, v_cast[to - 2]], inc=1)
        s_act[to] = P.at("scalar")
        P.emit("vector", lambda e, tb=tb: e.tensor_tensor(
            xa2b[tb][:], xa2b[tb][:], bc_g[:], ALU.mult),
            waits=[s_act[to]], inc=1)
        P.emit("vector", lambda e, tb=tb: e.tensor_tensor(
            xa2b[tb][:], xa2b[tb][:], bc_b[:], ALU.add), chain=True, inc=1)
        P.emit("vector", lambda e, tb=tb: e.tensor_copy(
            ag2[tb][:], xa2b[tb][:, 0:512]),
            waits=[t_tr[to - 2]], chain=True, inc=1)
        P.emit("vector", lambda e, tb=tb: e.tensor_copy(
            sg2[tb][:], xa2b[tb][:, 512:1024]),
            chain=True, inc=1)
        v_cast[to] = P.at("vector")
        P.emit("tensor", lambda e, tb=tb: (
            [e.transpose(pst2[tb][:, ic * 128:(ic + 1) * 128],
                         (ag2[tb] if ic < 4 else sg2[tb])[:, (ic % 4) * 128:
                                                          (ic % 4) * 128 + 128],
                         id_b) for ic in range(8)][-1]),
            waits=[v_cast[to], v_xfcp[to - 2]], inc=1)
        t_tr[to] = P.at("tensor")
        P.emit("vector", lambda e, to=to, tb=tb: (
            [e.tensor_copy(xf[ic][:, (to - 1) * 128:to * 128],
                           pst2[tb][:, ic * 128:(ic + 1) * 128])
             for ic in range(8)][-1]),
            waits=[t_tr[to]], inc=1)
        v_xfcp[to] = P.at("vector")
    P.barrier()

    w1ab3 = [wq_sb[0][:, 0:1024], wq_sb[1][:, 0:1024], wq_sb[6][:, 0:1024]]
    w1gb3 = [wq_sb[2][:, 0:1024], wq_sb[3][:, 0:1024], wq_sb[7][:, 0:1024]]
    ag2 = [ag_sb, ag_b]
    sg2 = [sqt.bitcast(BF16)[:, 0:512], sg_b]
    for g in range(4):
        gcol = g * 512
        t0 = P.at("tensor")
        wtoks = {}
        ctoks = {-1: t0, -2: t0, -3: t0}
        s_g = {-1: P.at("scalar"), -2: P.at("scalar")}
        v_m = {-1: P.at("vector"), -2: P.at("vector")}
        h1slots = [(psA[:, 0:512], psA[:, 512:1024]),
                   (psA[:, 1024:1536], psA[:, 1536:2048]),
                   (psS[0][:, 0:512], psS[1][:, 0:512]),
                   (psV[0:128, 0:512], psT[0:128, 0:512])]
        s_g.update({-3: s_g[-1], -4: s_g[-1]})
        for j in range(22 + 2):
            if j < 22:
                jb = j % 3
                vq = DMA_G if j % 2 == 0 else DMA_S
                P.emit(vq, lambda e, j=j, jb=jb: e.dma_start(
                    out=w1ab3[jb], in_=w1abf.ap()[:, j * 1024:(j + 1) * 1024]),
                    waits=[ctoks[j - 3], gw_cast if j < 3 else []], inc=16)
                P.emit(vq, lambda e, j=j, jb=jb: e.dma_start(
                    out=w1gb3[jb], in_=w1gbf.ap()[:, j * 1024:(j + 1) * 1024]),
                    waits=[ctoks[j - 3]], inc=16)
                wtoks[j] = P.at(vq)
            if j < 2:
                continue
            jj = j - 2
            jb = jj % 3
            sa, sgp = h1slots[jj % 4]
            gw = wtoks[jj]
            P.emit("tensor", lambda e, jj=jj, jb=jb, gcol=gcol, sa=sa: (
                [e.matmul(sa,
                          w1ab3[jb][:, ic * 128:(ic + 1) * 128],
                          xf[ic][:, gcol:gcol + 512],
                          start=(ic == 0), stop=(ic == 7)) for ic in range(8)][-1]),
                waits=[gw, s_g[jj - 4]] +
                      ([P.at("vector")] if jj < 4 and g == 0 else
                       [P.at("vector")] if jj < 2 else []),
                inc=1)
            P.emit("tensor", lambda e, jj=jj, jb=jb, gcol=gcol, sgp=sgp: (
                [e.matmul(sgp,
                          w1gb3[jb][:, ic * 128:(ic + 1) * 128],
                          xf[ic][:, gcol:gcol + 512],
                          start=(ic == 0), stop=(ic == 7)) for ic in range(8)][-1]),
                inc=1)
            ctoks[jj] = P.at("tensor")
            P.emit("scalar", lambda e, j=jj, sa=sa: e.activation(
                out=ag2[j % 2][:], in_=sa, func=AF.Identity,
                bias=b1aT[:, j:j + 1], scale=1.0),
                waits=[ctoks[jj], v_m[jj - 2]], inc=1)
            P.emit("scalar", lambda e, j=jj, sgp=sgp: e.activation(
                out=sg2[j % 2][:], in_=sgp,
                func=AF.Gelu, bias=b1gT[:, j:j + 1], scale=1.0),
                chain=True, inc=1)
            s_g[jj] = P.at("scalar")
            P.emit("vector", lambda e, j=jj: e.tensor_tensor(
                hgt(j), ag2[j % 2][:], sg2[j % 2][:], ALU.mult),
                waits=[s_g[jj]], inc=1)
            v_m[jj] = P.at("vector")
        for hgp in range(2):
            w2toks = {-1: P.at("tensor"), -2: P.at("tensor")}
            for j in range(22 + 1):
                if j < 22:
                    vq2 = DMA_S if j % 2 == 0 else DMA_G
                    P.emit(vq2, lambda e, j=j: e.dma_start(
                        out=w2buf2[j % 2],
                        in_=w2bf.ap()[j * 128:(j + 1) * 128, :]),
                        waits=[w2toks.get(j - 2, []),
                               gw_cast if j < 2 else []], inc=16)
                    w2toks[j] = P.at(vq2)
                if j == 0:
                    continue
                jj = j - 1
                gw2 = w2toks[jj]
                for ts2 in range(2):
                    ts = hgp * 2 + ts2
                    for half in range(2):
                        P.emit("tensor", lambda e, j=jj, ts=ts, ts2=ts2,
                               half=half: e.matmul(
                                   psA[:, ts2 * 1024 + half * 512:
                                       ts2 * 1024 + (half + 1) * 512],
                                   hgt(j)[:, ts * 128:(ts + 1) * 128],
                                   w2buf2[j % 2][:, half * 512:(half + 1) * 512],
                                   start=(j == 0), stop=(j == 21)),
                               waits=[gw2] + ([P.at("vector"), s_g[21]]
                                              if j == 1 else []),
                               inc=1)
                w2toks[jj] = P.at("tensor")
            for ts2 in range(2):
                ts = hgp * 2 + ts2
                to = g * 4 + ts + 1
                P.emit(DMA_S, lambda e, to=to: e.dma_start(
                    out=xn[to % 2][:], in_=xnew_scr.ap()[to]),
                    waits=[P.at("vector")], inc=16)
                dmw = (P.at(DMA_S),)
                if last:
                    P.emit(DMA_G, lambda e, to=to: e.indirect_dma_start(
                        out=xo[to % 2][:], out_offset=None, in_=y1_src[:, :],
                        in_offset=IndirectOffsetOnAxis(
                            idxown_sb[:, to - 1:to], 0)),
                        waits=[P.at("vector"),
                               ccw_own1[to - 1] if ccw_own1 else ccw1],
                        inc=16)
                dmw = (P.at(DMA_S), P.at(DMA_G))
                P.emit("vector", lambda e, ts2=ts2, to=to: e.tensor_tensor(
                    xa[:], psA[:, ts2 * 1024:(ts2 + 1) * 1024], xn[to % 2][:],
                    ALU.add),
                    waits=[P.at("tensor")] + list(dmw), inc=1)
                P.emit("vector", lambda e: e.tensor_tensor(xa[:], xa[:],
                                                           bc_b2[:], ALU.add),
                       chain=True, inc=1)
                if last:
                    P.emit("vector", lambda e, to=to: e.tensor_tensor(
                        xa[:], xa[:], xo[to % 2][:], ALU.add),
                        chain=True, inc=1)
                    P.emit("vector", lambda e: e.bn_stats(ln_s[:, 8:14], xa[:, 0:512]),
                           waits=[], chain=True, inc=1)
                    P.emit("vector", lambda e: e.bn_stats(ln_s[:, 14:20], xa[:, 512:1024]),
                           chain=True, inc=1)
                    P.emit("vector", lambda e: e.bn_aggr(ln_s[:, 0:2], ln_s[:, 8:20]),
                           chain=True, inc=1)
                    P.emit("vector", lambda e: e.tensor_scalar_add(ln_s[:, 2:3],
                                                                   ln_s[:, 1:2], 1e-5),
                           chain=True, inc=1)
                    P.emit("scalar", lambda e: e.activation(out=ln_s[:, 3:4],
                                                            in_=ln_s[:, 2:3], func=AF.Sqrt),
                           waits=[P.at("vector")], inc=1)
                    P.emit("vector", lambda e: e.reciprocal(ln_s[:, 4:5], ln_s[:, 3:4]),
                           waits=[P.at("scalar")], inc=1)
                    P.emit("vector", lambda e: e.tensor_tensor(ln_s[:, 5:6], ln_s[:, 0:1],
                                                               ln_s[:, 4:5], ALU.mult),
                           chain=True, inc=1)
                    P.emit("vector", lambda e: e.tensor_scalar_mul(ln_s[:, 5:6],
                                                                   ln_s[:, 5:6], -1.0),
                           chain=True, inc=1)
                    P.emit("scalar", lambda e: e.activation(
                        out=y_sb[:], in_=xa[:], func=AF.Identity,
                        bias=ln_s[:, 5:6], scale=ln_s[:, 4:5]),
                        waits=[P.at("vector"), P.at(DMA_S)], inc=1)
                    P.emit(DMA_S, lambda e, to=to: e.dma_start(
                        out=y[(to - 1) * 128:to * 128, :], in_=y_sb[:]),
                        waits=[P.at("scalar")], inc=16)
                else:
                    P.emit(DMA_S, lambda e, to=to: e.dma_start(
                        out=y[(to - 1) * 128:to * 128, :], in_=xa[:]),
                        waits=[P.at("vector")], inc=16)
        if ccp is not None:
            ccp(P, g, P.at(DMA_S))
    P.emit(DMA_S, lambda e: None, waits=[P.at(DMA_S)])




def emit_stage_lstm(nc, P, AL, consts, D):
    NT = 512
    x_src = D["x_src"]
    wnames = ["wf", "uf", "wi", "ui", "wc", "uc", "wo", "uo"]
    wd = {n: D[n] for n in wnames}
    vnames = ["bf_v", "bi_v", "bc_v", "bo_v", "in_c"]
    vd = {n: D[n] for n in vnames}
    y = D["ydst"]
    id_b = consts["id_bf"][:]
    id_f = consts["id_f32"]
    ones_f = consts["ones_f32"]
    ccw = D.get("ccw") or []

    idxl_sb = AL("idxl_sb", [128, 16], I32)
    P.emit("sync", lambda e: e.dma_start(out=idxl_sb, in_=D["idx_lstm"]), inc=16)
    xf = [[AL(f"xf{t}_{i}", [128, NT], BF16) for i in range(8)]
          for t in range(4)]
    wT = [[AL(f"w{n}_{i}", [128, 1024], BF16)
           for i in range(8)] for n in range(6)]
    c_a = [AL(f"ca{i}", [128, NT], F32) for i in range(8)]
    c_b = [AL(f"cb{i}", [128, NT], F32) for i in range(8)]
    fga = AL("fga", [128, NT], F32)
    fgb = AL("fgb", [128, NT], F32)
    fgc = AL("fgc", [128, NT], F32)
    wrow = AL("wrow", [128, 1024], BF16)
    bT = AL("bT", [128, 8 * 5], F32)
    bT_raw = AL("bT_raw", [8, 128], F32)
    bc_bo = AL("bc_bo", [128, 1024], F32)
    msc = AL("msc", [128, 1024], F32)
    xo = [AL(f"xo{i}", [128, 1024], F32) for i in range(2)]
    xa = AL("xa", [128, 1024], F32)
    ln_s = AL("ln_s", [128, 24], F32)
    y_sb = AL("y_sb", [128, 1024], F32)
    sq2 = y_sb
    th = y_sb

    cb16 = [AL(f"cb16_{i}", [128, NT], BF16) for i in range(8)]
    psA = D["psA"]
    psT = D["psT"]
    psTb = psT.bitcast(BF16)

    DMA_G, DMA_S = "gpsimd", "sync"

    def load_wT(dst, name):
        for ic in range(8):
            P.emit(DMA_G, lambda e, dst=dst, ic=ic, name=name: e.dma_start(
                out=dst[ic][:],
                in_=wd[name][ic * 128:(ic + 1) * 128, :]),
                waits=[P.at("tensor")], inc=16)
    for n6 in range(6):
        load_wT(wT[n6], wnames[n6])
    gw_w6 = P.at(DMA_G)

    xs2 = [wrow, msc.bitcast(BF16)[:, 0:1024]]
    pss2 = [psTb[0:128, 0:1024], D["psS"][0].bitcast(BF16)[:, 0:1024]]
    t_tr = {-1: P.at("tensor"), -2: P.at("tensor")}
    v_cp = {-1: P.at("vector"), -2: P.at("vector")}
    for t in range(4):
        for tt in range(NT // 128):
            ti = t * 4 + tt
            tb = ti % 2
            w = [t_tr[ti - 2], ccw]
            if ti == 0:
                w += [P.at("sync"), P.at("vector"), P.at("tensor")]
            P.emit(DMA_G, lambda e, ti=ti, tb=tb: e.indirect_dma_start(
                out=xs2[tb][:], out_offset=None, in_=x_src[:, :],
                in_offset=IndirectOffsetOnAxis(idxl_sb[:, ti:ti + 1], 0)),
                waits=w, inc=16)
            gw = P.at(DMA_G)
            P.emit("tensor", lambda e, tb=tb: (
                [e.transpose(pss2[tb][:, ic * 128:(ic + 1) * 128],
                             xs2[tb][:, ic * 128:(ic + 1) * 128], id_b)
                 for ic in range(8)][-1]),
                waits=[gw, v_cp[ti - 2]], inc=1)
            t_tr[ti] = P.at("tensor")
            P.emit("vector", lambda e, t=t, tt=tt, tb=tb: (
                [e.tensor_copy(xf[t][ic][:, tt * 128:(tt + 1) * 128],
                               pss2[tb][:, ic * 128:(ic + 1) * 128])
                 for ic in range(8)][-1]),
                waits=[t_tr[ti]], inc=1)
            v_cp[ti] = P.at("vector")

    for vi, n in enumerate(vnames):
        P.emit(DMA_S, lambda e, n=n: e.dma_start(
            out=bT_raw[:], in_=vd[n].rearrange("(c p) -> c p", p=128)),
            waits=[P.at("tensor")], inc=16)
        P.barrier()
        P.emit("tensor", lambda e: e.transpose(psT[0:128, 0:8], bT_raw[:],
                                               id_f[0:8, 0:8]), inc=1)
        P.emit("vector", lambda e, vi=vi: e.tensor_copy(
            bT[:, vi * 8:(vi + 1) * 8], psT[0:128, 0:8]),
            waits=[P.at("tensor")], inc=1)
        P.barrier()
    P.emit(DMA_S, lambda e: e.dma_start(out=msc[0:1, 0:1024],
                                        in_=vd["bo_v"].unsqueeze(0)),
           waits=[P.at("tensor")], inc=16)
    P.barrier()
    for half in range(2):
        P.emit("tensor", lambda e, half=half: e.matmul(
            psT[0:128, 0:512], ones_f[0:1, 0:128],
            msc[0:1, half * 512:(half + 1) * 512], start=True, stop=True),
            waits=[P.at("vector")], inc=1)
        P.emit("vector", lambda e, half=half: e.tensor_copy(
            bc_bo[:, half * 512:(half + 1) * 512], psT[0:128, 0:512]),
            waits=[P.at("tensor")], inc=1)
    P.barrier()

    P.emit("tensor", lambda e: None, waits=[gw_w6])
    P.barrier()

    for ic in range(8):
        P.emit("vector", lambda e, ic=ic: e.memset(c_a[ic][:], 0.0), inc=1)
        P.emit("vector", lambda e, ic=ic: e.tensor_scalar_add(
            c_a[ic][:], c_a[ic][:], bT[:, 4 * 8 + ic:4 * 8 + ic + 1]),
            waits=[P.at("vector")], inc=1)
    P.barrier()

    s_bank = {0: P.at("scalar"), 1: P.at("scalar"), 2: P.at("scalar")}
    v_cup = P.at("vector")
    for t in range(4):
        c_src = c_a if t % 2 == 0 else c_b
        c_dst = c_b if t % 2 == 0 else c_a
        for ic in range(8):
            P.emit("vector", lambda e, ic=ic, c_src=c_src: e.tensor_copy(
                cb16[ic][:], c_src[ic][:]),
                waits=[P.at("tensor")] if ic == 0 else [], chain=True, inc=1)
        v_cb = P.at("vector")
        for oc in range(8):
            for gi, (wn, un, bvi, fn, dst) in enumerate((
                    (0, 1, 0, AF.Sigmoid, fga),
                    (2, 3, 1, AF.Sigmoid, fgb),
                    (4, 5, 2, AF.Tanh, fgc))):
                w = [s_bank[gi]] + ([v_cb] if oc == 0 and gi == 0 else [])
                P.emit("tensor", lambda e, t=t, oc=oc, gi=gi, wn=wn, un=un: (
                    [e.matmul(psA[:, gi * 512:(gi + 1) * 512],
                              wT[wn][ic][:, oc * 128:(oc + 1) * 128],
                              xf[t][ic][:], start=(ic == 0), stop=False)
                     for ic in range(8)] +
                    [e.matmul(psA[:, gi * 512:(gi + 1) * 512],
                              wT[un][ic][:, oc * 128:(oc + 1) * 128],
                              cb16[ic][:], start=False, stop=(ic == 7))
                     for ic in range(8)])[-1],
                    waits=w, inc=1)
                t_mm = P.at("tensor")
                P.emit("scalar", lambda e, gi=gi, oc=oc, bvi=bvi, fn=fn,
                       dst=dst: e.activation(
                           out=dst[:], in_=psA[:, gi * 512:(gi + 1) * 512],
                           func=fn, bias=bT[:, bvi * 8 + oc:bvi * 8 + oc + 1],
                           scale=1.0),
                       waits=[t_mm, v_cup], inc=1)
                s_bank[gi] = P.at("scalar")
            s3 = P.at("scalar")
            P.emit("vector", lambda e, oc=oc, c_src=c_src, c_dst=c_dst:
                   e.tensor_tensor(c_dst[oc][:], fga[:], c_src[oc][:],
                                   ALU.mult),
                   waits=[s3], inc=1)
            P.emit("vector", lambda e: e.tensor_tensor(fgb[:], fgb[:], fgc[:],
                                                       ALU.mult),
                   chain=True, inc=1)
            P.emit("vector", lambda e, oc=oc, c_dst=c_dst: e.tensor_tensor(
                c_dst[oc][:], c_dst[oc][:], fgb[:], ALU.add),
                chain=True, inc=1)
            v_cup = P.at("vector")
    c_fin = c_a if 4 % 2 == 0 else c_b

    load_wT(wT[0], "wo")
    load_wT(wT[1], "uo")
    P.barrier()

    for ic in range(8):
        P.emit("vector", lambda e, ic=ic: e.tensor_copy(cb16[ic][:],
                                                        c_fin[ic][:]),
               waits=[P.at("tensor")] if ic == 0 else [], inc=1)
    P.barrier()
    v_xa = {-1: P.at("vector"), -2: P.at("vector")}
    v_xo = {-1: P.at("vector"), -2: P.at("vector")}
    s_y = {-1: P.at("scalar")}
    st = {-1: P.at(DMA_S)}
    for t in range(4):
        for tt in range(NT // 128):
            ti = t * 4 + tt
            pa = (ti % 2) * 1024
            P.emit(DMA_G, lambda e, ti=ti: e.indirect_dma_start(
                out=xo[ti % 2][:], out_offset=None, in_=x_src[:, :],
                in_offset=IndirectOffsetOnAxis(idxl_sb[:, ti:ti + 1], 0)),
                waits=[v_xo[ti - 2], ccw] +
                      ([P.at("vector")] if ti < 2 else []),
                inc=16)
            dmw = P.at(DMA_G)
            P.emit("tensor", lambda e, t=t, tt=tt, pa=pa: (
                [e.matmul(psA[:, pa + half * 512:pa + (half + 1) * 512],
                          xf[t][ic][:, tt * 128:(tt + 1) * 128],
                          wT[0][ic][:, half * 512:(half + 1) * 512],
                          start=(ic == 0), stop=False)
                 for half in range(2) for ic in range(8)] +
                [e.matmul(psA[:, pa + half * 512:pa + (half + 1) * 512],
                          cb16[ic][:, tt * 128:(tt + 1) * 128],
                          wT[1][ic][:, half * 512:(half + 1) * 512],
                          start=False, stop=(ic == 7))
                 for half in range(2) for ic in range(8)])[-1],
                waits=[v_xa[ti - 2]] +
                      ([P.at("vector"), P.at("scalar")] if ti == 0 else []),
                inc=1)
            t_mm = P.at("tensor")
            P.emit("vector", lambda e, pa=pa: e.tensor_tensor(
                th[:], psA[:, pa:pa + 1024], bc_bo[:], ALU.add),
                waits=[t_mm, s_y[ti - 1], st[ti - 1]], inc=1)
            v_xa[ti] = P.at("vector")
            P.emit("scalar", lambda e: e.activation(out=th[:], in_=th[:],
                                                    func=AF.Tanh),
                   waits=[v_xa[ti]], inc=1)
            s_th = P.at("scalar")
            P.emit("vector", lambda e, ti=ti: e.tensor_tensor(
                xa[:], th[:], xo[ti % 2][:], ALU.add),
                waits=[s_th, dmw], inc=1)
            v_xo[ti] = P.at("vector")
            P.emit("vector", lambda e: e.bn_stats(ln_s[:, 8:14], xa[:, 0:512]),
                   chain=True, inc=1)
            P.emit("vector", lambda e: e.bn_stats(ln_s[:, 14:20],
                                                  xa[:, 512:1024]),
                   chain=True, inc=1)
            P.emit("vector", lambda e: e.bn_aggr(ln_s[:, 0:2], ln_s[:, 8:20]),
                   chain=True, inc=1)
            P.emit("vector", lambda e: e.tensor_scalar_add(ln_s[:, 2:3],
                                                           ln_s[:, 1:2], 1e-5),
                   chain=True, inc=1)
            v_ln = P.at("vector")
            P.emit("scalar", lambda e: e.activation(
                out=ln_s[:, 3:4], in_=ln_s[:, 2:3], func=AF.Sqrt),
                waits=[v_ln], inc=1)
            s_sq3 = P.at("scalar")
            P.emit("vector", lambda e: e.reciprocal(ln_s[:, 4:5], ln_s[:, 3:4]),
                   waits=[s_sq3], inc=1)
            P.emit("vector", lambda e: e.tensor_tensor(
                ln_s[:, 5:6], ln_s[:, 0:1], ln_s[:, 4:5], ALU.mult),
                chain=True, inc=1)
            P.emit("vector", lambda e: e.tensor_scalar_mul(
                ln_s[:, 5:6], ln_s[:, 5:6], -1.0), chain=True, inc=1)
            v_fin = P.at("vector")
            P.emit("scalar", lambda e: e.activation(
                out=y_sb[:], in_=xa[:], func=AF.Identity, bias=ln_s[:, 5:6],
                scale=ln_s[:, 4:5]),
                waits=[v_fin], inc=1)
            s_y[ti] = P.at("scalar")
            P.emit(DMA_S, lambda e, t=t, tt=tt: e.dma_start(
                out=y[t, tt * 128:(tt + 1) * 128, :], in_=y_sb[:]),
                waits=[s_y[ti]], inc=16)
            st[ti] = P.at(DMA_S)
    P.emit(DMA_S, lambda e: None, waits=[P.at(DMA_S)])




def host_stage1_inputs(x, w_attn, b_attn, w_proj, b_proj, l1=False):
    in_maps = []
    for core in range(8):
        b, half = core // 2, core % 2
        xb = np.ascontiguousarray(x[b])
        own = xb[half * 2048:(half + 1) * 2048]
        other = xb[(1 - half) * 2048:(2 - half) * 2048]
        x_kv = np.ascontiguousarray(np.concatenate([other, own], axis=0))
        kvpos = np.concatenate([np.arange(2048) + (1 - half) * 2048,
                                np.arange(2048) + half * 2048])
        kdead = np.zeros((4, 32, 128), dtype=np.float32)
        dmask = np.zeros((4, 8, 128, 512), dtype=BF)
        qv = np.arange(512)
        kv128 = np.arange(128)
        for j in range(4):
            qpos = half * 2048 + j * 512 + qv
            for kt in range(32):
                kp = kvpos[kt * 128:(kt + 1) * 128]
                kdead[j, kt] = np.where(kp > qpos.max(), NEG, 0.0)
            slots = list(range(4 * j, 4 * j + 4)) + \
                    list(range(16 + 4 * j, 16 + 4 * j + 4))
            for si, kt in enumerate(slots):
                kp = kvpos[kt * 128:(kt + 1) * 128]
                m = np.where(qpos[None, :] >= kp[:, None], 0.0, NEG)
                dmask[j, si] = m.astype(BF)
        in_maps.append({
            "x_own": own.copy(),
            "x_kv": x_kv,
            "w_attn": np.ascontiguousarray(w_attn),
            "b_attn": np.ascontiguousarray(b_attn),
            "w_proj": np.ascontiguousarray(w_proj),
            "b_proj": np.ascontiguousarray(b_proj),
            "kdead": kdead,
            "dmask": dmask,
        })
    return in_maps




def _rotary_tables(pos, n=4096):
    half_idx = np.arange(0, 64, 2, dtype=np.float64)
    inv_freq = 1.0 / 10000.0 ** (half_idx / 64.0)
    t = pos.astype(np.float64)
    freqs = t[:, None] * inv_freq[None, :]
    base = (half_idx + 0.4 * 64) / (1.4 * 64)
    power = (t - n // 2) / 256.0
    scale = base[None, :] ** power[:, None]
    cos = np.cos(freqs); sin = np.sin(freqs)
    def cat(v):
        v64 = np.concatenate([v, v], axis=1).T
        return np.concatenate([v64, v64], axis=0).astype(np.float32)
    aq = cat(cos * scale); bq = cat(sin * scale)
    ak = cat(cos / scale); bk = cat(sin / scale)
    return aq, bq, ak, bk


def host_stage_ffn_inputs(x_full, wqkv_l, qs_l, ks_l, wo_l, lng_l, lnb_l,
                          w1_l, b1_l, w2_l, b2_l, y1=None):
    HID = 2730
    HIDP = 2816
    w1a = np.zeros((1024, HIDP), dtype=np.float32)
    w1g = np.zeros((1024, HIDP), dtype=np.float32)
    w1a[:, :HID] = w1_l[:, :HID]; w1g[:, :HID] = w1_l[:, HID:]
    b1a = np.zeros(HIDP, dtype=np.float32); b1g = np.zeros(HIDP, dtype=np.float32)
    b1a[:HID] = b1_l[:HID]; b1g[:HID] = b1_l[HID:]
    w2p = np.zeros((HIDP, 1024), dtype=np.float32)
    w2p[:HID] = w2_l
    qs_blk = np.zeros((2, 128), dtype=np.float32)
    qs_blk[0, 0:64] = qs_l; qs_blk[1, 64:128] = qs_l
    ks_blk = np.zeros((2, 128), dtype=np.float32)
    ks_blk[0, 0:64] = ks_l; ks_blk[1, 64:128] = ks_l
    in_maps = []
    for core in range(8):
        b, half = core // 2, core % 2
        start = half * 2048
        xh = np.zeros((2816, 1024), dtype=np.float32)
        lo = start - 768
        src_lo = max(lo, 0)
        xh[src_lo - lo:] = x_full[b, src_lo:start + 2048]
        pos = np.clip(np.arange(lo, start + 2048), 0, 4095)
        aq, bq, ak, bk = _rotary_tables(pos)
        kd = np.zeros((21, 128), dtype=np.float32)
        gpos = np.arange(lo + 128, lo + 128 + 2688)
        kd_flat = np.where(gpos < 0, NEG, 0.0).astype(np.float32)
        kd[:] = kd_flat.reshape(21, 128)
        m = {
            "x_halo": xh, "wqkv": np.ascontiguousarray(wqkv_l),
            "qs_blk": qs_blk, "ks_blk": ks_blk,
            "wo": np.ascontiguousarray(wo_l),
            "ln_g": np.ascontiguousarray(lng_l),
            "ln_b": np.ascontiguousarray(lnb_l),
            "w1a": w1a, "w1g": w1g, "b1a": b1a, "b1g": b1g,
            "w2p": w2p, "b2": np.ascontiguousarray(b2_l),
            "kdead": kd,
            "aq": aq, "bq": bq, "ak": ak, "bk": bk,
        }
        if y1 is not None:
            m["y1_own"] = np.ascontiguousarray(y1[b, start:start + 2048])
        in_maps.append(m)
    return in_maps


def host_stage_lstm_inputs(x_full, inp):
    in_maps = []
    for core in range(8):
        sl = slice(core * 512, (core + 1) * 512)
        m = {"x_own": np.ascontiguousarray(x_full[:, sl, :])}
        for n, k in (("wf", "L1_ffn_wf"), ("uf", "L1_ffn_uf"),
                     ("wi", "L1_ffn_wi"), ("ui", "L1_ffn_ui"),
                     ("wc", "L1_ffn_wc"), ("uc", "L1_ffn_uc"),
                     ("wo", "L1_ffn_wo"), ("uo", "L1_ffn_uo")):
            m[n] = np.ascontiguousarray(inp[k])
        for n, k in (("bf_v", "L1_ffn_bf"), ("bi_v", "L1_ffn_bi"),
                     ("bc_v", "L1_ffn_bc"), ("bo_v", "L1_ffn_bo"),
                     ("in_c", "L1_ffn_in_c")):
            m[n] = np.ascontiguousarray(inp[k])
        in_maps.append(m)
    return in_maps





NFULL = 16384
ZROW = 16384

ABLK = [0, 3, 4, 7, 8, 11, 12, 15]
BBLK = [1, 2, 5, 6, 9, 10, 13, 14]
HALFOF = np.zeros(16, dtype=np.int64)
ORDIX = np.zeros(16, dtype=np.int64)
for _i, _b in enumerate(ABLK):
    HALFOF[_b] = 0
    ORDIX[_b] = _i
for _i, _b in enumerate(BBLK):
    HALFOF[_b] = 1
    ORDIX[_b] = _i


def piece_of_zig(pos):
    beta = pos // 256
    pr = ORDIX[beta] * 256 + pos % 256
    return int(pr // 512)


def piece_of_contig(pos):
    return int((pos % 2048) // 512)


def piece_reqs():
    def halo_ps(t, zig):
        ps = set()
        for half in (0, 1):
            for i in range(128):
                p = half * 2048 - 768 + t * 128 + i
                if 0 <= p < 4096:
                    ps.add(piece_of_zig(p) if zig else piece_of_contig(p))
        return sorted(ps)

    def kv_ps(t):
        ps = set()
        for half in (0, 1):
            own = ABLK if half == 0 else BBLK
            kvloc = []
            for g in range(4):
                grp = list(range(4 * g, 4 * g + 4))
                for x_ in ([x for x in grp if x not in own] +
                           [x for x in grp if x in own]):
                    kvloc.extend(range(x_ * 256, x_ * 256 + 256))
            for p in kvloc[t * 128:(t + 1) * 128]:
                ps.add(piece_of_contig(p))
        return sorted(ps)

    def attnown_ps(t):
        ps = set()
        for half in (0, 1):
            own = ABLK if half == 0 else BBLK
            for i in range(128):
                ps.add(piece_of_contig(own[t // 2] * 256 + (t % 2) * 128 + i))
        return sorted(ps)

    def ffnown_ps(t):
        ps = set()
        for half in (0, 1):
            for i in range(128):
                ps.add(piece_of_zig(half * 2048 + t * 128 + i))
        return sorted(ps)

    def lstm_ps(ti):
        tt = ti % 4
        ps = set()
        for core in range(8):
            for i in range(128):
                ps.add(piece_of_zig(core * 512 + tt * 128 + i))
        return sorted(ps)

    return {
        "halo0": [halo_ps(t, True) for t in range(22)],
        "halo1": [halo_ps(t, False) for t in range(22)],
        "kv_p": [kv_ps(t) for t in range(32)],
        "own_p": [attnown_ps(t) for t in range(16)],
        "own_f": [ffnown_ps(t) for t in range(16)],
        "lstm": [lstm_ps(ti) for ti in range(16)],
    }


def build_fused():
    nc = bass.Bass("TRN2", target_bir_lowering=False, num_devices=8)

    x0 = nc.dram_tensor("x0", [NFULL, 1024], F32, kind="ExternalInput").ap()
    ein = {}

    def EIN(name, shape, dt=F32):
        ein[name] = nc.dram_tensor(name, shape, dt, kind="ExternalInput").ap()
        return ein[name]

    EIN("idx_kv", [128, 32], I32)
    EIN("idx_own", [128, 16], I32)
    EIN("idx_kv_p", [128, 32], I32)
    EIN("idx_own_p", [128, 16], I32)
    EIN("idx_halo0", [128, 22], I32)
    EIN("idx_halo1", [128, 22], I32)
    EIN("idx_own_f", [128, 16], I32)
    EIN("idx_lstm", [128, 16], I32)
    EIN("s1_w_attn", [1024, 3072])
    EIN("s1_b_attn", [3072])
    EIN("s1_w_proj", [1024, 1024])
    EIN("s1_b_proj", [1024])
    EIN("kdead_attn", [4, 32, 128])
    EIN("dmask_attn", [4, 8, 128, 512], BF16)
    for pfx in ("f0", "f1"):
        EIN(pfx + "_wqkv", [1024, 1536])
        EIN(pfx + "_qs_blk", [2, 128])
        EIN(pfx + "_ks_blk", [2, 128])
        EIN(pfx + "_wo", [512, 1024])
        EIN(pfx + "_ln_g", [1024])
        EIN(pfx + "_ln_b", [1024])
        EIN(pfx + "_w1a", [128, 22528])
        EIN(pfx + "_w1g", [128, 22528])
        EIN(pfx + "_b1a", [2816])
        EIN(pfx + "_b1g", [2816])
        EIN(pfx + "_w2p", [2816, 1024])
        EIN(pfx + "_b2", [1024])
    EIN("f_kdead", [21, 128])
    for tn in ("aq", "bq", "ak", "bk"):
        EIN("f_" + tn, [128, 2816])
    EIN("s4_w_attn", [1024, 3072])
    EIN("s4_b_attn", [3072])
    EIN("s4_w_proj", [1024, 1024])
    EIN("s4_b_proj", [1024])
    for n in ("wf", "uf", "wi", "ui", "wc", "uc", "wo", "uo"):
        EIN("s5_" + n, [1024, 1024])
    for n in ("bf_v", "bi_v", "bc_v", "bo_v", "in_c"):
        EIN("s5_" + n, [1024])
    y_out = nc.dram_tensor("y", [4, 512, 1024], F32, kind="ExternalOutput").ap()

    parts = [nc.dram_tensor(f"part{i}", [2048, 1024], F32, kind="Internal")
             for i in range(4)]
    fulls = [nc.dram_tensor(f"full{i}", [NFULL + 1, 1024], F32, kind="Internal")
             for i in range(4)]
    k_s1 = nc.dram_tensor("k_s1", [8, 128, 4096], BF16, kind="Internal")
    k_s4 = nc.dram_tensor("k_s4", [8, 128, 4096], BF16, kind="Internal")
    v_s4 = nc.dram_tensor("v_s4", [32, 128, 1024], BF16, kind="Internal")
    xnew_f = [nc.dram_tensor(f"xnew_f{i}", [17, 128, 1024], F32,
                             kind="Internal") for i in range(2)]
    w1abf = [nc.dram_tensor(f"w1abf{i}", [128, 22528], BF16, kind="Internal")
             for i in range(2)]
    w1gbf = [nc.dram_tensor(f"w1gbf{i}", [128, 22528], BF16, kind="Internal")
             for i in range(2)]
    w2bf = [nc.dram_tensor(f"w2bf{i}", [2816, 1024], BF16, kind="Internal")
            for i in range(2)]

    psA = nc.alloc_psum_tensor("psA", [128, 2048], F32)
    psS = [nc.alloc_psum_tensor(f"psS{i}", [128, 512], F32) for i in range(2)]
    psV = nc.alloc_psum_tensor("psV", [128, 512], F32)
    psT = nc.alloc_psum_tensor("psT", [128, 512], F32)
    PS = {"psA": psA, "psS": psS, "psV": psV, "psT": psT}

    AL = Arena(nc)
    consts = {}
    P0 = Prog(nc)
    for name, arr in all_np_consts().items():
        dt = BF16 if arr.dtype == BF else F32
        t = nc.alloc_sbuf_tensor("c_" + name, list(arr.shape), dt)
        dr = nc.inline_tensor(np.ascontiguousarray(arr), "ic_" + name)
        P0.emit("sync", lambda e, t=t, dr=dr: e.dma_start(out=t[:],
                                                          in_=dr.ap()),
                inc=16)
        consts[name] = t
    zrow = AL("zrow", [1, 1024], F32)
    AL.reset()
    P0.emit("vector", lambda e: e.memset(zrow[:], 0.0), inc=1)
    P0.barrier()
    for fl in fulls:
        P0.emit("sync", lambda e, fl=fl: e.dma_start(
            out=fl.ap()[ZROW:ZROW + 1, :], in_=zrow[:]), inc=16)
    P0.barrier()
    P0.build()

    _ccn = [0]

    def mk_ccp(part_t, full_t):
        _ccn[0] += 1
        sems = [nc.alloc_semaphore(f"ccp{_ccn[0]}_{g}") for g in range(4)]

        def emit_piece(P, g, wait_tok, part_t=part_t, full_t=full_t,
                       sems=sems):
            def f(e, g=g):
                e.collective_compute(
                    "AllGather", ALU.bypass, replica_groups=[list(range(8))],
                    ins=[part_t.ap()[g * 512:(g + 1) * 512, :].opt()],
                    outs=[full_t.ap()[g * 4096:(g + 1) * 4096, :].opt()]
                ).then_inc(sems[g], 1)
            P.emit("gpsimd", f, waits=[wait_tok], inc=0)
        return emit_piece, [(s, 1) for s in sems]

    AL.reset()
    P1 = Prog(nc)
    ccp = [None] * 4
    cc_tok = [None] * 4
    for i in range(4):
        ccp[i], cc_tok[i] = mk_ccp(parts[i], fulls[i])
    PR = piece_reqs()

    def ptoks(tok4, reqlist):
        return [[tok4[p] for p in ps] for ps in reqlist]
    D1 = {"x_src": x0, "idx_kv": ein["idx_kv"], "idx_own": ein["idx_own"],
          "w_attn": ein["s1_w_attn"], "b_attn": ein["s1_b_attn"],
          "w_proj": ein["s1_w_proj"], "b_proj": ein["s1_b_proj"],
          "kdead": ein["kdead_attn"], "dmask": ein["dmask_attn"],
          "ydst": parts[0].ap(), "k_dram": k_s1, "ccp": ccp[0]}
    D1.update(PS)
    emit_stage_attn(nc, P1, AL, consts, D1, l1=False)
    P1.barrier()
    P1.build()

    for li in range(2):
        AL.reset()
        Pf = Prog(nc)
        pfx = f"f{li}"
        Df = {"x_src": fulls[0].ap() if li == 0 else fulls[1].ap(),
              "idx_halo": ein[f"idx_halo{li}"], "idx_own": ein["idx_own_f"],
              "wqkv": ein[pfx + "_wqkv"], "qs_blk": ein[pfx + "_qs_blk"],
              "ks_blk": ein[pfx + "_ks_blk"], "wo": ein[pfx + "_wo"],
              "ln_g": ein[pfx + "_ln_g"], "ln_b": ein[pfx + "_ln_b"],
              "w1a": ein[pfx + "_w1a"], "w1g": ein[pfx + "_w1g"],
              "b1a": ein[pfx + "_b1a"], "b1g": ein[pfx + "_b1g"],
              "w2p": ein[pfx + "_w2p"], "b2": ein[pfx + "_b2"],
              "kdead": ein["f_kdead"],
              "aq": ein["f_aq"], "bq": ein["f_bq"], "ak": ein["f_ak"],
              "bk": ein["f_bk"],
              "y1_src": fulls[0].ap(),
              "ydst": parts[1 + li].ap(), "xnew_scr": xnew_f[li],
              "ccw": cc_tok[li], "ccw_y1": cc_tok[0], "ccp": ccp[1 + li],
              "ccw_halo": ptoks(cc_tok[li], PR["halo0" if li == 0 else "halo1"]),
              "ccw_own1": ptoks(cc_tok[0], PR["own_f"]),
              "w1abf": w1abf[li], "w1gbf": w1gbf[li], "w2bf": w2bf[li]}
        Df.update(PS)
        emit_stage_ffn(nc, Pf, AL, consts, Df, last=(li == 1))
        Pf.barrier()
        Pf.build()

    AL.reset()
    P4 = Prog(nc)
    D4 = {"x_src": fulls[2].ap(), "idx_kv": ein["idx_kv_p"],
          "idx_own": ein["idx_own_p"],
          "w_attn": ein["s4_w_attn"], "b_attn": ein["s4_b_attn"],
          "w_proj": ein["s4_w_proj"], "b_proj": ein["s4_b_proj"],
          "kdead": ein["kdead_attn"], "dmask": ein["dmask_attn"],
          "ydst": parts[3].ap(), "k_dram": k_s4, "v_dram": v_s4,
          "ccw": cc_tok[2], "ccp": ccp[3],
          "ccw_kv": ptoks(cc_tok[2], PR["kv_p"]),
          "ccw_own": ptoks(cc_tok[2], PR["own_p"])}
    D4.update(PS)
    emit_stage_attn(nc, P4, AL, consts, D4, l1=True)
    P4.barrier()
    P4.build()

    AL.reset()
    P5 = Prog(nc)
    D5 = {"x_src": fulls[3].ap(), "idx_lstm": ein["idx_lstm"], "ydst": y_out,
          "ccw": cc_tok[3]}
    for n in ("wf", "uf", "wi", "ui", "wc", "uc", "wo", "uo",
              "bf_v", "bi_v", "bc_v", "bo_v", "in_c"):
        D5[n] = ein["s5_" + n]
    D5.update(PS)
    emit_stage_lstm(nc, P5, AL, consts, D5)
    P5.build()
    return nc


def host_fused_inputs(inputs):
    x0 = np.asarray(inputs["L0_attn_hidden_states"],
                    dtype=np.float32).reshape(NFULL, 1024)
    HID = 2730
    in_maps = []
    shared = {
        "s1_w_attn": np.ascontiguousarray(
            np.asarray(inputs["L0_attn_c_attn_kernel"]).T),
        "s1_b_attn": np.ascontiguousarray(inputs["L0_attn_c_attn_bias"]),
        "s1_w_proj": np.ascontiguousarray(
            np.asarray(inputs["L0_attn_c_proj_kernel"]).T),
        "s1_b_proj": np.ascontiguousarray(inputs["L0_attn_c_proj_bias"]),
        "s4_w_attn": np.ascontiguousarray(inputs["L1_attn_w_attn"]),
        "s4_b_attn": np.ascontiguousarray(inputs["L1_attn_b_attn"]),
        "s4_w_proj": np.ascontiguousarray(inputs["L1_attn_w_proj"]),
        "s4_b_proj": np.ascontiguousarray(inputs["L1_attn_b_proj"]),
    }
    for li in range(2):
        pfx = f"f{li}"
        w1_l = inputs["L0_ffn_w1"][li]
        b1_l = inputs["L0_ffn_b1"][li]
        w1a = np.zeros((1024, 2816), dtype=np.float32)
        w1g = np.zeros((1024, 2816), dtype=np.float32)
        w1a[:, :HID] = w1_l[:, :HID]
        w1g[:, :HID] = w1_l[:, HID:]
        b1a = np.zeros(2816, dtype=np.float32)
        b1g = np.zeros(2816, dtype=np.float32)
        b1a[:HID] = b1_l[:HID]
        b1g[:HID] = b1_l[HID:]
        w2p = np.zeros((2816, 1024), dtype=np.float32)
        w2p[:HID] = inputs["L0_ffn_w2"][li]
        qs_blk = np.zeros((2, 128), dtype=np.float32)
        qs_blk[0, 0:64] = inputs["L0_ffn_q_scale"][li]
        qs_blk[1, 64:128] = inputs["L0_ffn_q_scale"][li]
        ks_blk = np.zeros((2, 128), dtype=np.float32)
        ks_blk[0, 0:64] = inputs["L0_ffn_k_scale"][li]
        ks_blk[1, 64:128] = inputs["L0_ffn_k_scale"][li]
        def _w1rel(w):
            return np.ascontiguousarray(
                w.reshape(8, 128, 22, 128).transpose(1, 2, 0, 3)
                 .reshape(128, 22528))
        shared.update({
            pfx + "_wqkv": np.ascontiguousarray(inputs["L0_ffn_wqkv"][li]),
            pfx + "_qs_blk": qs_blk, pfx + "_ks_blk": ks_blk,
            pfx + "_wo": np.ascontiguousarray(inputs["L0_ffn_wo"][li]),
            pfx + "_ln_g": np.ascontiguousarray(inputs["L0_ffn_ln_g"][li]),
            pfx + "_ln_b": np.ascontiguousarray(inputs["L0_ffn_ln_b"][li]),
            pfx + "_w1a": _w1rel(w1a), pfx + "_w1g": _w1rel(w1g),
            pfx + "_b1a": b1a, pfx + "_b1g": b1g,
            pfx + "_w2p": w2p,
            pfx + "_b2": np.ascontiguousarray(inputs["L0_ffn_b2"][li]),
        })
    for n in ("wf", "uf", "wi", "ui", "wc", "uc", "wo", "uo"):
        shared["s5_" + n] = np.ascontiguousarray(inputs["L1_ffn_" + n[:1] +
                                                        n[1:]])
    shared["s5_wf"] = np.ascontiguousarray(np.asarray(inputs["L1_ffn_wf"]).T)
    shared["s5_uf"] = np.ascontiguousarray(np.asarray(inputs["L1_ffn_uf"]).T)
    shared["s5_wi"] = np.ascontiguousarray(np.asarray(inputs["L1_ffn_wi"]).T)
    shared["s5_ui"] = np.ascontiguousarray(np.asarray(inputs["L1_ffn_ui"]).T)
    shared["s5_wc"] = np.ascontiguousarray(np.asarray(inputs["L1_ffn_wc"]).T)
    shared["s5_uc"] = np.ascontiguousarray(np.asarray(inputs["L1_ffn_uc"]).T)
    shared["s5_wo"] = np.ascontiguousarray(np.asarray(inputs["L1_ffn_wo"]).T)
    shared["s5_uo"] = np.ascontiguousarray(np.asarray(inputs["L1_ffn_uo"]).T)
    shared["s5_bf_v"] = np.ascontiguousarray(inputs["L1_ffn_bf"])
    shared["s5_bi_v"] = np.ascontiguousarray(inputs["L1_ffn_bi"])
    shared["s5_bc_v"] = np.ascontiguousarray(inputs["L1_ffn_bc"])
    shared["s5_bo_v"] = np.ascontiguousarray(inputs["L1_ffn_bo"])
    shared["s5_in_c"] = np.ascontiguousarray(inputs["L1_ffn_in_c"])

    ii = np.arange(128)
    halfof, ordix = HALFOF, ORDIX

    def _pmap_z(ix):
        ix = np.asarray(ix)
        bb = ix // 4096
        pos = ix % 4096
        beta = np.clip(pos // 256, 0, 15)
        corez = bb * 2 + halfof[beta]
        pr = ordix[beta] * 256 + pos % 256
        pm = (pr // 512) * 4096 + corez * 512 + pr % 512
        return np.where(ix >= NFULL, ix, pm).astype(np.int32)

    for core in range(8):
        b, half = core // 2, core % 2
        start = half * 2048
        base = b * 4096
        own_blocks = ABLK if half == 0 else BBLK
        kvloc_l = []
        for g in range(4):
            grp = list(range(4 * g, 4 * g + 4))
            oth = [x_ for x_ in grp if x_ not in own_blocks]
            owb = [x_ for x_ in grp if x_ in own_blocks]
            for x_ in oth + owb:
                kvloc_l.append(np.arange(256) + x_ * 256)
        kvloc = np.concatenate(kvloc_l)
        kvpos = kvloc + base
        idx_kv = np.zeros((128, 32), dtype=np.int32)
        for kt in range(32):
            idx_kv[:, kt] = kvpos[kt * 128:(kt + 1) * 128]
        idx_own = np.zeros((128, 16), dtype=np.int32)
        for t in range(16):
            idx_own[:, t] = (base + own_blocks[t // 2] * 256 +
                             (t % 2) * 128 + ii)
        idx_halo = np.zeros((128, 22), dtype=np.int32)
        for t in range(22):
            loc = start - 768 + t * 128 + ii
            idx_halo[:, t] = np.where(loc < 0, ZROW, base + loc)
        idx_lstm = np.zeros((128, 16), dtype=np.int32)
        for t in range(4):
            for tt in range(4):
                idx_lstm[:, t * 4 + tt] = (t * 4096 + core * 512 +
                                           tt * 128 + ii)

        def _pmap(ix):
            ix = np.asarray(ix)
            pm = ((ix % 2048) // 512) * 4096 + (ix // 2048) * 512 + ix % 512
            return np.where(ix >= NFULL, ix, pm).astype(np.int32)
        kdead = np.zeros((4, 32, 128), dtype=np.float32)
        dmask = np.zeros((4, 8, 128, 512), dtype=BF)
        for j in range(4):
            qpos = np.concatenate([own_blocks[2 * j] * 256 + np.arange(256),
                                   own_blocks[2 * j + 1] * 256 + np.arange(256)])
            for kt in range(32):
                kp = kvloc[kt * 128:(kt + 1) * 128]
                kdead[j, kt] = np.where(kp > qpos.max(), NEG, 0.0)
            for si in range(8):
                kt = 8 * j + si
                kp = kvloc[kt * 128:(kt + 1) * 128]
                dmask[j, si] = np.where(qpos[None, :] >= kp[:, None],
                                        0.0, NEG).astype(BF)
        lo = start - 768
        gpos = np.arange(lo + 128, lo + 128 + 2688)
        f_kdead = np.where(gpos < 0, NEG, 0.0).astype(np.float32) \
            .reshape(21, 128)
        pos = np.clip(np.arange(lo, start + 2048), 0, 4095)
        aq, bq, ak, bk = _rotary_tables(pos)
        m = dict(shared)
        m.update({
            "x0": x0, "idx_kv": idx_kv, "idx_own": idx_own,
            "idx_kv_p": _pmap(idx_kv), "idx_own_p": _pmap(idx_own),
            "idx_halo0": _pmap_z(idx_halo), "idx_halo1": _pmap(idx_halo),
            "idx_own_f": _pmap_z(base + start + np.arange(16)[None, :] * 128 +
                                 ii[:, None]),
            "idx_lstm": _pmap_z(idx_lstm),
            "kdead_attn": kdead, "dmask_attn": dmask,
            "f_kdead": f_kdead,
            "f_aq": aq, "f_bq": bq, "f_ak": ak, "f_bk": bk,
        })
        in_maps.append(m)
    return in_maps


_NC_CACHE = {}


def kernel(**inputs):
    if "fused" not in _NC_CACHE:
        _NC_CACHE["fused"] = build_fused()
    nc = _NC_CACHE["fused"]
    in_maps = host_fused_inputs(inputs)
    res = run_bass_kernel_spmd(nc, in_maps, core_ids=list(range(8)))
    out = np.zeros((4, 4096, 1024), dtype=np.float32)
    for core in range(8):
        out[:, core * 512:(core + 1) * 512, :] = res.results[core]["y"]
    return out


# version 29
# speedup vs baseline: 1.0185x; 1.0051x over previous
import numpy as np
import ml_dtypes
import concourse.bass as bass
import concourse.mybir as mybir
from concourse.bass_utils import run_bass_kernel_spmd

F32 = mybir.dt.float32
BF16 = mybir.dt.bfloat16
AF = mybir.ActivationFunctionType
ALU = mybir.AluOpType
AX = mybir.AxisListType

NEG = -1.0e9
BF = ml_dtypes.bfloat16
EPS_L = 1e-30


class Prog:
    ENGINES = ("tensor", "vector", "scalar", "gpsimd", "sync")
    DMAQ = ("gpsimd", "sync")
    NLANE = 4
    _SEQ = [0]

    def __init__(self, nc):
        self.nc = nc
        self._SEQ[0] += 1
        sid = self._SEQ[0]
        self.ops = {e: [] for e in self.ENGINES}
        self.sems = {}
        self.cnt = {}
        self.lane = {}
        for e in self.ENGINES:
            if e in self.DMAQ:
                self.sems[e] = [nc.alloc_semaphore(f"sem{sid}_{e}{i}")
                                for i in range(self.NLANE)]
                self.cnt[e] = [0] * self.NLANE
                self.lane[e] = 0
            else:
                self.sems[e] = nc.alloc_semaphore(f"sem{sid}_" + e)
                self.cnt[e] = 0

    def at(self, eng):
        if eng in self.DMAQ:
            return [(self.sems[eng][i], self.cnt[eng][i])
                    for i in range(self.NLANE) if self.cnt[eng][i] > 0]
        return [(self.sems[eng], self.cnt[eng])] if self.cnt[eng] > 0 else []

    @staticmethod
    def _flat(waits):
        out = []
        for tok in waits:
            for p in tok:
                if p not in out:
                    out.append(p)
        return out

    def emit(self, eng, fn, waits=(), inc=0, chain=False):
        w = self._flat(waits)
        if eng in self.DMAQ and inc:
            ln = self.lane[eng]
            self.lane[eng] = (ln + 1) % self.NLANE
            sem = self.sems[eng][ln]
            if self.cnt[eng][ln] > 0:
                w.append((sem, self.cnt[eng][ln]))
            self.cnt[eng][ln] += inc
        elif inc:
            sem = self.sems[eng]
            if chain and self.cnt[eng] > 0:
                w.append((sem, self.cnt[eng]))
            self.cnt[eng] += inc
        else:
            sem = None

        def g(e, w=w, fn=fn, inc=inc, sem=sem):
            for s_, val in w:
                e.wait_ge(s_, val)
            ins = fn(e)
            if inc:
                ins.then_inc(sem, inc)

        self.ops[eng].append(g)

    def barrier(self):
        toks = [self.at(e) for e in self.ENGINES]
        for e in self.ENGINES:
            own = self.at(e)
            w = [t for t in toks if t is not own]
            self.emit(e, lambda _e: None,
                      waits=[p for t in w for p in [t]])

    def build(self):
        with self.nc.Block() as block:
            for eng in self.ENGINES:
                ops = self.ops[eng]
                if not ops:
                    continue

                def make(ops):
                    def _(e):
                        for f in ops:
                            f(e)
                    return _
                getattr(block, eng)(make(ops))


def np_consts():
    return {
        "id_bf": np.eye(128, dtype=BF),
        "id_f32": np.eye(128, dtype=np.float32),
        "ones_f32": np.ones((128, 128), dtype=np.float32),
        "ones_col_bf": np.ones((128, 1), dtype=BF),
    }


def load_consts(nc, P):
    out = {}
    for name, arr in np_consts().items():
        dt = BF16 if arr.dtype == BF else F32
        t = nc.alloc_sbuf_tensor("c_" + name, list(arr.shape), dt)
        dr = nc.inline_tensor(np.ascontiguousarray(arr), "ic_" + name)
        P.emit("sync", lambda e, t=t, dr=dr: e.dma_start(out=t[:], in_=dr.ap()),
               inc=16)
        out[name] = t
    return out


from concourse.bass_core_rust import IndirectOffsetOnAxis

I32 = mybir.dt.int32


class Arena:
    SZ = {F32: 4, BF16: 2, I32: 4}

    def __init__(self, nc, kib=203):
        self.base = nc.alloc_sbuf_tensor("ARENA", [128, kib * 1024],
                                         mybir.dt.uint8)
        self.cap = kib * 1024
        self.off = 0
        self.hwm = 0

    def reset(self):
        self.off = 0

    def __call__(self, name, shape, dt):
        Pn, N = shape
        nb = N * self.SZ[dt]
        v = self.base[0:Pn, self.off:self.off + nb].bitcast(dt)
        self.off += (nb + 31) // 32 * 32
        self.hwm = max(self.hwm, self.off)
        assert self.off <= self.cap, (name, self.off)
        return v

    def at(self, off, shape, dt):
        Pn, N = shape
        nb = N * self.SZ[dt]
        assert off + nb <= self.cap
        return self.base[0:Pn, off:off + nb].bitcast(dt)


def ffn_np_consts():
    c = {}
    c["id_bf"] = np.eye(128, dtype=BF)
    c["id_f32"] = np.eye(128, dtype=np.float32)
    c["ones_f32"] = np.ones((128, 128), dtype=np.float32)
    hs = np.zeros((128, 2), dtype=np.float32)
    hs[0:64, 0] = 1.0
    hs[64:128, 1] = 1.0
    c["headsum"] = hs
    R = np.zeros((128, 128), dtype=np.float32)
    for d in range(128):
        if d % 64 < 32:
            R[d + 32, d] = -1.0
        else:
            R[d - 32, d] = 1.0
    c["rotT"] = R
    k = np.arange(128)[:, None]
    q = np.arange(128)[None, :]
    c["tri_lo"] = np.where(k <= q, 0.0, NEG).astype(np.float32)
    c["tri_hi"] = np.where(k >= q, 0.0, NEG).astype(np.float32)
    up = np.zeros((128, 128), dtype=np.float32)
    for r in range(1, 128):
        up[r - 1, r] = 1.0
    c["shift_up"] = up
    l2f = np.zeros((128, 128), dtype=np.float32)
    l2f[127, 0] = 1.0
    c["last2first"] = l2f
    c["eps_col"] = np.full((128, 1), EPS_L, dtype=np.float32)
    return c




def all_np_consts():
    c = np_consts()
    c.update(ffn_np_consts())
    return c


def emit_stage_attn(nc, P, AL, consts, DS, l1=False):
    NQ, NK, D = 2048, 4096, 1024
    NH, HD = (1, 1024) if l1 else (16, 64)
    scale = (1.0 / 32.0) if l1 else (1.0 / 8.0)
    DMA_S2 = "sync"
    x_src = DS["x_src"]
    w_attn = DS["w_attn"]
    b_attn = DS["b_attn"]
    w_proj = DS["w_proj"]
    b_proj = DS["b_proj"]
    kdead = DS["kdead"]
    dmask = DS["dmask"]
    y = DS["ydst"]
    k_dram = DS["k_dram"]
    if l1:
        v_dram = DS["v_dram"]
    id_b = consts["id_bf"][:]
    id_f = consts["id_f32"]
    ones_f = consts["ones_f32"]
    ones_col_bf = consts["ones_col_bf"]
    ccw = DS.get("ccw") or []
    ccw_kv = DS.get("ccw_kv")
    ccw_own = DS.get("ccw_own")
    ccp = DS.get("ccp")

    xkv = [AL(f"xkv{i}", [128, NK], BF16) for i in range(8)]
    wt_off = AL.off
    wt = [AL(f"wt{i}", [128, 1024], BF16) for i in range(8)]
    xrow = AL("xrow", [128, 1024], BF16)
    wrow = xrow
    xrowf = AL("xrowf", [128, 1024], F32)
    idxkv_sb = AL("idxkv_sb", [128, 32], I32)
    idxown_sb = AL("idxown_sb", [128, 16], I32)
    P.emit(DMA_S2, lambda e: e.dma_start(out=idxkv_sb, in_=DS["idx_kv"]), inc=16)
    P.emit(DMA_S2, lambda e: e.dma_start(out=idxown_sb, in_=DS["idx_own"]),
           inc=16)
    VW = NH * (HD + 1) if not l1 else 0
    if not l1:
        v_sb = [AL(f"v{i}", [128, VW], BF16)
                for i in range(NK // 128)]
        k_h = AL("k_h", [128, NK], BF16)
        k_hb = [k_h, AL.at(wt_off, [128, NK], BF16)]
        pt_sb = [AL(f"pt{i}", [128, 512], BF16)
                 for i in range(2)]
    else:
        v_sb = None
        kh_off = AL.off
        k_h = AL("k_h", [128, NK], BF16)
        pt_sb = [AL(f"pt{i}", [128, 512], BF16)
                 for i in range(32)]
        vbuf2 = [AL(f"vb{i}", [128, D], BF16) for i in range(2)]
        vbuf = [vbuf2[0][:, 0:512], vbuf2[0][:, 512:1024],
                vbuf2[1][:, 0:512], vbuf2[1][:, 512:1024]]
        kbuf = ([AL.at(kh_off, [128, 4096], BF16),
                 AL.at(wt_off, [128, 4096], BF16),
                 AL.at(wt_off + 8192, [128, 4096], BF16)] +
                [AL(f"kb{i}", [128, 4096], BF16) for i in range(5)])
    dm_sb = [AL(f"dm{i}", [128, 512], BF16) for i in range(8)]
    kdT = [AL(f"kdT{i}", [128, 32], F32) for i in range(4)]
    kdraw = AL("kdraw", [32, 128], F32)
    biasall = AL("biasall", [128, 32], F32)
    rm = AL("rm", [128, 16], F32)
    msc = AL("msc", [128, 1024], F32)
    bv_row = msc[0:1, 0:1024]
    bp_row = msc[0:1, 0:1024]
    negM = AL("negM", [128, 1], F32)
    batraw = AL("batraw", [24, 128], F32)
    batT = AL("batT", [128, 24], F32)
    bcast_bv = AL("bcast_bv", [128, D], F32)
    bcast_bp = AL("bcast_bp", [128, D], F32)
    rb_sb = AL("rb_sb", [128, 512], F32)
    xo = [xrowf, AL("xo1", [128, D], F32)]
    xa = AL("xa", [128, D], F32)
    ln_s = AL("ln_s", [128, 24], F32)
    y_sb = bcast_bv
    sq = y_sb

    psA = DS["psA"]
    psS = DS["psS"]
    psV = DS["psV"]
    psT = DS["psT"]
    psTb = psT.bitcast(BF16)

    DMA_G, DMA_S = "gpsimd", "sync"

    def fm_load(dst_list, idx_sb, ntiles, col_off=0):
        xr2 = [xrow, msc.bitcast(BF16)[:, 0:1024]]
        ps2 = [psTb[0:128, 0:1024], psS[0].bitcast(BF16)[:, 0:1024]]
        t_tr = {-1: P.at("tensor"), -2: P.at("tensor")}
        v_cp = {-1: P.at("vector"), -2: P.at("vector")}
        for t in range(ntiles):
            tb = t % 2
            w = [t_tr[t - 2], ccw_kv[t] if ccw_kv else ccw]
            if t == 0:
                w += [P.at(DMA_S2), P.at("vector")]
            P.emit(DMA_G, lambda e, t=t, tb=tb, idx_sb=idx_sb:
                   e.indirect_dma_start(
                       out=xr2[tb][:], out_offset=None, in_=x_src[:, :],
                       in_offset=IndirectOffsetOnAxis(idx_sb[:, t:t + 1], 0)),
                   waits=w, inc=16)
            gw = P.at(DMA_G)
            P.emit("tensor", lambda e, tb=tb: (
                [e.transpose(ps2[tb][:, ic * 128:(ic + 1) * 128],
                             xr2[tb][:, ic * 128:(ic + 1) * 128], id_b)
                 for ic in range(8)][-1]),
                waits=[gw, v_cp[t - 2]], inc=1)
            t_tr[t] = P.at("tensor")
            P.emit("vector", lambda e, t=t, tb=tb: (
                [e.tensor_copy(
                    dst_list[ic][:, col_off + t * 128:col_off + (t + 1) * 128],
                    ps2[tb][:, ic * 128:(ic + 1) * 128]) for ic in range(8)][-1]),
                waits=[t_tr[t]], inc=1)
            v_cp[t] = P.at("vector")

    fm_load(xkv, idxkv_sb, NK // 128)

    P.emit(DMA_S, lambda e: e.dma_start(
        out=batraw[:], in_=b_attn.rearrange("(c p) -> c p", p=128)),
        waits=[P.at("tensor")], inc=16)
    P.emit(DMA_S, lambda e: e.dma_start(
        out=bv_row, in_=b_attn[2048:3072].unsqueeze(0)),
        waits=[P.at("tensor")], inc=16)
    P.barrier()
    P.emit("tensor", lambda e: e.transpose(psT[0:128, 0:24], batraw[:], id_f[0:24, 0:24]),
           inc=1)
    P.emit("vector", lambda e: e.tensor_copy(batT[:, 0:24], psT[0:128, 0:24]),
           waits=[P.at("tensor")], inc=1)
    for j in range(4):
        P.emit(DMA_S, lambda e, j=j: e.dma_start(out=kdraw[:], in_=kdead[j]),
               waits=[P.at("tensor")], inc=16)
        P.emit("tensor", lambda e: e.transpose(psT[0:128, 0:32], kdraw[:],
                                               id_f[0:32, 0:32]),
               waits=[P.at(DMA_S), P.at("vector")], inc=1)
        P.emit("vector", lambda e, j=j: e.tensor_copy(kdT[j][:], psT[0:128, 0:32]),
               waits=[P.at("tensor")], inc=1)
    def bcast_row(dst):
        for half in range(2):
            P.emit("tensor", lambda e, half=half: e.matmul(
                psT[0:128, 0:512], ones_f[0:1, 0:128],
                bv_row[0:1, half * 512:(half + 1) * 512], start=True, stop=True),
                waits=[P.at("vector"), P.at(DMA_S)], inc=1)
            P.emit("vector", lambda e, dst=dst, half=half: e.tensor_copy(
                dst[:, half * 512:(half + 1) * 512], psT[0:128, 0:512]),
                waits=[P.at("tensor")], inc=1)
    bcast_row(bcast_bv)
    P.barrier()
    P.emit(DMA_S, lambda e: e.dma_start(
        out=bp_row, in_=b_proj.unsqueeze(0)), inc=16)
    P.barrier()
    bcast_row(bcast_bp)
    P.barrier()

    def load_w_third(third):
        for ic in range(8):
            P.emit(DMA_G, lambda e, ic=ic, third=third: e.dma_start(
                out=wt[ic][:], in_=w_attn[ic * 128:(ic + 1) * 128,
                                          third * D:(third + 1) * D]),
                waits=[P.at("tensor")], inc=16)
        P.barrier()

    def load_w_proj():
        for ic in range(8):
            P.emit(DMA_G, lambda e, ic=ic: e.dma_start(
                out=wt[ic][:], in_=w_proj[ic * 128:(ic + 1) * 128, :]),
                waits=[P.at("tensor")], inc=16)
        P.barrier()

    load_w_third(1)
    s_slot = {i: P.at("scalar") for i in range(4)}
    st_tok = P.at(DMA_S)
    for oc in range(8):
        for nt in range(NK // 512):
            sl = nt % 4
            P.emit("tensor", lambda e, oc=oc, nt=nt: (
                [e.matmul(psA[:, (nt % 4) * 512:(nt % 4) * 512 + 512],
                          wt[ic][:, oc * 128:(oc + 1) * 128],
                          xkv[ic][:, nt * 512:(nt + 1) * 512],
                          start=(ic == 0), stop=(ic == 7)) for ic in range(8)][-1]),
                waits=[s_slot[sl]], inc=1)
            t_mm = P.at("tensor")
            P.emit("scalar", lambda e, oc=oc, nt=nt: e.activation(
                out=k_h[:, nt * 512:(nt + 1) * 512],
                in_=psA[:, (nt % 4) * 512:(nt % 4) * 512 + 512],
                func=AF.Identity, bias=batT[:, 8 + oc:9 + oc], scale=1.0),
                waits=[t_mm] + ([st_tok] if nt == 0 else []), inc=1)
            s_slot[sl] = P.at("scalar")
        P.emit(DMA_S, lambda e, oc=oc: e.dma_start(
            out=k_dram.ap()[oc], in_=k_h[:]),
            waits=[s_slot[3]], inc=16)
        st_tok = P.at(DMA_S)
    P.barrier()

    load_w_third(2)
    v_slot = {0: P.at("vector"), 1: P.at("vector")}
    st2 = {-1: P.at(DMA_S), -2: P.at(DMA_S)}
    for t in range(NK // 128):
        po = (t % 2) * 1024
        P.emit("tensor", lambda e, t=t, po=po: (
            [e.matmul(psA[:, po + half * 512:po + (half + 1) * 512],
                      xkv[ic][:, t * 128:(t + 1) * 128],
                      wt[ic][:, half * 512:(half + 1) * 512],
                      start=(ic == 0), stop=(ic == 7))
             for half in range(2) for ic in range(8)][-1]),
            waits=[v_slot[t % 2]], inc=1)
        t_mm = P.at("tensor")
        if not l1:
            P.emit("vector", lambda e, t=t, po=po: e.tensor_tensor(
                v_sb[t][:].rearrange("p (h c) -> p h c", c=HD + 1)[:, :, 0:HD],
                psA[:, po:po + 1024].rearrange("p (h c) -> p h c", c=HD),
                bcast_bv[:].rearrange("p (h c) -> p h c", c=HD), ALU.add),
                waits=[t_mm], inc=1)
            P.emit("vector", lambda e, t=t: e.memset(
                v_sb[t][:].rearrange("p (h c) -> p h c", c=HD + 1)[:, :, HD:HD + 1],
                1.0), inc=1)
            v_slot[t % 2] = P.at("vector")
        else:
            P.emit("vector", lambda e, t=t, po=po: e.tensor_tensor(
                k_h[:, po:po + 1024], psA[:, po:po + 1024], bcast_bv[:],
                ALU.add),
                waits=[t_mm, st2[t - 2]], inc=1)
            v_slot[t % 2] = P.at("vector")
            P.emit(DMA_S, lambda e, t=t, po=po: e.dma_start(
                out=v_dram.ap()[t], in_=k_h[:, po:po + 1024]),
                waits=[v_slot[t % 2]], inc=16)
            st2[t] = P.at(DMA_S)
    P.barrier()

    load_w_third(0)
    s_slot = {i: P.at("scalar") for i in range(4)}
    for nt in range(NQ // 512):
        for oc in range(8):
            sl = oc % 4
            P.emit("tensor", lambda e, oc=oc, nt=nt: (
                [e.matmul(psA[:, (oc % 4) * 512:(oc % 4) * 512 + 512],
                          wt[ic][:, oc * 128:(oc + 1) * 128],
                          xkv[ic][:, nt * 1024 + 512:(nt + 1) * 1024],
                          start=(ic == 0), stop=(ic == 7)) for ic in range(8)][-1]),
                waits=[s_slot[sl]], inc=1)
            t_mm = P.at("tensor")
            P.emit("scalar", lambda e, oc=oc, nt=nt: e.activation(
                out=xkv[oc][:, nt * 512:(nt + 1) * 512],
                in_=psA[:, (oc % 4) * 512:(oc % 4) * 512 + 512],
                func=AF.Identity, bias=batT[:, oc:oc + 1], scale=1.0),
                waits=[t_mm], inc=1)
            s_slot[sl] = P.at("scalar")
    P.barrier()

    q_of = 0
    ao_of = NQ

    def unit_l0(hp, hh, j, k_ap, gw_k):
        h = hp * 2 + (0 if hh == 0 else 1)
        nkt = 8 * (j + 1)
        klen = nkt * 128
        q_ap = xkv[hp][hh:hh + 64, q_of + j * 512:q_of + (j + 1) * 512]
        P.emit("vector", lambda e: e.memset(rm[:], -3.0e38),
               waits=[P.at("tensor")], inc=1)
        v0 = P.at("vector")
        v_red = {-1: v0, -2: v0}
        idx = 0
        for sub in range(4):
            qs = xkv[hp][hh:hh + 64,
                         q_of + j * 512 + sub * 128:q_of + j * 512 + (sub + 1) * 128]
            for r in range((klen + 1023) // 1024):
                kl = min(1024, klen - r * 1024)
                po = (idx % 2) * 1024
                nmm = (kl + 511) // 512
                w = [v_red[idx - 2]] + ([gw_k] if idx == 0 else [])
                P.emit("tensor", lambda e, qs=qs, r=r, kl=kl, nmm=nmm, po=po,
                       k_ap=k_ap: (
                    [e.matmul(psA[:, po + m * 512:
                                  po + m * 512 + min(512, kl - m * 512)],
                              qs, k_ap[hh:hh + 64,
                                       r * 1024 + m * 512:
                                       r * 1024 + m * 512 + min(512, kl - m * 512)],
                              start=True, stop=True) for m in range(nmm)][-1]),
                    waits=w, inc=1)
                t_mm = P.at("tensor")
                P.emit("vector", lambda e, idx=idx, kl=kl, po=po: e.reduce_max(
                    rm[:, idx:idx + 1], psA[:, po:po + kl], axis=AX.X),
                    waits=[t_mm], inc=1)
                v_red[idx] = P.at("vector")
                idx += 1
        P.emit("tensor", lambda e: e.transpose(psT[0:16, 0:128], rm[:], id_f[:]),
               waits=[P.at("vector")], inc=1)
        P.emit("vector", lambda e: e.reduce_max(msc[0:16, 0:1], psT[0:16, 0:128],
                                                axis=AX.X),
               waits=[P.at("tensor")], inc=1)
        P.emit("tensor", lambda e: e.transpose(psT[0:1, 0:16], msc[0:16, 0:1],
                                               id_f[0:16, 0:16]),
               waits=[P.at("vector")], inc=1)
        P.emit("vector", lambda e: e.reduce_max(msc[0:1, 64:65], psT[0:1, 0:16],
                                                axis=AX.X),
               waits=[P.at("tensor")], inc=1)
        P.emit("tensor", lambda e: e.matmul(psT[0:128, 0:1], ones_f[0:1, 0:128],
                                            msc[0:1, 64:65], start=True,
                                            stop=True),
               waits=[P.at("vector")], inc=1)
        P.emit("vector", lambda e: e.tensor_scalar_mul(negM[:], psT[0:128, 0:1],
                                                       -scale),
               waits=[P.at("tensor")], inc=1)
        P.emit("vector", lambda e, j=j: e.tensor_scalar(
            biasall[:, 0:32], kdT[j][:], negM[0:128, 0:1], None, ALU.add),
            chain=True, inc=1)
        v_bias = P.at("vector")
        mslots = {8 * j + s: s for s in range(8)}
        s_prev = P.at("scalar")
        t_S = {}
        t_AV = {}
        v_mask = {}
        s_exp = {}

        def emit_S(kt):
            w = [s_prev, v_bias] if kt < 2 else []
            P.emit("tensor", lambda e, kt=kt, q_ap=q_ap, k_ap=k_ap: e.matmul(
                psS[kt % 2][:], k_ap[hh:hh + 64, kt * 128:(kt + 1) * 128], q_ap,
                start=True, stop=True), waits=w, inc=1)
            t_S[kt] = P.at("tensor")

        emit_S(0)
        for kt in range(nkt):
            buf = kt % 2
            if kt + 1 < nkt:
                emit_S(kt + 1)
            ew = [t_S[kt]]
            if kt in mslots:
                P.emit("vector", lambda e, kt=kt, buf=buf, s=mslots[kt]:
                       e.tensor_tensor(psS[buf][:], psS[buf][:], dm_sb[s][:],
                                       ALU.add),
                       waits=[t_S[kt]], inc=1)
                v_mask[kt] = P.at("vector")
                ew.append(v_mask[kt])
            if kt >= 2:
                ew.append(t_AV[kt - 2])
            P.emit("scalar", lambda e, kt=kt, buf=buf: e.activation(
                out=pt_sb[buf][:], in_=psS[buf][:], func=AF.Exp,
                bias=biasall[:, kt:kt + 1], scale=scale),
                waits=ew, inc=1)
            s_exp[kt] = P.at("scalar")
            P.emit("tensor", lambda e, kt=kt, buf=buf, h=h: e.matmul(
                psV[0:65, :], v_sb[kt][:, h * 65:h * 65 + 65], pt_sb[buf][:],
                start=(kt == 0), stop=(kt == nkt - 1)),
                waits=[s_exp[kt]], inc=1)
            t_AV[kt] = P.at("tensor")
        P.emit("scalar", lambda e: e.activation(out=msc[64:65, 0:512],
                                                in_=psV[64:65, :],
                                                func=AF.Ln,
                                                bias=consts["eps_col"][64:65, 0:1]),
               waits=[P.at("tensor")], inc=1)
        s_l = P.at("scalar")
        P.emit("tensor", lambda e: e.matmul(psT[0:64, 0:512],
                                            ones_f[64:65, 0:64],
                                            msc[64:65, 0:512],
                                            start=True, stop=True),
               waits=[s_l], inc=1)
        t_bc = P.at("tensor")
        P.emit("scalar", lambda e: e.activation(out=rb_sb[0:64, 0:512],
                                                in_=psT[0:64, 0:512],
                                                func=AF.Exp, scale=-1.0),
               waits=[t_bc], inc=1)
        s_rb = P.at("scalar")
        P.emit("vector", lambda e, hp=hp, hh=hh, j=j: e.tensor_tensor(
            xkv[hp][hh:hh + 64, ao_of + j * 512:ao_of + (j + 1) * 512],
            psV[0:64, :], rb_sb[0:64, 0:512], ALU.mult),
            waits=[s_rb], inc=1)

    if not l1:
        for j in range(4):
            for s in range(8):
                P.emit(DMA_S, lambda e, j=j, s=s: e.dma_start(
                    out=dm_sb[s][:], in_=dmask[j, s]),
                    waits=[P.at("scalar"), P.at("vector")], inc=16)
            P.barrier()
            t_end = {}
            gw_k = {}

            def emit_kload(hp):
                kw = [t_end[hp - 2]] if hp >= 2 else [P.at("tensor")]
                P.emit(DMA_G, lambda e, hp=hp: e.dma_start(
                    out=k_hb[hp % 2][:], in_=k_dram.ap()[hp]),
                    waits=kw, inc=16)
                gw_k[hp] = P.at(DMA_G)

            emit_kload(0)
            emit_kload(1)
            for hp in range(8):
                for hh in (0, 64):
                    unit_l0(hp, hh, j, k_hb[hp % 2], gw_k[hp])
                t_end[hp] = P.at("tensor")
                if hp + 2 < 8:
                    emit_kload(hp + 2)
            P.barrier()
    else:
        t_kfree = P.at("tensor")
        s_prevexp = P.at("scalar")
        av0 = P.at("tensor")
        av_free = {0: av0, 1: av0, 2: av0, 3: av0}
        for j in range(4):
            nkt = 8 * (j + 1)
            klen = nkt * 128
            ka = min(2048, klen)
            for ic in range(8):
                P.emit(DMA_G if ic % 2 == 0 else DMA_S,
                       lambda e, ic=ic, ka=ka: e.dma_start(
                           out=kbuf[ic][:, 0:ka],
                           in_=k_dram.ap()[ic][:, 0:ka]),
                       waits=[t_kfree], inc=16)
            gwA = P.at(DMA_G) + P.at(DMA_S)
            for ic in range(8 if klen > 2048 else 0):
                P.emit(DMA_G if ic % 2 == 0 else DMA_S,
                       lambda e, ic=ic, klen=klen: e.dma_start(
                           out=kbuf[ic][:, 2048:klen],
                           in_=k_dram.ap()[ic][:, 2048:klen]),
                       waits=[t_kfree], inc=16)
            gwB = P.at(DMA_G) + P.at(DMA_S)
            for s8 in range(8):
                P.emit(DMA_S, lambda e, j=j, s8=s8: e.dma_start(
                    out=dm_sb[s8][:], in_=dmask[j, s8]),
                    waits=[s_prevexp, P.at("vector") if s8 == 0 else []],
                    inc=16)
            gw_dm = P.at(DMA_S)
            mslots = {8 * j + s8: s8 for s8 in range(8)}
            P.emit("vector", lambda e: e.memset(rm[:], -3.0e38),
                   waits=[P.at("tensor")], inc=1)
            v0 = P.at("vector")
            v_red = {-1: v0, -2: v0}
            idx = 0
            for sub in range(4):
                qcol = j * 512 + sub * 128
                for r in range((klen + 1023) // 1024):
                    kl = min(1024, klen - r * 1024)
                    po = (idx % 2) * 1024
                    nmm = (kl + 511) // 512
                    gw = gwA if (r + 1) * 1024 <= 2048 else gwB
                    P.emit("tensor", lambda e, qcol=qcol, r=r, kl=kl,
                           nmm=nmm, po=po: (
                        [e.matmul(psA[:, po + m * 512:
                                      po + m * 512 + min(512, kl - m * 512)],
                                  xkv[ic][:, q_of + qcol:q_of + qcol + 128],
                                  kbuf[ic][:, r * 1024 + m * 512:
                                           r * 1024 + m * 512 +
                                           min(512, kl - m * 512)],
                                  start=(ic == 0), stop=(ic == 7))
                         for m in range(nmm) for ic in range(8)][-1]),
                        waits=[gw, v_red[idx - 2]], inc=1)
                    t_mm = P.at("tensor")
                    P.emit("vector", lambda e, idx=idx, kl=kl, po=po:
                           e.reduce_max(rm[:, idx:idx + 1], psA[:, po:po + kl],
                                        axis=AX.X),
                           waits=[t_mm], inc=1)
                    v_red[idx] = P.at("vector")
                    idx += 1
            v_p1 = P.at("vector")
            P.emit("tensor", lambda e: e.transpose(psT[0:16, 0:128], rm[:],
                                                   id_f[:]),
                   waits=[v_p1], inc=1)
            P.emit("vector", lambda e: e.reduce_max(msc[0:16, 0:1],
                                                    psT[0:16, 0:128],
                                                    axis=AX.X),
                   waits=[P.at("tensor")], inc=1)
            P.emit("tensor", lambda e: e.transpose(psT[0:1, 0:16],
                                                   msc[0:16, 0:1],
                                                   id_f[0:16, 0:16]),
                   waits=[P.at("vector")], inc=1)
            P.emit("vector", lambda e: e.reduce_max(msc[0:1, 64:65],
                                                    psT[0:1, 0:16],
                                                    axis=AX.X),
                   waits=[P.at("tensor")], inc=1)
            P.emit("tensor", lambda e: e.matmul(psT[0:128, 0:1],
                                                ones_f[0:1, 0:128],
                                                msc[0:1, 64:65],
                                                start=True, stop=True),
                   waits=[P.at("vector")], inc=1)
            P.emit("vector", lambda e: e.tensor_scalar_mul(
                negM[:], psT[0:128, 0:1], -scale),
                waits=[P.at("tensor")], inc=1)
            P.emit("vector", lambda e, j=j: e.tensor_scalar(
                biasall[:, 0:32], kdT[j][:], negM[0:128, 0:1], None,
                ALU.add), chain=True, inc=1)
            v_bias = P.at("vector")
            s_exp = {}
            t_S = {}
            for kt in range(nkt):
                buf = kt % 2
                w = [s_exp[kt - 2]] if kt >= 2 else [s_prevexp, v_bias]
                if kt * 128 >= 2048 and (kt - 2 < 0 or (kt - 2) * 128 < 2048):
                    w.append(gwB)
                P.emit("tensor", lambda e, kt=kt, buf=buf, j=j: (
                    [e.matmul(psS[buf][:, 0:512],
                              kbuf[ic][:, kt * 128:(kt + 1) * 128],
                              xkv[ic][:, q_of + j * 512:q_of + (j + 1) * 512],
                              start=(ic == 0), stop=(ic == 7))
                     for ic in range(8)][-1]),
                    waits=w, inc=1)
                t_S[kt] = P.at("tensor")
                ew = [t_S[kt]]
                if kt in mslots:
                    P.emit("vector", lambda e, buf=buf, s8=mslots[kt]:
                           e.tensor_tensor(psS[buf][:, 0:512],
                                           psS[buf][:, 0:512],
                                           dm_sb[s8][:], ALU.add),
                           waits=[t_S[kt], gw_dm], inc=1)
                    ew.append(P.at("vector"))
                P.emit("scalar", lambda e, kt=kt, buf=buf: e.activation(
                    out=pt_sb[kt][:], in_=psS[buf][:, 0:512],
                    func=AF.Exp, bias=biasall[:, kt:kt + 1],
                    scale=scale),
                    waits=ew, inc=1)
                s_exp[kt] = P.at("scalar")
                P.emit("tensor", lambda e, kt=kt, nkt=nkt: e.matmul(
                    psV[0:1, 0:512], ones_col_bf[:], pt_sb[kt][:],
                    start=(kt == 0), stop=(kt == nkt - 1)),
                    waits=[s_exp[kt]], inc=1)
            t_kfree = P.at("tensor")
            P.emit("scalar", lambda e: e.activation(
                out=msc[0:1, 0:512], in_=psV[0:1, 0:512],
                func=AF.Ln, bias=consts["eps_col"][0:1, 0:1]),
                waits=[t_kfree], inc=1)
            s_l = P.at("scalar")
            P.emit("tensor", lambda e: e.matmul(psT[0:128, 0:512],
                                                ones_f[0:1, 0:128],
                                                msc[0:1, 0:512],
                                                start=True, stop=True),
                   waits=[s_l], inc=1)
            t_bc = P.at("tensor")
            P.emit("scalar", lambda e: e.activation(
                out=rb_sb[0:128, 0:512], in_=psT[0:128, 0:512],
                func=AF.Exp, scale=-1.0),
                waits=[t_bc], inc=1)
            s_rb = P.at("scalar")
            s_prevexp = s_rb
            v_mults = None
            for half in range(2):
                hof = half * 512
                for kt in range(nkt):
                    g = half * nkt + kt
                    slot = g % 4
                    vq = DMA_G if slot % 2 == 0 else DMA_S
                    P.emit(vq, lambda e, kt=kt, hof=hof, slot=slot:
                           e.dma_start(out=vbuf[slot],
                                       in_=v_dram.ap()[kt][:, hof:hof + 512]),
                           waits=[av_free[slot]], inc=16)
                    gwv = P.at(vq)
                    wv = [gwv]
                    if half == 0:
                        wv.append(s_exp[kt])
                        if kt == 0:
                            wv.append(v_p1)
                    elif kt == 0:
                        wv.append(v_mults)
                    P.emit("tensor", lambda e, kt=kt, slot=slot, nkt=nkt: (
                        [e.matmul(psA[:, dc * 512:(dc + 1) * 512],
                                  vbuf[slot][:, dc * 128:(dc + 1) * 128],
                                  pt_sb[kt][:],
                                  start=(kt == 0), stop=(kt == nkt - 1),
                                  skip_group_check=True)
                         for dc in range(4)][-1]),
                        waits=wv, inc=1)
                    av_free[slot] = P.at("tensor")
                t_avs = P.at("tensor")
                for dc2 in range(4):
                    dc = half * 4 + dc2
                    P.emit("vector", lambda e, dc=dc, dc2=dc2, j=j:
                           e.tensor_tensor(
                               xkv[dc][:, ao_of + j * 512:ao_of + (j + 1) * 512],
                               psA[:, dc2 * 512:(dc2 + 1) * 512],
                               rb_sb[0:128, 0:512], ALU.mult),
                           waits=[t_avs, s_rb] if dc2 == 0 else [], inc=1)
                v_mults = P.at("vector")
        P.barrier()


    load_w_proj()
    v_xa = {-1: P.at("vector"), -2: P.at("vector")}
    s_y = {-1: P.at("scalar")}
    st = {-1: P.at(DMA_S)}
    for t in range(NQ // 128):
        pa = (t % 2) * 1024
        P.emit(DMA_G, lambda e, t=t: e.indirect_dma_start(
            out=xo[t % 2][:], out_offset=None, in_=x_src[:, :],
            in_offset=IndirectOffsetOnAxis(idxown_sb[:, t:t + 1], 0)),
            waits=[v_xa[t - 2], ccw_own[t] if ccw_own else ccw], inc=16)
        dmaw = P.at(DMA_G)
        P.emit("tensor", lambda e, t=t, pa=pa: (
            [e.matmul(psA[:, pa + half * 512:pa + (half + 1) * 512],
                      xkv[ic][:, ao_of + t * 128:ao_of + (t + 1) * 128],
                      wt[ic][:, half * 512:(half + 1) * 512],
                      start=(ic == 0), stop=(ic == 7))
             for half in range(2) for ic in range(8)][-1]),
            waits=[v_xa[t - 2]] + ([P.at("scalar")] if t == 0 else []), inc=1)
        t_mm = P.at("tensor")
        P.emit("vector", lambda e, t=t, pa=pa: e.tensor_tensor(
            xa[:], psA[:, pa:pa + 1024], xo[t % 2][:], ALU.add),
            waits=[t_mm, dmaw, s_y[t - 1]], inc=1)
        v_xa[t] = P.at("vector")
        P.emit("vector", lambda e: e.tensor_tensor(xa[:], xa[:], bcast_bp[:],
                                                   ALU.add), chain=True, inc=1)
        P.emit("vector", lambda e: e.bn_stats(ln_s[:, 8:14], xa[:, 0:512]),
               chain=True, inc=1)
        P.emit("vector", lambda e: e.bn_stats(ln_s[:, 14:20], xa[:, 512:1024]),
               chain=True, inc=1)
        P.emit("vector", lambda e: e.bn_aggr(ln_s[:, 0:2], ln_s[:, 8:20]),
               chain=True, inc=1)
        P.emit("vector", lambda e: e.tensor_scalar_add(ln_s[:, 2:3],
                                                       ln_s[:, 1:2], 1e-5),
               chain=True, inc=1)
        v_ln = P.at("vector")
        P.emit("scalar", lambda e: e.activation(out=ln_s[:, 3:4],
                                                in_=ln_s[:, 2:3], func=AF.Sqrt),
               waits=[v_ln], inc=1)
        s_sq2 = P.at("scalar")
        P.emit("vector", lambda e: e.reciprocal(ln_s[:, 4:5], ln_s[:, 3:4]),
               waits=[s_sq2], inc=1)
        P.emit("vector", lambda e: e.tensor_tensor(ln_s[:, 5:6], ln_s[:, 0:1],
                                                   ln_s[:, 4:5], ALU.mult),
               chain=True, inc=1)
        P.emit("vector", lambda e: e.tensor_scalar_mul(ln_s[:, 5:6],
                                                       ln_s[:, 5:6], -1.0),
               chain=True, inc=1)
        v_fin = P.at("vector")
        P.emit("scalar", lambda e, t=t: e.activation(
            out=y_sb[:], in_=xa[:], func=AF.Identity, bias=ln_s[:, 5:6],
            scale=ln_s[:, 4:5]),
            waits=[v_fin, st[t - 1]], inc=1)
        s_y[t] = P.at("scalar")
        P.emit(DMA_S, lambda e, t=t: e.dma_start(
            out=y[t * 128:(t + 1) * 128, :], in_=y_sb[:]),
            waits=[s_y[t]], inc=16)
        st[t] = P.at(DMA_S)
        if ccp is not None and t % 4 == 3:
            ccp(P, t // 4, P.at(DMA_S))
    P.emit(DMA_S, lambda e: None, waits=[P.at(DMA_S)])




def emit_stage_ffn(nc, P, AL, consts, D, last=False):
    NH2 = 2816
    NKV = 2688
    NATT = 2176
    NOWN = 2048
    HIDP = 2816
    x_src = D["x_src"]
    wqkv = D["wqkv"]
    qs_blk = D["qs_blk"]
    ks_blk = D["ks_blk"]
    wo = D["wo"]
    ln_g = D["ln_g"]
    ln_b = D["ln_b"]
    w1a = D["w1a"]
    w1g = D["w1g"]
    b1a = D["b1a"]
    b1g = D["b1g"]
    w2p = D["w2p"]
    b2 = D["b2"]
    kdead = D["kdead"]
    tabs = {tn: D[tn] for tn in ("aq", "bq", "ak", "bk")}
    if last:
        y1_src = D["y1_src"]
    y = D["ydst"]
    xnew_scr = D["xnew_scr"]
    id_b = consts["id_bf"][:]
    id_f = consts["id_f32"]
    ones_f = consts["ones_f32"]
    headsum = consts["headsum"]
    rotT = consts["rotT"]
    ccw = D.get("ccw") or []
    ccw1 = D.get("ccw_y1") or []
    ccw_halo = D.get("ccw_halo")
    ccw_own1 = D.get("ccw_own1")
    ccp = D.get("ccp")
    w1abf = D["w1abf"]
    w1gbf = D["w1gbf"]
    w2bf = D["w2bf"]

    idxh_sb = AL("idxh_sb", [128, 22], I32)
    idxown_sb = AL("idxown2_sb", [128, 16], I32)
    P.emit("sync", lambda e: e.dma_start(out=idxh_sb, in_=D["idx_halo"]), inc=16)
    P.emit("sync", lambda e: e.dma_start(out=idxown_sb, in_=D["idx_own"]),
           inc=16)
    xf = [AL(f"xf{i}", [128, NH2], BF16) for i in range(8)]
    wq_sb = [AL(f"wq{i}", [128, 1536], BF16) for i in range(8)]
    qr = [AL(f"qr{i}", [128, NATT], BF16) for i in range(4)]
    kr = [AL(f"kr{i}", [128, NKV], BF16) for i in range(4)]
    v_sb = [AL(f"vv{i}", [128, 520], BF16) for i in range(21)]

    def hgt(j):
        if j < 20:
            return kr[j % 4][:, (j // 4) * 512:(j // 4) * 512 + 512]
        return qr[j - 20][:, 0:512]
    tbuf_off = AL.off
    tbuf = {tn: AL("tb_" + tn, [128, 512], F32)
            for tn in ("aq", "bq", "ak", "bk")}
    lm_sb = AL.at(tbuf_off, [128, 4096], BF16)
    qhat_off = AL.off
    qhat = AL("qhat", [128, 512], F32)
    t1 = AL("t1", [128, 512], F32)
    t2_off = AL.off
    t2 = AL("t2", [128, 512], F32)
    ssb = AL("ssb", [128, 512], F32)
    sqt = AL("sqt", [128, 512], F32)
    rn = AL("rn", [2, 512], F32)
    pt_sb = [AL(f"pt{i}", [128, 512], BF16) for i in range(2)]
    kdT = AL("kdT", [128, 21], F32)
    kdraw = AL("kdraw", [21, 128], F32)
    biasu = AL("biasu", [128, 21], F32)
    negM = AL("negM", [128, 1], F32)
    msc = AL("msc", [128, 1024], F32)
    rb_sb = AL("rb_sb", [128, 512], F32)
    b1aT = AL("b1aT", [128, 22], F32)
    b1gT = AL("b1gT", [128, 22], F32)
    bT_raw = AL("bT_raw", [22, 128], F32)
    bc_g = AL("bc_g", [128, 1024], F32)
    bc_b = AL("bc_b", [128, 1024], F32)
    bc_b2 = AL("bc_b2", [128, 1024], F32)
    qsb_sb = AL("qsb_sb", [2, 128], F32)
    ksb_sb = AL("ksb_sb", [2, 128], F32)
    w1abuf2 = [wq_sb[0][:, 0:1024], wq_sb[1][:, 0:1024]]
    w1gbuf2 = [wq_sb[2][:, 0:1024], wq_sb[3][:, 0:1024]]
    w2buf2 = [wq_sb[4][:, 0:1024], wq_sb[5][:, 0:1024]]
    xo = [AL(f"xo{i}", [128, 1024], F32) for i in range(2)]
    xn = [AL(f"xn{i}", [128, 1024], F32) for i in range(2)]
    h2 = AL("h2", [128, 1024], F32)
    xa = AL("xa", [128, 1024], F32)
    ag_sb = AL("ag_sb", [128, 512], BF16)
    ag_b = AL("ag_b", [128, 512], BF16)
    sg_b = AL("sg_b", [128, 512], BF16)
    ln_s = AL("ln_s", [128, 24], F32)
    y_sb = AL("y_sb", [128, 1024], F32)
    sq2 = y_sb

    psA = D["psA"]
    psS = D["psS"]
    psV = D["psV"]
    psT = D["psT"]
    psTb = psT.bitcast(BF16)

    DMA_G, DMA_S = "gpsimd", "sync"

    xs2 = [sqt.bitcast(BF16)[:, 0:1024], qhat.bitcast(BF16)[:, 0:1024]]
    pss2 = [psTb[0:128, 0:1024], psS[0].bitcast(BF16)[:, 0:1024]]
    t_tr = {-1: P.at("tensor"), -2: P.at("tensor")}
    v_cp = {-1: P.at("vector"), -2: P.at("vector")}
    for t in range(NH2 // 128):
        tb = t % 2
        w = [t_tr[t - 2], ccw_halo[t] if ccw_halo else ccw]
        if t == 0:
            w += [P.at("sync"), P.at("vector")]
        P.emit(DMA_G, lambda e, t=t, tb=tb: e.indirect_dma_start(
            out=xs2[tb][:], out_offset=None, in_=x_src[:, :],
            in_offset=IndirectOffsetOnAxis(idxh_sb[:, t:t + 1], 0)),
            waits=w, inc=16)
        gw = P.at(DMA_G)
        P.emit("tensor", lambda e, tb=tb: (
            [e.transpose(pss2[tb][:, ic * 128:(ic + 1) * 128],
                         xs2[tb][:, ic * 128:(ic + 1) * 128], id_b)
             for ic in range(8)][-1]),
            waits=[gw, v_cp[t - 2]], inc=1)
        t_tr[t] = P.at("tensor")
        P.emit("vector", lambda e, t=t, tb=tb: (
            [e.tensor_copy(xf[ic][:, t * 128:(t + 1) * 128],
                           pss2[tb][:, ic * 128:(ic + 1) * 128])
             for ic in range(8)][-1]),
            waits=[t_tr[t]], inc=1)
        v_cp[t] = P.at("vector")

    for ic in range(8):
        P.emit(DMA_G, lambda e, ic=ic: e.dma_start(
            out=wq_sb[ic][:], in_=wqkv[ic * 128:(ic + 1) * 128, :]), inc=16)
    P.emit(DMA_S, lambda e: e.dma_start(out=kdraw[:], in_=kdead), inc=16)
    P.emit(DMA_S, lambda e: e.dma_start(out=qsb_sb[:], in_=qs_blk), inc=16)
    P.emit(DMA_S, lambda e: e.dma_start(out=ksb_sb[:], in_=ks_blk), inc=16)
    P.barrier()
    P.emit(DMA_G, lambda e: e.dma_start(out=w1abf.ap(), in_=w1a), inc=16)
    P.emit(DMA_G, lambda e: e.dma_start(out=w1gbf.ap(), in_=w1g), inc=16)
    for c4 in range(4):
        P.emit(DMA_G, lambda e, c4=c4: e.dma_start(
            out=w2bf.ap()[c4 * 704:(c4 + 1) * 704, :],
            in_=w2p[c4 * 704:(c4 + 1) * 704, :]), inc=16)
    gw_cast = P.at(DMA_G)
    P.emit("tensor", lambda e: e.transpose(psT[0:128, 0:21], kdraw[:],
                                           id_f[0:21, 0:21]), inc=1)
    P.emit("vector", lambda e: e.tensor_copy(kdT[:], psT[0:128, 0:21]),
           waits=[P.at("tensor")], inc=1)
    P.barrier()

    def hcol(c, c0, w):
        off = 128 if c < 4 else 127
        return xf[c][:, off + c0: off + c0 + w]

    rst = {"s_sq": P.at("scalar"), "v_qhat": P.at("vector"),
           "t_hs": P.at("tensor"), "v_max": P.at("vector"),
           "t_scale": P.at("tensor"), "v_fin": P.at("vector")}

    def rotary_block(dst, oc, c0, w, qk, tile_cols):
        A = tbuf["a" + qk]
        B = tbuf["b" + qk]
        sblk = qsb_sb if qk == "q" else ksb_sb
        third = 0 if qk == "q" else 1
        pos0 = 128 + tile_cols
        P.emit(DMA_S, lambda e: e.dma_start(
            out=A[:, 0:w], in_=tabs["a" + qk][:, pos0:pos0 + w]),
            waits=[rst["v_fin"]], inc=16)
        P.emit(DMA_S, lambda e: e.dma_start(
            out=B[:, 0:w], in_=tabs["b" + qk][:, pos0:pos0 + w]),
            waits=[rst["v_fin"]], inc=16)
        dmw = P.at(DMA_S)
        P.emit("tensor", lambda e: (
            [e.matmul(psA[:, 0:w], wq_sb[ic][:, third * 512 + oc * 128:
                                             third * 512 + (oc + 1) * 128],
                      hcol(ic, tile_cols, w), start=(ic == 0), stop=(ic == 7))
             for ic in range(8)][-1]),
            waits=[rst["s_sq"], rst["v_qhat"]], inc=1)
        t_mm = P.at("tensor")
        P.emit("scalar", lambda e: e.activation(out=sqt[:, 0:w], in_=psA[:, 0:w],
                                                func=AF.Square),
               waits=[t_mm, rst["t_hs"]], inc=1)
        rst["s_sq"] = P.at("scalar")
        P.emit("tensor", lambda e: e.matmul(psT[0:2, 0:w], headsum[:],
                                            sqt[:, 0:w], start=True, stop=True),
               waits=[rst["s_sq"], rst["v_max"]], inc=1)
        rst["t_hs"] = P.at("tensor")
        P.emit("vector", lambda e: e.tensor_scalar_max(rn[0:2, 0:w],
                                                       psT[0:2, 0:w], 1e-24),
               waits=[rst["t_hs"], rst["t_scale"]], inc=1)
        rst["v_max"] = P.at("vector")
        P.emit("scalar", lambda e: e.activation(out=rn[0:2, 0:w], in_=rn[0:2, 0:w],
                                                func=AF.Sqrt),
               waits=[rst["v_max"]], inc=1)
        s_sqrt = P.at("scalar")
        P.emit("vector", lambda e: e.reciprocal(rn[0:2, 0:w], rn[0:2, 0:w]),
               waits=[s_sqrt], inc=1)
        v_recip = P.at("vector")
        P.emit("tensor", lambda e: e.matmul(psS[0][:, 0:w], sblk[:], rn[0:2, 0:w],
                                            start=True, stop=True),
               waits=[v_recip], inc=1)
        rst["t_scale"] = P.at("tensor")
        P.emit("vector", lambda e: e.tensor_copy(ssb[:, 0:w], psS[0][:, 0:w]),
               waits=[rst["t_scale"]], inc=1)
        P.emit("vector", lambda e: e.tensor_tensor(qhat[:, 0:w], psA[:, 0:w],
                                                   ssb[:, 0:w], ALU.mult),
               chain=True, inc=1)
        rst["v_qhat"] = P.at("vector")
        P.emit("tensor", lambda e: e.matmul(psS[1][:, 0:w],
                                            rotT[:], qhat[:, 0:w],
                                            start=True, stop=True),
               waits=[rst["v_qhat"]], inc=1)
        t_rot = P.at("tensor")
        P.emit("vector", lambda e: e.tensor_tensor(t1[:, 0:w], qhat[:, 0:w],
                                                   A[:, 0:w], ALU.mult),
               waits=[dmw], chain=True, inc=1)
        P.emit("vector", lambda e: e.tensor_tensor(t2[:, 0:w], psS[1][:, 0:w],
                                                   B[:, 0:w], ALU.mult),
               waits=[t_rot], chain=True, inc=1)
        P.emit("vector", lambda e, dst=dst, oc=oc, c0=c0: e.tensor_tensor(
            dst[oc][:, c0:c0 + w], t1[:, 0:w], t2[:, 0:w], ALU.add),
            chain=True, inc=1)
        rst["v_fin"] = P.at("vector")

    for oc in range(4):
        c0 = 0
        while c0 < NATT:
            w = min(512, NATT - c0)
            rotary_block(qr, oc, c0, w, "q", 512 + c0)
            c0 += w
    for oc in range(4):
        c0 = 0
        while c0 < NKV:
            w = min(512, NKV - c0)
            rotary_block(kr, oc, c0, w, "k", c0)
            c0 += w

    v_vt = {-1: P.at("vector"), -2: P.at("vector")}
    for t in range(21):
        pa = (t % 2) * 512
        P.emit("tensor", lambda e, t=t, pa=pa: (
            [e.matmul(psA[:, pa:pa + 512],
                      hcol(ic, t * 128, 128),
                      wq_sb[ic][:, 1024:1536],
                      start=(ic == 0), stop=(ic == 7)) for ic in range(8)][-1]),
            waits=[v_vt[t - 2]] +
                  ([P.at("vector"), P.at("scalar")] if t == 0 else []),
            inc=1)
        t_mm = P.at("tensor")
        P.emit("vector", lambda e, t=t, pa=pa: e.tensor_copy(
            v_sb[t][:].rearrange("p (h c) -> p h c", c=65)[:, :, 0:64],
            psA[:, pa:pa + 512].rearrange("p (h c) -> p h c", c=64)),
            waits=[t_mm], inc=1)
        P.emit("vector", lambda e, t=t: e.memset(
            v_sb[t][:].rearrange("p (h c) -> p h c", c=65)[:, :, 64:65], 1.0),
            inc=1)
        v_vt[t] = P.at("vector")
    P.barrier()

    for m in range(8):
        for s in range(4):
            ms = m - s
            dst = lm_sb[:, m * 512 + s * 128:m * 512 + (s + 1) * 128]
            if ms == 0:
                P.emit("vector", lambda e, dst=dst: e.tensor_copy(
                    dst, consts["tri_hi"][:]), inc=1)
            elif ms == 4:
                P.emit("vector", lambda e, dst=dst: e.tensor_copy(
                    dst, consts["tri_lo"][:]), inc=1)
            elif 1 <= ms <= 3:
                P.emit("vector", lambda e, dst=dst: e.memset(dst, 0.0), inc=1)
            else:
                P.emit("vector", lambda e, dst=dst: e.memset(dst, NEG), inc=1)

    ao_of = 0

    def lattn_unit(hp, hh, h, qt0, G):
        nm = G + 4
        QW = G * 128
        q_ap = qr[hp][hh:hh + 64, qt0 * 128:(qt0 + G) * 128]
        v0 = P.at("vector")
        v_red = {-1: v0, -2: v0}
        for s in range(G):
            po = (s % 2) * 1024
            kw = nm * 128
            qs = qr[hp][hh:hh + 64, (qt0 + s) * 128:(qt0 + s + 1) * 128]
            P.emit("tensor", lambda e, qs=qs, po=po, kw=kw, hp=hp, hh=hh: (
                [e.matmul(psA[:, po + m5 * 512:po + m5 * 512 +
                              min(512, kw - m5 * 512)],
                          qs, kr[hp][hh:hh + 64,
                                     qt0 * 128 + m5 * 512:
                                     qt0 * 128 + m5 * 512 +
                                     min(512, kw - m5 * 512)],
                          start=True, stop=True)
                 for m5 in range((kw + 511) // 512)][-1]),
                waits=[v_red[s - 2]], inc=1)
            t_mm = P.at("tensor")
            P.emit("vector", lambda e, s=s, po=po, kw=kw: e.reduce_max(
                msc[:, s:s + 1], psA[:, po:po + kw], axis=AX.X),
                waits=[t_mm], inc=1)
            v_red[s] = P.at("vector")
        P.emit("tensor", lambda e, G=G: e.transpose(
            psT[0:G, 0:128], msc[:, 0:G], id_f[:]),
            waits=[P.at("vector")], inc=1)
        P.emit("vector", lambda e, G=G: e.reduce_max(
            msc[0:G, 8:9], psT[0:G, 0:128], axis=AX.X),
            waits=[P.at("tensor")], inc=1)
        P.emit("tensor", lambda e, G=G: e.transpose(
            psT[0:1, 0:G], msc[0:G, 8:9], id_f[0:G, 0:G]),
            waits=[P.at("vector")], inc=1)
        P.emit("vector", lambda e, G=G: e.reduce_max(
            msc[0:1, 8:9], psT[0:1, 0:G], axis=AX.X),
            waits=[P.at("tensor")], inc=1)
        P.emit("tensor", lambda e: e.matmul(psT[0:128, 0:1], ones_f[0:1, 0:128],
                                            msc[0:1, 8:9], start=True,
                                            stop=True),
               waits=[P.at("vector")], inc=1)
        P.emit("vector", lambda e: e.tensor_scalar_mul(negM[:],
                                                       psT[0:128, 0:1], -8.0),
               waits=[P.at("tensor")], inc=1)
        P.emit("vector", lambda e, qt0=qt0, nm=nm: e.tensor_scalar(
            biasu[:, 0:nm], kdT[:, qt0:qt0 + nm], negM[0:128, 0:1], None,
            ALU.add), chain=True, inc=1)
        v_bias = P.at("vector")
        s_prev = P.at("scalar")
        t_S = {}
        t_AV = {}
        s_exp = {}

        def emit_S(m):
            w = [s_prev, v_bias] if m < 2 else []
            P.emit("tensor", lambda e, m=m, q_ap=q_ap, hp=hp, hh=hh: e.matmul(
                psS[m % 2][:, 0:QW],
                kr[hp][hh:hh + 64, (qt0 + m) * 128:(qt0 + m + 1) * 128],
                q_ap, start=True, stop=True), waits=w, inc=1)
            t_S[m] = P.at("tensor")

        emit_S(0)
        for m in range(nm):
            buf = m % 2
            if m + 1 < nm:
                emit_S(m + 1)
            ew = [t_S[m]]
            if G == 4:
                P.emit("vector", lambda e, m=m, buf=buf: e.tensor_tensor(
                    psS[buf][:, 0:QW], psS[buf][:, 0:QW],
                    lm_sb[:, m * 512:m * 512 + QW], ALU.add),
                    waits=[t_S[m]], inc=1)
                ew.append(P.at("vector"))
            elif m == 0 or m == nm - 1:
                tri = consts["tri_hi"] if m == 0 else consts["tri_lo"]
                P.emit("vector", lambda e, m=m, buf=buf, tri=tri: e.tensor_tensor(
                    psS[buf][:, 0:QW], psS[buf][:, 0:QW], tri[:], ALU.add),
                    waits=[t_S[m]], inc=1)
                ew.append(P.at("vector"))
            if m >= 2:
                ew.append(t_AV[m - 2])
            P.emit("scalar", lambda e, m=m, buf=buf: e.activation(
                out=pt_sb[buf][:, 0:QW], in_=psS[buf][:, 0:QW], func=AF.Exp,
                bias=biasu[:, m:m + 1], scale=8.0),
                waits=ew, inc=1)
            s_exp[m] = P.at("scalar")
            P.emit("tensor", lambda e, m=m, buf=buf, h=h, nm=nm: e.matmul(
                psV[0:65, 0:QW], v_sb[qt0 + m][:, h * 65:h * 65 + 65],
                pt_sb[buf][:, 0:QW], start=(m == 0), stop=(m == nm - 1)),
                waits=[s_exp[m]], inc=1)
            t_AV[m] = P.at("tensor")
        P.emit("scalar", lambda e: e.activation(
            out=msc[64:65, 512:512 + QW], in_=psV[64:65, 0:QW],
            func=AF.Ln, bias=consts["eps_col"][64:65, 0:1]),
            waits=[P.at("tensor")], inc=1)
        s_l = P.at("scalar")
        P.emit("tensor", lambda e: e.matmul(psT[0:64, 0:QW],
                                            ones_f[64:65, 0:64],
                                            msc[64:65, 512:512 + QW],
                                            start=True, stop=True),
               waits=[s_l], inc=1)
        t_bc = P.at("tensor")
        P.emit("scalar", lambda e: e.activation(
            out=rb_sb[0:64, 0:QW], in_=psT[0:64, 0:QW],
            func=AF.Exp, scale=-1.0),
            waits=[t_bc], inc=1)
        s_rb = P.at("scalar")
        P.emit("vector", lambda e, hp=hp, hh=hh, qt0=qt0: e.tensor_tensor(
            xf[hp][hh:hh + 64, qt0 * 128:qt0 * 128 + QW],
            psV[0:64, 0:QW], rb_sb[0:64, 0:QW], ALU.mult),
            waits=[s_rb], inc=1)

    for qt0, G in ((0, 4), (4, 4), (8, 4), (12, 4), (16, 1)):
        for h in range(8):
            lattn_unit(h // 2, (h % 2) * 64, h, qt0, G)
    P.barrier()

    for ic in range(4):
        P.emit(DMA_G, lambda e, ic=ic: e.dma_start(
            out=wq_sb[ic][:, 0:1024], in_=wo[ic * 128:(ic + 1) * 128, :]),
            waits=[P.at("tensor")], inc=16)
    P.barrier()
    v_add = {-1: P.at("vector"), -2: P.at("vector")}
    st_x = {-1: P.at(DMA_S), -2: P.at(DMA_S)}
    for tq in range(17):
        pa = (tq % 2) * 1024
        P.emit(DMA_G, lambda e, tq=tq: e.indirect_dma_start(
            out=xo[tq % 2][:], out_offset=None, in_=x_src[:, :],
            in_offset=IndirectOffsetOnAxis(idxh_sb[:, 5 + tq:6 + tq], 0)),
            waits=[v_add[tq - 2],
                   ccw_halo[5 + tq] if ccw_halo else ccw], inc=16)
        dmw = P.at(DMA_G)
        P.emit("tensor", lambda e, tq=tq, pa=pa: (
            [e.matmul(psA[:, pa + half * 512:pa + (half + 1) * 512],
                      xf[ic][:, tq * 128:(tq + 1) * 128],
                      wq_sb[ic][:, half * 512:(half + 1) * 512],
                      start=(ic == 0), stop=(ic == 3))
             for half in range(2) for ic in range(4)][-1]),
            waits=[v_add[tq - 2]] +
                  ([P.at("vector"), P.at("scalar")] if tq == 0 else []),
            inc=1)
        t_mm = P.at("tensor")
        P.emit("vector", lambda e, tq=tq, pa=pa: e.tensor_tensor(
            xn[tq % 2][:], psA[:, pa:pa + 1024], xo[tq % 2][:], ALU.add),
            waits=[t_mm, dmw, st_x[tq - 2]], inc=1)
        v_add[tq] = P.at("vector")
        P.emit(DMA_S, lambda e, tq=tq: e.dma_start(
            out=xnew_scr.ap()[tq], in_=xn[tq % 2][:]),
            waits=[v_add[tq]], inc=16)
        st_x[tq] = P.at(DMA_S)
    P.barrier()

    stg2 = [psT[0:128, 0:512], psS[0][:, 0:512]]
    v_bc = P.at("vector")
    t_bc = {-1: P.at("tensor"), -2: P.at("tensor")}
    v_stg = {0: v_bc, 1: v_bc}
    bi = 0
    for (vec, dst) in ((ln_g, bc_g), (ln_b, bc_b), (b2, bc_b2)):
        mr = 0 if bi == 0 else 64
        P.emit(DMA_S, lambda e, vec=vec, mr=mr: e.dma_start(
            out=msc[mr:mr + 1, 0:1024], in_=vec.unsqueeze(0)),
            waits=[P.at("tensor")] if bi == 0 else [], inc=16)
        dw = P.at(DMA_S)
        for half in range(2):
            sb = (2 * bi + half) % 2
            P.emit("tensor", lambda e, half=half, mr=mr, sb=sb: e.matmul(
                stg2[sb][:], ones_f[mr:mr + 1, 0:128],
                msc[mr:mr + 1, half * 512:(half + 1) * 512],
                start=True, stop=True),
                waits=[dw, v_stg[sb]],
                inc=1)
            t_bc[2 * bi + half] = P.at("tensor")
            P.emit("vector", lambda e, dst=dst, half=half, sb=sb: e.tensor_copy(
                dst[:, half * 512:(half + 1) * 512], stg2[sb][:]),
                waits=[t_bc[2 * bi + half]], inc=1)
            v_stg[sb] = P.at("vector")
        bi = (bi + 1) % 2
    v_bc2 = P.at("vector")
    t_b1 = P.at("tensor")
    for ti2, (vec, dst) in enumerate(((b1a, b1aT), (b1g, b1gT))):
        P.emit(DMA_S, lambda e, vec=vec: e.dma_start(
            out=bT_raw[:], in_=vec.rearrange("(c p) -> c p", p=128)),
            waits=[t_b1], inc=16)
        dw = P.at(DMA_S)
        P.emit("tensor", lambda e, ti2=ti2: e.transpose(
            psT[0:128, 128 + ti2 * 32:128 + ti2 * 32 + 22],
            bT_raw[:], id_f[0:22, 0:22]),
            waits=[dw] + ([v_bc2] if ti2 == 0 else []), inc=1)
        t_b1 = P.at("tensor")
        P.emit("vector", lambda e, dst=dst, ti2=ti2: e.tensor_copy(
            dst[:], psT[0:128, 128 + ti2 * 32:128 + ti2 * 32 + 22]),
            waits=[t_b1], inc=1)
    P.barrier()

    h2b2 = [h2, AL.at(qhat_off, [128, 1024], F32)]
    xa2b = [xa, AL.at(t2_off, [128, 1024], F32)]
    pst2 = [psTb[0:128, 0:1024], psS[0].bitcast(BF16)[:, 0:1024]]
    t_sh = {-1: P.at("tensor"), 0: P.at("tensor")}
    t_tr = {-1: P.at("tensor"), 0: P.at("tensor")}
    v_h2cp = {-1: P.at("vector"), 0: P.at("vector")}
    v_cast = {-1: P.at("vector"), 0: P.at("vector")}
    v_xfcp = {-1: P.at("vector"), 0: P.at("vector")}
    s_act = {-1: P.at("scalar"), 0: P.at("scalar")}
    for to in range(1, 17):
        tb = to % 2
        P.emit(DMA_S, lambda e, to=to: e.dma_start(
            out=xn[to % 2][:], in_=xnew_scr.ap()[to]),
            waits=[t_sh[to - 1], v_h2cp[to - 2]] +
                  ([P.at("vector")] if to == 1 else []),
            inc=16)
        if to == 1:
            P.emit(DMA_S, lambda e: e.dma_start(out=xn[0][:],
                                                in_=xnew_scr.ap()[0]),
                   inc=16)
        dmw = P.at(DMA_S)
        po = tb * 512
        P.emit("tensor", lambda e, to=to, po=po: (
            e.matmul(psA[:, po:po + 512], consts["shift_up"][:],
                     xn[to % 2][:, 512:1024], start=True, stop=False),
            e.matmul(psA[:, po:po + 512], consts["last2first"][:],
                     xn[(to - 1) % 2][:, 512:1024], start=False, stop=True))[-1],
            waits=[dmw, v_h2cp[to - 2]], inc=1)
        t_sh[to] = P.at("tensor")
        P.emit("vector", lambda e, to=to, tb=tb: e.tensor_copy(
            h2b2[tb][:, 0:512], xn[to % 2][:, 0:512]),
            waits=[dmw, s_act[to - 2]], inc=1)
        P.emit("vector", lambda e, tb=tb, po=po: e.tensor_copy(
            h2b2[tb][:, 512:1024], psA[:, po:po + 512]),
            waits=[t_sh[to]], inc=1)
        v_h2cp[to] = P.at("vector")
        P.emit("vector", lambda e, tb=tb: e.bn_stats(ln_s[:, 8:14],
                                                     h2b2[tb][:, 0:512]),
               chain=True, inc=1)
        P.emit("vector", lambda e, tb=tb: e.bn_stats(ln_s[:, 14:20],
                                                     h2b2[tb][:, 512:1024]),
               chain=True, inc=1)
        P.emit("vector", lambda e: e.bn_aggr(ln_s[:, 0:2], ln_s[:, 8:20]),
               chain=True, inc=1)
        P.emit("vector", lambda e: e.tensor_scalar_add(ln_s[:, 2:3],
                                                       ln_s[:, 1:2], 1e-5),
               chain=True, inc=1)
        v_ln = P.at("vector")
        P.emit("scalar", lambda e: e.activation(out=ln_s[:, 3:4],
                                                in_=ln_s[:, 2:3], func=AF.Sqrt),
               waits=[v_ln], inc=1)
        s_sq4 = P.at("scalar")
        P.emit("vector", lambda e: e.reciprocal(ln_s[:, 4:5], ln_s[:, 3:4]),
               waits=[s_sq4], inc=1)
        P.emit("vector", lambda e: e.tensor_tensor(ln_s[:, 5:6], ln_s[:, 0:1],
                                                   ln_s[:, 4:5], ALU.mult),
               chain=True, inc=1)
        P.emit("vector", lambda e: e.tensor_scalar_mul(ln_s[:, 5:6],
                                                       ln_s[:, 5:6], -1.0),
               chain=True, inc=1)
        v_fin = P.at("vector")
        P.emit("scalar", lambda e, tb=tb: e.activation(
            out=xa2b[tb][:], in_=h2b2[tb][:], func=AF.Identity,
            bias=ln_s[:, 5:6], scale=ln_s[:, 4:5]),
            waits=[v_fin, v_cast[to - 2]], inc=1)
        s_act[to] = P.at("scalar")
        P.emit("vector", lambda e, tb=tb: e.tensor_tensor(
            xa2b[tb][:], xa2b[tb][:], bc_g[:], ALU.mult),
            waits=[s_act[to]], inc=1)
        P.emit("vector", lambda e, tb=tb: e.tensor_tensor(
            xa2b[tb][:], xa2b[tb][:], bc_b[:], ALU.add), chain=True, inc=1)
        P.emit("vector", lambda e, tb=tb: e.tensor_copy(
            ag2[tb][:], xa2b[tb][:, 0:512]),
            waits=[t_tr[to - 2]], chain=True, inc=1)
        P.emit("vector", lambda e, tb=tb: e.tensor_copy(
            sg2[tb][:], xa2b[tb][:, 512:1024]),
            chain=True, inc=1)
        v_cast[to] = P.at("vector")
        P.emit("tensor", lambda e, tb=tb: (
            [e.transpose(pst2[tb][:, ic * 128:(ic + 1) * 128],
                         (ag2[tb] if ic < 4 else sg2[tb])[:, (ic % 4) * 128:
                                                          (ic % 4) * 128 + 128],
                         id_b) for ic in range(8)][-1]),
            waits=[v_cast[to], v_xfcp[to - 2]], inc=1)
        t_tr[to] = P.at("tensor")
        P.emit("vector", lambda e, to=to, tb=tb: (
            [e.tensor_copy(xf[ic][:, (to - 1) * 128:to * 128],
                           pst2[tb][:, ic * 128:(ic + 1) * 128])
             for ic in range(8)][-1]),
            waits=[t_tr[to]], inc=1)
        v_xfcp[to] = P.at("vector")
    P.barrier()

    w1ab3 = [wq_sb[0][:, 0:1024], wq_sb[1][:, 0:1024], wq_sb[6][:, 0:1024]]
    w1gb3 = [wq_sb[2][:, 0:1024], wq_sb[3][:, 0:1024], wq_sb[7][:, 0:1024]]
    ag2 = [ag_sb, ag_b]
    sg2 = [sqt.bitcast(BF16)[:, 0:512], sg_b]
    for g in range(4):
        gcol = g * 512
        t0 = P.at("tensor")
        wtoks = {}
        ctoks = {-1: t0, -2: t0, -3: t0}
        s_g = {-1: P.at("scalar"), -2: P.at("scalar")}
        v_m = {-1: P.at("vector"), -2: P.at("vector")}
        h1slots = [(psA[:, 0:512], psA[:, 512:1024]),
                   (psA[:, 1024:1536], psA[:, 1536:2048]),
                   (psS[0][:, 0:512], psS[1][:, 0:512]),
                   (psV[0:128, 0:512], psT[0:128, 0:512])]
        s_g.update({-3: s_g[-1], -4: s_g[-1]})
        for j in range(22 + 2):
            if j < 22:
                jb = j % 3
                vq = DMA_G if j % 2 == 0 else DMA_S
                P.emit(vq, lambda e, j=j, jb=jb: e.dma_start(
                    out=w1ab3[jb], in_=w1abf.ap()[:, j * 1024:(j + 1) * 1024]),
                    waits=[ctoks[j - 3], gw_cast if j < 3 else []], inc=16)
                P.emit(vq, lambda e, j=j, jb=jb: e.dma_start(
                    out=w1gb3[jb], in_=w1gbf.ap()[:, j * 1024:(j + 1) * 1024]),
                    waits=[ctoks[j - 3]], inc=16)
                wtoks[j] = P.at(vq)
            if j < 2:
                continue
            jj = j - 2
            jb = jj % 3
            sa, sgp = h1slots[jj % 4]
            gw = wtoks[jj]
            P.emit("tensor", lambda e, jj=jj, jb=jb, gcol=gcol, sa=sa: (
                [e.matmul(sa,
                          w1ab3[jb][:, ic * 128:(ic + 1) * 128],
                          xf[ic][:, gcol:gcol + 512],
                          start=(ic == 0), stop=(ic == 7)) for ic in range(8)][-1]),
                waits=[gw, s_g[jj - 4]] +
                      ([P.at("vector")] if jj < 4 and g == 0 else
                       [P.at("vector")] if jj < 2 else []),
                inc=1)
            P.emit("tensor", lambda e, jj=jj, jb=jb, gcol=gcol, sgp=sgp: (
                [e.matmul(sgp,
                          w1gb3[jb][:, ic * 128:(ic + 1) * 128],
                          xf[ic][:, gcol:gcol + 512],
                          start=(ic == 0), stop=(ic == 7)) for ic in range(8)][-1]),
                inc=1)
            ctoks[jj] = P.at("tensor")
            P.emit("scalar", lambda e, j=jj, sa=sa: e.activation(
                out=ag2[j % 2][:], in_=sa, func=AF.Identity,
                bias=b1aT[:, j:j + 1], scale=1.0),
                waits=[ctoks[jj], v_m[jj - 2]], inc=1)
            P.emit("scalar", lambda e, j=jj, sgp=sgp: e.activation(
                out=sg2[j % 2][:], in_=sgp,
                func=AF.Gelu, bias=b1gT[:, j:j + 1], scale=1.0),
                chain=True, inc=1)
            s_g[jj] = P.at("scalar")
            P.emit("vector", lambda e, j=jj: e.tensor_tensor(
                hgt(j), ag2[j % 2][:], sg2[j % 2][:], ALU.mult),
                waits=[s_g[jj]], inc=1)
            v_m[jj] = P.at("vector")
        for hgp in range(2):
            w2toks = {-1: P.at("tensor"), -2: P.at("tensor")}
            for j in range(22 + 1):
                if j < 22:
                    vq2 = DMA_S if j % 2 == 0 else DMA_G
                    P.emit(vq2, lambda e, j=j: e.dma_start(
                        out=w2buf2[j % 2],
                        in_=w2bf.ap()[j * 128:(j + 1) * 128, :]),
                        waits=[w2toks.get(j - 2, []),
                               gw_cast if j < 2 else []], inc=16)
                    w2toks[j] = P.at(vq2)
                if j == 0:
                    continue
                jj = j - 1
                gw2 = w2toks[jj]
                for ts2 in range(2):
                    ts = hgp * 2 + ts2
                    for half in range(2):
                        P.emit("tensor", lambda e, j=jj, ts=ts, ts2=ts2,
                               half=half: e.matmul(
                                   psA[:, ts2 * 1024 + half * 512:
                                       ts2 * 1024 + (half + 1) * 512],
                                   hgt(j)[:, ts * 128:(ts + 1) * 128],
                                   w2buf2[j % 2][:, half * 512:(half + 1) * 512],
                                   start=(j == 0), stop=(j == 21)),
                               waits=[gw2] + ([P.at("vector"), s_g[21]]
                                              if j == 1 else []),
                               inc=1)
                w2toks[jj] = P.at("tensor")
            for ts2 in range(2):
                ts = hgp * 2 + ts2
                to = g * 4 + ts + 1
                P.emit(DMA_S, lambda e, to=to: e.dma_start(
                    out=xn[to % 2][:], in_=xnew_scr.ap()[to]),
                    waits=[P.at("vector")], inc=16)
                dmw = (P.at(DMA_S),)
                if last:
                    P.emit(DMA_G, lambda e, to=to: e.indirect_dma_start(
                        out=xo[to % 2][:], out_offset=None, in_=y1_src[:, :],
                        in_offset=IndirectOffsetOnAxis(
                            idxown_sb[:, to - 1:to], 0)),
                        waits=[P.at("vector"),
                               ccw_own1[to - 1] if ccw_own1 else ccw1],
                        inc=16)
                dmw = (P.at(DMA_S), P.at(DMA_G))
                P.emit("vector", lambda e, ts2=ts2, to=to: e.tensor_tensor(
                    xa[:], psA[:, ts2 * 1024:(ts2 + 1) * 1024], xn[to % 2][:],
                    ALU.add),
                    waits=[P.at("tensor")] + list(dmw), inc=1)
                P.emit("vector", lambda e: e.tensor_tensor(xa[:], xa[:],
                                                           bc_b2[:], ALU.add),
                       chain=True, inc=1)
                if last:
                    P.emit("vector", lambda e, to=to: e.tensor_tensor(
                        xa[:], xa[:], xo[to % 2][:], ALU.add),
                        chain=True, inc=1)
                    P.emit("vector", lambda e: e.bn_stats(ln_s[:, 8:14], xa[:, 0:512]),
                           waits=[], chain=True, inc=1)
                    P.emit("vector", lambda e: e.bn_stats(ln_s[:, 14:20], xa[:, 512:1024]),
                           chain=True, inc=1)
                    P.emit("vector", lambda e: e.bn_aggr(ln_s[:, 0:2], ln_s[:, 8:20]),
                           chain=True, inc=1)
                    P.emit("vector", lambda e: e.tensor_scalar_add(ln_s[:, 2:3],
                                                                   ln_s[:, 1:2], 1e-5),
                           chain=True, inc=1)
                    P.emit("scalar", lambda e: e.activation(out=ln_s[:, 3:4],
                                                            in_=ln_s[:, 2:3], func=AF.Sqrt),
                           waits=[P.at("vector")], inc=1)
                    P.emit("vector", lambda e: e.reciprocal(ln_s[:, 4:5], ln_s[:, 3:4]),
                           waits=[P.at("scalar")], inc=1)
                    P.emit("vector", lambda e: e.tensor_tensor(ln_s[:, 5:6], ln_s[:, 0:1],
                                                               ln_s[:, 4:5], ALU.mult),
                           chain=True, inc=1)
                    P.emit("vector", lambda e: e.tensor_scalar_mul(ln_s[:, 5:6],
                                                                   ln_s[:, 5:6], -1.0),
                           chain=True, inc=1)
                    P.emit("scalar", lambda e: e.activation(
                        out=y_sb[:], in_=xa[:], func=AF.Identity,
                        bias=ln_s[:, 5:6], scale=ln_s[:, 4:5]),
                        waits=[P.at("vector"), P.at(DMA_S)], inc=1)
                    P.emit(DMA_S, lambda e, to=to: e.dma_start(
                        out=y[(to - 1) * 128:to * 128, :], in_=y_sb[:]),
                        waits=[P.at("scalar")], inc=16)
                else:
                    P.emit(DMA_S, lambda e, to=to: e.dma_start(
                        out=y[(to - 1) * 128:to * 128, :], in_=xa[:]),
                        waits=[P.at("vector")], inc=16)
        if ccp is not None:
            ccp(P, g, P.at(DMA_S))
    P.emit(DMA_S, lambda e: None, waits=[P.at(DMA_S)])




def emit_stage_lstm(nc, P, AL, consts, D):
    NT = 512
    x_src = D["x_src"]
    wnames = ["wf", "uf", "wi", "ui", "wc", "uc", "wo", "uo"]
    wd = {n: D[n] for n in wnames}
    vnames = ["bf_v", "bi_v", "bc_v", "bo_v", "in_c"]
    vd = {n: D[n] for n in vnames}
    y = D["ydst"]
    id_b = consts["id_bf"][:]
    id_f = consts["id_f32"]
    ones_f = consts["ones_f32"]
    ccw = D.get("ccw") or []

    idxl_sb = AL("idxl_sb", [128, 16], I32)
    P.emit("sync", lambda e: e.dma_start(out=idxl_sb, in_=D["idx_lstm"]), inc=16)
    xf = [[AL(f"xf{t}_{i}", [128, NT], BF16) for i in range(8)]
          for t in range(4)]
    wT = [[AL(f"w{n}_{i}", [128, 1024], BF16)
           for i in range(8)] for n in range(6)]
    c_a = [AL(f"ca{i}", [128, NT], F32) for i in range(8)]
    c_b = [AL(f"cb{i}", [128, NT], F32) for i in range(8)]
    fga = AL("fga", [128, NT], F32)
    fgb = AL("fgb", [128, NT], F32)
    fgc = AL("fgc", [128, NT], F32)
    wrow = AL("wrow", [128, 1024], BF16)
    bT = AL("bT", [128, 8 * 5], F32)
    bT_raw = AL("bT_raw", [8, 128], F32)
    bc_bo = AL("bc_bo", [128, 1024], F32)
    msc = AL("msc", [128, 1024], F32)
    xo = [AL(f"xo{i}", [128, 1024], F32) for i in range(2)]
    xa = AL("xa", [128, 1024], F32)
    ln_s = AL("ln_s", [128, 24], F32)
    y_sb = AL("y_sb", [128, 1024], F32)
    sq2 = y_sb
    th = y_sb

    cb16 = [AL(f"cb16_{i}", [128, NT], BF16) for i in range(8)]
    psA = D["psA"]
    psT = D["psT"]
    psTb = psT.bitcast(BF16)

    DMA_G, DMA_S = "gpsimd", "sync"

    def load_wT(dst, name):
        for ic in range(8):
            P.emit(DMA_G, lambda e, dst=dst, ic=ic, name=name: e.dma_start(
                out=dst[ic][:],
                in_=wd[name][ic * 128:(ic + 1) * 128, :]),
                waits=[P.at("tensor")], inc=16)
    for n6 in range(6):
        load_wT(wT[n6], wnames[n6])
    gw_w6 = P.at(DMA_G)

    xs2 = [wrow, msc.bitcast(BF16)[:, 0:1024]]
    pss2 = [psTb[0:128, 0:1024], D["psS"][0].bitcast(BF16)[:, 0:1024]]
    t_tr = {-1: P.at("tensor"), -2: P.at("tensor")}
    v_cp = {-1: P.at("vector"), -2: P.at("vector")}
    for t in range(4):
        for tt in range(NT // 128):
            ti = t * 4 + tt
            tb = ti % 2
            w = [t_tr[ti - 2], ccw]
            if ti == 0:
                w += [P.at("sync"), P.at("vector"), P.at("tensor")]
            P.emit(DMA_G, lambda e, ti=ti, tb=tb: e.indirect_dma_start(
                out=xs2[tb][:], out_offset=None, in_=x_src[:, :],
                in_offset=IndirectOffsetOnAxis(idxl_sb[:, ti:ti + 1], 0)),
                waits=w, inc=16)
            gw = P.at(DMA_G)
            P.emit("tensor", lambda e, tb=tb: (
                [e.transpose(pss2[tb][:, ic * 128:(ic + 1) * 128],
                             xs2[tb][:, ic * 128:(ic + 1) * 128], id_b)
                 for ic in range(8)][-1]),
                waits=[gw, v_cp[ti - 2]], inc=1)
            t_tr[ti] = P.at("tensor")
            P.emit("vector", lambda e, t=t, tt=tt, tb=tb: (
                [e.tensor_copy(xf[t][ic][:, tt * 128:(tt + 1) * 128],
                               pss2[tb][:, ic * 128:(ic + 1) * 128])
                 for ic in range(8)][-1]),
                waits=[t_tr[ti]], inc=1)
            v_cp[ti] = P.at("vector")

    for vi, n in enumerate(vnames):
        P.emit(DMA_S, lambda e, n=n: e.dma_start(
            out=bT_raw[:], in_=vd[n].rearrange("(c p) -> c p", p=128)),
            waits=[P.at("tensor")], inc=16)
        P.barrier()
        P.emit("tensor", lambda e: e.transpose(psT[0:128, 0:8], bT_raw[:],
                                               id_f[0:8, 0:8]), inc=1)
        P.emit("vector", lambda e, vi=vi: e.tensor_copy(
            bT[:, vi * 8:(vi + 1) * 8], psT[0:128, 0:8]),
            waits=[P.at("tensor")], inc=1)
        P.barrier()
    P.emit(DMA_S, lambda e: e.dma_start(out=msc[0:1, 0:1024],
                                        in_=vd["bo_v"].unsqueeze(0)),
           waits=[P.at("tensor")], inc=16)
    P.barrier()
    for half in range(2):
        P.emit("tensor", lambda e, half=half: e.matmul(
            psT[0:128, 0:512], ones_f[0:1, 0:128],
            msc[0:1, half * 512:(half + 1) * 512], start=True, stop=True),
            waits=[P.at("vector")], inc=1)
        P.emit("vector", lambda e, half=half: e.tensor_copy(
            bc_bo[:, half * 512:(half + 1) * 512], psT[0:128, 0:512]),
            waits=[P.at("tensor")], inc=1)
    P.barrier()

    P.emit("tensor", lambda e: None, waits=[gw_w6])
    P.barrier()

    for ic in range(8):
        P.emit("vector", lambda e, ic=ic: e.memset(c_a[ic][:], 0.0), inc=1)
        P.emit("vector", lambda e, ic=ic: e.tensor_scalar_add(
            c_a[ic][:], c_a[ic][:], bT[:, 4 * 8 + ic:4 * 8 + ic + 1]),
            waits=[P.at("vector")], inc=1)
    P.barrier()

    s_bank = {0: P.at("scalar"), 1: P.at("scalar"), 2: P.at("scalar")}
    v_cup = P.at("vector")
    for t in range(4):
        c_src = c_a if t % 2 == 0 else c_b
        c_dst = c_b if t % 2 == 0 else c_a
        for ic in range(8):
            P.emit("vector", lambda e, ic=ic, c_src=c_src: e.tensor_copy(
                cb16[ic][:], c_src[ic][:]),
                waits=[P.at("tensor")] if ic == 0 else [], chain=True, inc=1)
        v_cb = P.at("vector")
        for oc in range(8):
            for gi, (wn, un, bvi, fn, dst) in enumerate((
                    (0, 1, 0, AF.Sigmoid, fga),
                    (2, 3, 1, AF.Sigmoid, fgb),
                    (4, 5, 2, AF.Tanh, fgc))):
                w = [s_bank[gi]] + ([v_cb] if oc == 0 and gi == 0 else [])
                P.emit("tensor", lambda e, t=t, oc=oc, gi=gi, wn=wn, un=un: (
                    [e.matmul(psA[:, gi * 512:(gi + 1) * 512],
                              wT[wn][ic][:, oc * 128:(oc + 1) * 128],
                              xf[t][ic][:], start=(ic == 0), stop=False)
                     for ic in range(8)] +
                    [e.matmul(psA[:, gi * 512:(gi + 1) * 512],
                              wT[un][ic][:, oc * 128:(oc + 1) * 128],
                              cb16[ic][:], start=False, stop=(ic == 7))
                     for ic in range(8)])[-1],
                    waits=w, inc=1)
                t_mm = P.at("tensor")
                P.emit("scalar", lambda e, gi=gi, oc=oc, bvi=bvi, fn=fn,
                       dst=dst: e.activation(
                           out=dst[:], in_=psA[:, gi * 512:(gi + 1) * 512],
                           func=fn, bias=bT[:, bvi * 8 + oc:bvi * 8 + oc + 1],
                           scale=1.0),
                       waits=[t_mm, v_cup], inc=1)
                s_bank[gi] = P.at("scalar")
            s3 = P.at("scalar")
            P.emit("vector", lambda e, oc=oc, c_src=c_src, c_dst=c_dst:
                   e.tensor_tensor(c_dst[oc][:], fga[:], c_src[oc][:],
                                   ALU.mult),
                   waits=[s3], inc=1)
            P.emit("vector", lambda e: e.tensor_tensor(fgb[:], fgb[:], fgc[:],
                                                       ALU.mult),
                   chain=True, inc=1)
            P.emit("vector", lambda e, oc=oc, c_dst=c_dst: e.tensor_tensor(
                c_dst[oc][:], c_dst[oc][:], fgb[:], ALU.add),
                chain=True, inc=1)
            v_cup = P.at("vector")
    c_fin = c_a if 4 % 2 == 0 else c_b

    load_wT(wT[0], "wo")
    load_wT(wT[1], "uo")
    P.barrier()

    for ic in range(8):
        P.emit("vector", lambda e, ic=ic: e.tensor_copy(cb16[ic][:],
                                                        c_fin[ic][:]),
               waits=[P.at("tensor")] if ic == 0 else [], inc=1)
    P.barrier()
    v_xa = {-1: P.at("vector"), -2: P.at("vector")}
    v_xo = {-1: P.at("vector"), -2: P.at("vector")}
    s_y = {-1: P.at("scalar")}
    st = {-1: P.at(DMA_S)}
    for t in range(4):
        for tt in range(NT // 128):
            ti = t * 4 + tt
            pa = (ti % 2) * 1024
            P.emit(DMA_G, lambda e, ti=ti: e.indirect_dma_start(
                out=xo[ti % 2][:], out_offset=None, in_=x_src[:, :],
                in_offset=IndirectOffsetOnAxis(idxl_sb[:, ti:ti + 1], 0)),
                waits=[v_xo[ti - 2], ccw] +
                      ([P.at("vector")] if ti < 2 else []),
                inc=16)
            dmw = P.at(DMA_G)
            P.emit("tensor", lambda e, t=t, tt=tt, pa=pa: (
                [e.matmul(psA[:, pa + half * 512:pa + (half + 1) * 512],
                          xf[t][ic][:, tt * 128:(tt + 1) * 128],
                          wT[0][ic][:, half * 512:(half + 1) * 512],
                          start=(ic == 0), stop=False)
                 for half in range(2) for ic in range(8)] +
                [e.matmul(psA[:, pa + half * 512:pa + (half + 1) * 512],
                          cb16[ic][:, tt * 128:(tt + 1) * 128],
                          wT[1][ic][:, half * 512:(half + 1) * 512],
                          start=False, stop=(ic == 7))
                 for half in range(2) for ic in range(8)])[-1],
                waits=[v_xa[ti - 2]] +
                      ([P.at("vector"), P.at("scalar")] if ti == 0 else []),
                inc=1)
            t_mm = P.at("tensor")
            P.emit("vector", lambda e, pa=pa: e.tensor_tensor(
                th[:], psA[:, pa:pa + 1024], bc_bo[:], ALU.add),
                waits=[t_mm, s_y[ti - 1], st[ti - 1]], inc=1)
            v_xa[ti] = P.at("vector")
            P.emit("scalar", lambda e: e.activation(out=th[:], in_=th[:],
                                                    func=AF.Tanh),
                   waits=[v_xa[ti]], inc=1)
            s_th = P.at("scalar")
            P.emit("vector", lambda e, ti=ti: e.tensor_tensor(
                xa[:], th[:], xo[ti % 2][:], ALU.add),
                waits=[s_th, dmw], inc=1)
            v_xo[ti] = P.at("vector")
            P.emit("vector", lambda e: e.bn_stats(ln_s[:, 8:14], xa[:, 0:512]),
                   chain=True, inc=1)
            P.emit("vector", lambda e: e.bn_stats(ln_s[:, 14:20],
                                                  xa[:, 512:1024]),
                   chain=True, inc=1)
            P.emit("vector", lambda e: e.bn_aggr(ln_s[:, 0:2], ln_s[:, 8:20]),
                   chain=True, inc=1)
            P.emit("vector", lambda e: e.tensor_scalar_add(ln_s[:, 2:3],
                                                           ln_s[:, 1:2], 1e-5),
                   chain=True, inc=1)
            v_ln = P.at("vector")
            P.emit("scalar", lambda e: e.activation(
                out=ln_s[:, 3:4], in_=ln_s[:, 2:3], func=AF.Sqrt),
                waits=[v_ln], inc=1)
            s_sq3 = P.at("scalar")
            P.emit("vector", lambda e: e.reciprocal(ln_s[:, 4:5], ln_s[:, 3:4]),
                   waits=[s_sq3], inc=1)
            P.emit("vector", lambda e: e.tensor_tensor(
                ln_s[:, 5:6], ln_s[:, 0:1], ln_s[:, 4:5], ALU.mult),
                chain=True, inc=1)
            P.emit("vector", lambda e: e.tensor_scalar_mul(
                ln_s[:, 5:6], ln_s[:, 5:6], -1.0), chain=True, inc=1)
            v_fin = P.at("vector")
            P.emit("scalar", lambda e: e.activation(
                out=y_sb[:], in_=xa[:], func=AF.Identity, bias=ln_s[:, 5:6],
                scale=ln_s[:, 4:5]),
                waits=[v_fin], inc=1)
            s_y[ti] = P.at("scalar")
            P.emit(DMA_S, lambda e, t=t, tt=tt: e.dma_start(
                out=y[t, tt * 128:(tt + 1) * 128, :], in_=y_sb[:]),
                waits=[s_y[ti]], inc=16)
            st[ti] = P.at(DMA_S)
    P.emit(DMA_S, lambda e: None, waits=[P.at(DMA_S)])




def host_stage1_inputs(x, w_attn, b_attn, w_proj, b_proj, l1=False):
    in_maps = []
    for core in range(8):
        b, half = core // 2, core % 2
        xb = np.ascontiguousarray(x[b])
        own = xb[half * 2048:(half + 1) * 2048]
        other = xb[(1 - half) * 2048:(2 - half) * 2048]
        x_kv = np.ascontiguousarray(np.concatenate([other, own], axis=0))
        kvpos = np.concatenate([np.arange(2048) + (1 - half) * 2048,
                                np.arange(2048) + half * 2048])
        kdead = np.zeros((4, 32, 128), dtype=np.float32)
        dmask = np.zeros((4, 8, 128, 512), dtype=BF)
        qv = np.arange(512)
        kv128 = np.arange(128)
        for j in range(4):
            qpos = half * 2048 + j * 512 + qv
            for kt in range(32):
                kp = kvpos[kt * 128:(kt + 1) * 128]
                kdead[j, kt] = np.where(kp > qpos.max(), NEG, 0.0)
            slots = list(range(4 * j, 4 * j + 4)) + \
                    list(range(16 + 4 * j, 16 + 4 * j + 4))
            for si, kt in enumerate(slots):
                kp = kvpos[kt * 128:(kt + 1) * 128]
                m = np.where(qpos[None, :] >= kp[:, None], 0.0, NEG)
                dmask[j, si] = m.astype(BF)
        in_maps.append({
            "x_own": own.copy(),
            "x_kv": x_kv,
            "w_attn": np.ascontiguousarray(w_attn),
            "b_attn": np.ascontiguousarray(b_attn),
            "w_proj": np.ascontiguousarray(w_proj),
            "b_proj": np.ascontiguousarray(b_proj),
            "kdead": kdead,
            "dmask": dmask,
        })
    return in_maps




def _rotary_tables(pos, n=4096):
    half_idx = np.arange(0, 64, 2, dtype=np.float64)
    inv_freq = 1.0 / 10000.0 ** (half_idx / 64.0)
    t = pos.astype(np.float64)
    freqs = t[:, None] * inv_freq[None, :]
    base = (half_idx + 0.4 * 64) / (1.4 * 64)
    power = (t - n // 2) / 256.0
    scale = base[None, :] ** power[:, None]
    cos = np.cos(freqs); sin = np.sin(freqs)
    def cat(v):
        v64 = np.concatenate([v, v], axis=1).T
        return np.concatenate([v64, v64], axis=0).astype(np.float32)
    aq = cat(cos * scale); bq = cat(sin * scale)
    ak = cat(cos / scale); bk = cat(sin / scale)
    return aq, bq, ak, bk


def host_stage_ffn_inputs(x_full, wqkv_l, qs_l, ks_l, wo_l, lng_l, lnb_l,
                          w1_l, b1_l, w2_l, b2_l, y1=None):
    HID = 2730
    HIDP = 2816
    w1a = np.zeros((1024, HIDP), dtype=np.float32)
    w1g = np.zeros((1024, HIDP), dtype=np.float32)
    w1a[:, :HID] = w1_l[:, :HID]; w1g[:, :HID] = w1_l[:, HID:]
    b1a = np.zeros(HIDP, dtype=np.float32); b1g = np.zeros(HIDP, dtype=np.float32)
    b1a[:HID] = b1_l[:HID]; b1g[:HID] = b1_l[HID:]
    w2p = np.zeros((HIDP, 1024), dtype=np.float32)
    w2p[:HID] = w2_l
    qs_blk = np.zeros((2, 128), dtype=np.float32)
    qs_blk[0, 0:64] = qs_l; qs_blk[1, 64:128] = qs_l
    ks_blk = np.zeros((2, 128), dtype=np.float32)
    ks_blk[0, 0:64] = ks_l; ks_blk[1, 64:128] = ks_l
    in_maps = []
    for core in range(8):
        b, half = core // 2, core % 2
        start = half * 2048
        xh = np.zeros((2816, 1024), dtype=np.float32)
        lo = start - 768
        src_lo = max(lo, 0)
        xh[src_lo - lo:] = x_full[b, src_lo:start + 2048]
        pos = np.clip(np.arange(lo, start + 2048), 0, 4095)
        aq, bq, ak, bk = _rotary_tables(pos)
        kd = np.zeros((21, 128), dtype=np.float32)
        gpos = np.arange(lo + 128, lo + 128 + 2688)
        kd_flat = np.where(gpos < 0, NEG, 0.0).astype(np.float32)
        kd[:] = kd_flat.reshape(21, 128)
        m = {
            "x_halo": xh, "wqkv": np.ascontiguousarray(wqkv_l),
            "qs_blk": qs_blk, "ks_blk": ks_blk,
            "wo": np.ascontiguousarray(wo_l),
            "ln_g": np.ascontiguousarray(lng_l),
            "ln_b": np.ascontiguousarray(lnb_l),
            "w1a": w1a, "w1g": w1g, "b1a": b1a, "b1g": b1g,
            "w2p": w2p, "b2": np.ascontiguousarray(b2_l),
            "kdead": kd,
            "aq": aq, "bq": bq, "ak": ak, "bk": bk,
        }
        if y1 is not None:
            m["y1_own"] = np.ascontiguousarray(y1[b, start:start + 2048])
        in_maps.append(m)
    return in_maps


def host_stage_lstm_inputs(x_full, inp):
    in_maps = []
    for core in range(8):
        sl = slice(core * 512, (core + 1) * 512)
        m = {"x_own": np.ascontiguousarray(x_full[:, sl, :])}
        for n, k in (("wf", "L1_ffn_wf"), ("uf", "L1_ffn_uf"),
                     ("wi", "L1_ffn_wi"), ("ui", "L1_ffn_ui"),
                     ("wc", "L1_ffn_wc"), ("uc", "L1_ffn_uc"),
                     ("wo", "L1_ffn_wo"), ("uo", "L1_ffn_uo")):
            m[n] = np.ascontiguousarray(inp[k])
        for n, k in (("bf_v", "L1_ffn_bf"), ("bi_v", "L1_ffn_bi"),
                     ("bc_v", "L1_ffn_bc"), ("bo_v", "L1_ffn_bo"),
                     ("in_c", "L1_ffn_in_c")):
            m[n] = np.ascontiguousarray(inp[k])
        in_maps.append(m)
    return in_maps





NFULL = 16384
ZROW = 16384

ABLK = [0, 3, 4, 7, 8, 11, 12, 15]
BBLK = [1, 2, 5, 6, 9, 10, 13, 14]
HALFOF = np.zeros(16, dtype=np.int64)
ORDIX = np.zeros(16, dtype=np.int64)
for _i, _b in enumerate(ABLK):
    HALFOF[_b] = 0
    ORDIX[_b] = _i
for _i, _b in enumerate(BBLK):
    HALFOF[_b] = 1
    ORDIX[_b] = _i


def piece_of_zig(pos):
    beta = pos // 256
    pr = ORDIX[beta] * 256 + pos % 256
    return int(pr // 512)


def piece_of_contig(pos):
    return int((pos % 2048) // 512)


def piece_reqs():
    def halo_ps(t, zig):
        ps = set()
        for half in (0, 1):
            for i in range(128):
                p = half * 2048 - 768 + t * 128 + i
                if 0 <= p < 4096:
                    ps.add(piece_of_zig(p) if zig else piece_of_contig(p))
        return sorted(ps)

    def kv_ps(t):
        ps = set()
        for half in (0, 1):
            own = ABLK if half == 0 else BBLK
            kvloc = []
            for g in range(4):
                grp = list(range(4 * g, 4 * g + 4))
                for x_ in ([x for x in grp if x not in own] +
                           [x for x in grp if x in own]):
                    kvloc.extend(range(x_ * 256, x_ * 256 + 256))
            for p in kvloc[t * 128:(t + 1) * 128]:
                ps.add(piece_of_contig(p))
        return sorted(ps)

    def attnown_ps(t):
        ps = set()
        for half in (0, 1):
            own = ABLK if half == 0 else BBLK
            for i in range(128):
                ps.add(piece_of_contig(own[t // 2] * 256 + (t % 2) * 128 + i))
        return sorted(ps)

    def ffnown_ps(t):
        ps = set()
        for half in (0, 1):
            for i in range(128):
                ps.add(piece_of_zig(half * 2048 + t * 128 + i))
        return sorted(ps)

    def lstm_ps(ti):
        tt = ti % 4
        ps = set()
        for core in range(8):
            for i in range(128):
                ps.add(piece_of_zig(core * 512 + tt * 128 + i))
        return sorted(ps)

    return {
        "halo0": [halo_ps(t, True) for t in range(22)],
        "halo1": [halo_ps(t, False) for t in range(22)],
        "kv_p": [kv_ps(t) for t in range(32)],
        "own_p": [attnown_ps(t) for t in range(16)],
        "own_f": [ffnown_ps(t) for t in range(16)],
        "lstm": [lstm_ps(ti) for ti in range(16)],
    }


def build_fused():
    nc = bass.Bass("TRN2", target_bir_lowering=False, num_devices=8)

    x0 = nc.dram_tensor("x0", [NFULL, 1024], F32, kind="ExternalInput").ap()
    ein = {}

    def EIN(name, shape, dt=F32):
        ein[name] = nc.dram_tensor(name, shape, dt, kind="ExternalInput").ap()
        return ein[name]

    EIN("idx_kv", [128, 32], I32)
    EIN("idx_own", [128, 16], I32)
    EIN("idx_kv_p", [128, 32], I32)
    EIN("idx_own_p", [128, 16], I32)
    EIN("idx_halo0", [128, 22], I32)
    EIN("idx_halo1", [128, 22], I32)
    EIN("idx_own_f", [128, 16], I32)
    EIN("idx_lstm", [128, 16], I32)
    EIN("s1_w_attn", [1024, 3072])
    EIN("s1_b_attn", [3072])
    EIN("s1_w_proj", [1024, 1024])
    EIN("s1_b_proj", [1024])
    EIN("kdead_attn", [4, 32, 128])
    EIN("dmask_attn", [4, 8, 128, 512], BF16)
    for pfx in ("f0", "f1"):
        EIN(pfx + "_wqkv", [1024, 1536])
        EIN(pfx + "_qs_blk", [2, 128])
        EIN(pfx + "_ks_blk", [2, 128])
        EIN(pfx + "_wo", [512, 1024])
        EIN(pfx + "_ln_g", [1024])
        EIN(pfx + "_ln_b", [1024])
        EIN(pfx + "_w1a", [128, 22528])
        EIN(pfx + "_w1g", [128, 22528])
        EIN(pfx + "_b1a", [2816])
        EIN(pfx + "_b1g", [2816])
        EIN(pfx + "_w2p", [2816, 1024])
        EIN(pfx + "_b2", [1024])
    EIN("f_kdead", [21, 128])
    for tn in ("aq", "bq", "ak", "bk"):
        EIN("f_" + tn, [128, 2816])
    EIN("s4_w_attn", [1024, 3072])
    EIN("s4_b_attn", [3072])
    EIN("s4_w_proj", [1024, 1024])
    EIN("s4_b_proj", [1024])
    for n in ("wf", "uf", "wi", "ui", "wc", "uc", "wo", "uo"):
        EIN("s5_" + n, [1024, 1024])
    for n in ("bf_v", "bi_v", "bc_v", "bo_v", "in_c"):
        EIN("s5_" + n, [1024])
    y_out = nc.dram_tensor("y", [4, 512, 1024], F32, kind="ExternalOutput").ap()

    parts = [nc.dram_tensor(f"part{i}", [2048, 1024], F32, kind="Internal")
             for i in range(4)]
    fulls = [nc.dram_tensor(f"full{i}", [NFULL + 1, 1024], F32, kind="Internal")
             for i in range(4)]
    k_s1 = nc.dram_tensor("k_s1", [8, 128, 4096], BF16, kind="Internal")
    k_s4 = nc.dram_tensor("k_s4", [8, 128, 4096], BF16, kind="Internal")
    v_s4 = nc.dram_tensor("v_s4", [32, 128, 1024], BF16, kind="Internal")
    xnew_f = [nc.dram_tensor(f"xnew_f{i}", [17, 128, 1024], F32,
                             kind="Internal") for i in range(2)]
    w1abf = [nc.dram_tensor(f"w1abf{i}", [128, 22528], BF16, kind="Internal")
             for i in range(2)]
    w1gbf = [nc.dram_tensor(f"w1gbf{i}", [128, 22528], BF16, kind="Internal")
             for i in range(2)]
    w2bf = [nc.dram_tensor(f"w2bf{i}", [2816, 1024], BF16, kind="Internal")
            for i in range(2)]

    psA = nc.alloc_psum_tensor("psA", [128, 2048], F32)
    psS = [nc.alloc_psum_tensor(f"psS{i}", [128, 512], F32) for i in range(2)]
    psV = nc.alloc_psum_tensor("psV", [128, 512], F32)
    psT = nc.alloc_psum_tensor("psT", [128, 512], F32)
    PS = {"psA": psA, "psS": psS, "psV": psV, "psT": psT}

    AL = Arena(nc)
    consts = {}
    P0 = Prog(nc)
    for name, arr in all_np_consts().items():
        dt = BF16 if arr.dtype == BF else F32
        t = nc.alloc_sbuf_tensor("c_" + name, list(arr.shape), dt)
        dr = nc.inline_tensor(np.ascontiguousarray(arr), "ic_" + name)
        P0.emit("sync", lambda e, t=t, dr=dr: e.dma_start(out=t[:],
                                                          in_=dr.ap()),
                inc=16)
        consts[name] = t
    zrow = AL("zrow", [1, 1024], F32)
    AL.reset()
    P0.emit("vector", lambda e: e.memset(zrow[:], 0.0), inc=1)
    P0.barrier()
    for fl in fulls:
        P0.emit("sync", lambda e, fl=fl: e.dma_start(
            out=fl.ap()[ZROW:ZROW + 1, :], in_=zrow[:]), inc=16)
    P0.barrier()
    P0.build()

    _ccn = [0]

    def mk_ccp(part_t, full_t):
        _ccn[0] += 1
        sems = [nc.alloc_semaphore(f"ccp{_ccn[0]}_{g}") for g in range(4)]

        def emit_piece(P, g, wait_tok, part_t=part_t, full_t=full_t,
                       sems=sems):
            def f(e, g=g):
                e.collective_compute(
                    "AllGather", ALU.bypass, replica_groups=[list(range(8))],
                    ins=[part_t.ap()[g * 512:(g + 1) * 512, :].opt()],
                    outs=[full_t.ap()[g * 4096:(g + 1) * 4096, :].opt()]
                ).then_inc(sems[g], 1)
            P.emit("gpsimd", f, waits=[wait_tok], inc=0)
        return emit_piece, [(s, 1) for s in sems]

    AL.reset()
    P1 = Prog(nc)
    ccp = [None] * 4
    cc_tok = [None] * 4
    for i in range(4):
        ccp[i], cc_tok[i] = mk_ccp(parts[i], fulls[i])
    PR = piece_reqs()

    def ptoks(tok4, reqlist):
        return [[tok4[p] for p in ps] for ps in reqlist]
    D1 = {"x_src": x0, "idx_kv": ein["idx_kv"], "idx_own": ein["idx_own"],
          "w_attn": ein["s1_w_attn"], "b_attn": ein["s1_b_attn"],
          "w_proj": ein["s1_w_proj"], "b_proj": ein["s1_b_proj"],
          "kdead": ein["kdead_attn"], "dmask": ein["dmask_attn"],
          "ydst": parts[0].ap(), "k_dram": k_s1, "ccp": ccp[0]}
    D1.update(PS)
    emit_stage_attn(nc, P1, AL, consts, D1, l1=False)
    P1.barrier()
    P1.build()

    for li in range(2):
        AL.reset()
        Pf = Prog(nc)
        pfx = f"f{li}"
        Df = {"x_src": fulls[0].ap() if li == 0 else fulls[1].ap(),
              "idx_halo": ein[f"idx_halo{li}"], "idx_own": ein["idx_own_f"],
              "wqkv": ein[pfx + "_wqkv"], "qs_blk": ein[pfx + "_qs_blk"],
              "ks_blk": ein[pfx + "_ks_blk"], "wo": ein[pfx + "_wo"],
              "ln_g": ein[pfx + "_ln_g"], "ln_b": ein[pfx + "_ln_b"],
              "w1a": ein[pfx + "_w1a"], "w1g": ein[pfx + "_w1g"],
              "b1a": ein[pfx + "_b1a"], "b1g": ein[pfx + "_b1g"],
              "w2p": ein[pfx + "_w2p"], "b2": ein[pfx + "_b2"],
              "kdead": ein["f_kdead"],
              "aq": ein["f_aq"], "bq": ein["f_bq"], "ak": ein["f_ak"],
              "bk": ein["f_bk"],
              "y1_src": fulls[0].ap(),
              "ydst": parts[1 + li].ap(), "xnew_scr": xnew_f[li],
              "ccw": cc_tok[li], "ccw_y1": cc_tok[0], "ccp": ccp[1 + li],
              "ccw_halo": ptoks(cc_tok[li], PR["halo0" if li == 0 else "halo1"]),
              "ccw_own1": ptoks(cc_tok[0], PR["own_f"]),
              "w1abf": w1abf[li], "w1gbf": w1gbf[li], "w2bf": w2bf[li]}
        Df.update(PS)
        emit_stage_ffn(nc, Pf, AL, consts, Df, last=(li == 1))
        Pf.barrier()
        Pf.build()

    AL.reset()
    P4 = Prog(nc)
    D4 = {"x_src": fulls[2].ap(), "idx_kv": ein["idx_kv_p"],
          "idx_own": ein["idx_own_p"],
          "w_attn": ein["s4_w_attn"], "b_attn": ein["s4_b_attn"],
          "w_proj": ein["s4_w_proj"], "b_proj": ein["s4_b_proj"],
          "kdead": ein["kdead_attn"], "dmask": ein["dmask_attn"],
          "ydst": parts[3].ap(), "k_dram": k_s4, "v_dram": v_s4,
          "ccw": cc_tok[2], "ccp": ccp[3],
          "ccw_kv": ptoks(cc_tok[2], PR["kv_p"]),
          "ccw_own": ptoks(cc_tok[2], PR["own_p"])}
    D4.update(PS)
    emit_stage_attn(nc, P4, AL, consts, D4, l1=True)
    P4.barrier()
    P4.build()

    AL.reset()
    P5 = Prog(nc)
    D5 = {"x_src": fulls[3].ap(), "idx_lstm": ein["idx_lstm"], "ydst": y_out,
          "ccw": cc_tok[3]}
    for n in ("wf", "uf", "wi", "ui", "wc", "uc", "wo", "uo",
              "bf_v", "bi_v", "bc_v", "bo_v", "in_c"):
        D5[n] = ein["s5_" + n]
    D5.update(PS)
    emit_stage_lstm(nc, P5, AL, consts, D5)
    P5.build()
    return nc


def host_fused_inputs(inputs):
    x0 = np.asarray(inputs["L0_attn_hidden_states"],
                    dtype=np.float32).reshape(NFULL, 1024)
    HID = 2730
    in_maps = []
    shared = {
        "s1_w_attn": np.ascontiguousarray(
            np.asarray(inputs["L0_attn_c_attn_kernel"]).T),
        "s1_b_attn": np.ascontiguousarray(inputs["L0_attn_c_attn_bias"]),
        "s1_w_proj": np.ascontiguousarray(
            np.asarray(inputs["L0_attn_c_proj_kernel"]).T),
        "s1_b_proj": np.ascontiguousarray(inputs["L0_attn_c_proj_bias"]),
        "s4_w_attn": np.ascontiguousarray(inputs["L1_attn_w_attn"]),
        "s4_b_attn": np.ascontiguousarray(inputs["L1_attn_b_attn"]),
        "s4_w_proj": np.ascontiguousarray(inputs["L1_attn_w_proj"]),
        "s4_b_proj": np.ascontiguousarray(inputs["L1_attn_b_proj"]),
    }
    for li in range(2):
        pfx = f"f{li}"
        w1_l = inputs["L0_ffn_w1"][li]
        b1_l = inputs["L0_ffn_b1"][li]
        w1a = np.zeros((1024, 2816), dtype=np.float32)
        w1g = np.zeros((1024, 2816), dtype=np.float32)
        w1a[:, :HID] = w1_l[:, :HID]
        w1g[:, :HID] = w1_l[:, HID:]
        b1a = np.zeros(2816, dtype=np.float32)
        b1g = np.zeros(2816, dtype=np.float32)
        b1a[:HID] = b1_l[:HID]
        b1g[:HID] = b1_l[HID:]
        w2p = np.zeros((2816, 1024), dtype=np.float32)
        w2p[:HID] = inputs["L0_ffn_w2"][li]
        qs_blk = np.zeros((2, 128), dtype=np.float32)
        qs_blk[0, 0:64] = inputs["L0_ffn_q_scale"][li]
        qs_blk[1, 64:128] = inputs["L0_ffn_q_scale"][li]
        ks_blk = np.zeros((2, 128), dtype=np.float32)
        ks_blk[0, 0:64] = inputs["L0_ffn_k_scale"][li]
        ks_blk[1, 64:128] = inputs["L0_ffn_k_scale"][li]
        def _w1rel(w):
            return np.ascontiguousarray(
                w.reshape(8, 128, 22, 128).transpose(1, 2, 0, 3)
                 .reshape(128, 22528))
        shared.update({
            pfx + "_wqkv": np.ascontiguousarray(inputs["L0_ffn_wqkv"][li]),
            pfx + "_qs_blk": qs_blk, pfx + "_ks_blk": ks_blk,
            pfx + "_wo": np.ascontiguousarray(inputs["L0_ffn_wo"][li]),
            pfx + "_ln_g": np.ascontiguousarray(inputs["L0_ffn_ln_g"][li]),
            pfx + "_ln_b": np.ascontiguousarray(inputs["L0_ffn_ln_b"][li]),
            pfx + "_w1a": _w1rel(w1a), pfx + "_w1g": _w1rel(w1g),
            pfx + "_b1a": b1a, pfx + "_b1g": b1g,
            pfx + "_w2p": w2p,
            pfx + "_b2": np.ascontiguousarray(inputs["L0_ffn_b2"][li]),
        })
    for n in ("wf", "uf", "wi", "ui", "wc", "uc", "wo", "uo"):
        shared["s5_" + n] = np.ascontiguousarray(inputs["L1_ffn_" + n[:1] +
                                                        n[1:]])
    shared["s5_wf"] = np.ascontiguousarray(np.asarray(inputs["L1_ffn_wf"]).T)
    shared["s5_uf"] = np.ascontiguousarray(np.asarray(inputs["L1_ffn_uf"]).T)
    shared["s5_wi"] = np.ascontiguousarray(np.asarray(inputs["L1_ffn_wi"]).T)
    shared["s5_ui"] = np.ascontiguousarray(np.asarray(inputs["L1_ffn_ui"]).T)
    shared["s5_wc"] = np.ascontiguousarray(np.asarray(inputs["L1_ffn_wc"]).T)
    shared["s5_uc"] = np.ascontiguousarray(np.asarray(inputs["L1_ffn_uc"]).T)
    shared["s5_wo"] = np.ascontiguousarray(np.asarray(inputs["L1_ffn_wo"]).T)
    shared["s5_uo"] = np.ascontiguousarray(np.asarray(inputs["L1_ffn_uo"]).T)
    shared["s5_bf_v"] = np.ascontiguousarray(inputs["L1_ffn_bf"])
    shared["s5_bi_v"] = np.ascontiguousarray(inputs["L1_ffn_bi"])
    shared["s5_bc_v"] = np.ascontiguousarray(inputs["L1_ffn_bc"])
    shared["s5_bo_v"] = np.ascontiguousarray(inputs["L1_ffn_bo"])
    shared["s5_in_c"] = np.ascontiguousarray(inputs["L1_ffn_in_c"])

    ii = np.arange(128)
    halfof, ordix = HALFOF, ORDIX

    def _pmap_z(ix):
        ix = np.asarray(ix)
        bb = ix // 4096
        pos = ix % 4096
        beta = np.clip(pos // 256, 0, 15)
        corez = bb * 2 + halfof[beta]
        pr = ordix[beta] * 256 + pos % 256
        pm = (pr // 512) * 4096 + corez * 512 + pr % 512
        return np.where(ix >= NFULL, ix, pm).astype(np.int32)

    for core in range(8):
        b, half = core // 2, core % 2
        start = half * 2048
        base = b * 4096
        own_blocks = ABLK if half == 0 else BBLK
        kvloc_l = []
        for g in range(4):
            grp = list(range(4 * g, 4 * g + 4))
            oth = [x_ for x_ in grp if x_ not in own_blocks]
            owb = [x_ for x_ in grp if x_ in own_blocks]
            for x_ in oth + owb:
                kvloc_l.append(np.arange(256) + x_ * 256)
        kvloc = np.concatenate(kvloc_l)
        kvpos = kvloc + base
        idx_kv = np.zeros((128, 32), dtype=np.int32)
        for kt in range(32):
            idx_kv[:, kt] = kvpos[kt * 128:(kt + 1) * 128]
        idx_own = np.zeros((128, 16), dtype=np.int32)
        for t in range(16):
            idx_own[:, t] = (base + own_blocks[t // 2] * 256 +
                             (t % 2) * 128 + ii)
        idx_halo = np.zeros((128, 22), dtype=np.int32)
        for t in range(22):
            loc = start - 768 + t * 128 + ii
            idx_halo[:, t] = np.where(loc < 0, ZROW, base + loc)
        idx_lstm = np.zeros((128, 16), dtype=np.int32)
        for t in range(4):
            for tt in range(4):
                idx_lstm[:, t * 4 + tt] = (t * 4096 + core * 512 +
                                           tt * 128 + ii)

        def _pmap(ix):
            ix = np.asarray(ix)
            pm = ((ix % 2048) // 512) * 4096 + (ix // 2048) * 512 + ix % 512
            return np.where(ix >= NFULL, ix, pm).astype(np.int32)
        kdead = np.zeros((4, 32, 128), dtype=np.float32)
        dmask = np.zeros((4, 8, 128, 512), dtype=BF)
        for j in range(4):
            qpos = np.concatenate([own_blocks[2 * j] * 256 + np.arange(256),
                                   own_blocks[2 * j + 1] * 256 + np.arange(256)])
            for kt in range(32):
                kp = kvloc[kt * 128:(kt + 1) * 128]
                kdead[j, kt] = np.where(kp > qpos.max(), NEG, 0.0)
            for si in range(8):
                kt = 8 * j + si
                kp = kvloc[kt * 128:(kt + 1) * 128]
                dmask[j, si] = np.where(qpos[None, :] >= kp[:, None],
                                        0.0, NEG).astype(BF)
        lo = start - 768
        gpos = np.arange(lo + 128, lo + 128 + 2688)
        f_kdead = np.where(gpos < 0, NEG, 0.0).astype(np.float32) \
            .reshape(21, 128)
        pos = np.clip(np.arange(lo, start + 2048), 0, 4095)
        aq, bq, ak, bk = _rotary_tables(pos)
        m = dict(shared)
        m.update({
            "x0": x0, "idx_kv": idx_kv, "idx_own": idx_own,
            "idx_kv_p": _pmap(idx_kv), "idx_own_p": _pmap(idx_own),
            "idx_halo0": _pmap_z(idx_halo), "idx_halo1": _pmap(idx_halo),
            "idx_own_f": _pmap_z(base + start + np.arange(16)[None, :] * 128 +
                                 ii[:, None]),
            "idx_lstm": _pmap_z(idx_lstm),
            "kdead_attn": kdead, "dmask_attn": dmask,
            "f_kdead": f_kdead,
            "f_aq": aq, "f_bq": bq, "f_ak": ak, "f_bk": bk,
        })
        in_maps.append(m)
    return in_maps


_NC_CACHE = {}


def kernel(**inputs):
    if "fused" not in _NC_CACHE:
        _NC_CACHE["fused"] = build_fused()
    nc = _NC_CACHE["fused"]
    in_maps = host_fused_inputs(inputs)
    res = run_bass_kernel_spmd(nc, in_maps, core_ids=list(range(8)))
    out = np.zeros((4, 4096, 1024), dtype=np.float32)
    for core in range(8):
        out[:, core * 512:(core + 1) * 512, :] = res.results[core]["y"]
    return out
